# version 02
# baseline (speedup 1.0000x reference)
import numpy as np
import concourse.bass as bass
import concourse.mybir as mybir
from concourse.bass_utils import run_bass_kernel_spmd
from contextlib import ExitStack

F32 = mybir.dt.float32
BF16 = mybir.dt.bfloat16
I32 = mybir.dt.int32
AF = mybir.ActivationFunctionType
ALU = mybir.AluOpType
AX = mybir.AxisListType

D = 4096
KC = 32
TOK = 1024
TT = 2
TW = 512
NQH = 32
NKV = 8
HD = 128
RSCALE = float(1.0 / np.float64(np.float32(np.sqrt(np.float64(HD)))))
EPS = 1e-5
QT = 4
QW = 1024
KCQ = QW // 128
KCNT = (8, 16)

ENGINES = ("sync", "scalar", "vector", "tensor", "gpsimd")

PI_IN = 3.1415925
INV2PI = float(1.0 / (2.0 * np.pi))
NEG2PI = float(-2.0 * np.pi)
PIH = float(np.pi / 2.0)
CONSTS_F32 = [-1.0, 2.0, 1.0 / 4096.0, EPS, RSCALE, -RSCALE, 0.5, 7.0,
              64.0, 10000.0, 1.0, 0.0, -1.0 / 64.0, -1e30,
              INV2PI, NEG2PI, PI_IN, -PI_IN, PIH, -64.0]
CONSTS_BF16 = [1.0, -1.0, 2.0, 0.0]


class Prog:
    def __init__(self):
        self.ops = {e: [] for e in ENGINES}
        self.cnt = {}
        self.gate_pending = {e: None for e in ENGINES}
        self.serial = False
        for s in ("dS", "dG", "T", "V", "S", "G", "C"):
            self.cnt[s] = 0

    def op(self, eng, fn, waits=(), inc=None, amt=1):
        w = [x for x in waits if x is not None]
        g = self.gate_pending[eng]
        if g is not None:
            w = list(g.items()) + w
            self.gate_pending[eng] = None
        ev = None
        if inc is not None:
            self.cnt[inc] += amt
            ev = (inc, self.cnt[inc])
        self.ops[eng].append((tuple(w), fn, (inc, amt) if inc else None))
        if self.serial and inc is not None:
            self.gate()
        return ev

    def dma(self, eng, out, in_, waits=(), sem=None):
        if sem is None:
            sem = "dS" if eng == "sync" else "dG"
        self.cnt.setdefault(sem, 0)
        return self.op(eng, lambda e, o=out, i=in_: e.dma_start(out=o, in_=i),
                       waits, inc=sem, amt=16)

    def gate(self):
        snap = {k: v for k, v in self.cnt.items() if v > 0}
        for e in ENGINES:
            self.gate_pending[e] = snap

    def final_wait(self, eng, ev):
        self.ops[eng].append(((ev,), None, None))

    def emit(self, block, sems):
        for eng_name in ENGINES:
            ops = self.ops[eng_name]
            if not ops:
                continue

            def make(ops):
                def body(eng):
                    waited = {}
                    for waits, fn, inc in ops:
                        merged = {}
                        for s, v in waits:
                            if v > merged.get(s, 0):
                                merged[s] = v
                        for s, v in merged.items():
                            if v > 0 and waited.get(s, 0) < v:
                                eng.wait_ge(sems[s], v)
                                waited[s] = v
                        if fn is None:
                            continue
                        instr = fn(eng)
                        if inc is not None:
                            instr.then_inc(sems[inc[0]], inc[1])
                return body

            getattr(block, eng_name)(make(ops))


def build_nc(debug=False):
    nc = bass.Bass("TRN2", target_bir_lowering=False)

    def din(name, shape, dtype=F32):
        return nc.dram_tensor(name, shape, dtype, kind="ExternalInput").ap()

    def dscr(name, shape, dtype=F32):
        dbg = debug if isinstance(debug, (tuple, list, set)) else ()
        kind = "ExternalOutput" if name in dbg else "Internal"
        return nc.dram_tensor(name, shape, dtype, kind=kind).ap()

    x0 = din("x0", [D, TOK])
    w0a = din("w0a", [D, D])
    m0a = din("m0a", [D, D], I32)
    w0f = din("w0f", [D, D])
    b0f = din("b0f", [128, KC])
    g0 = din("g0", [128, KC * 8])
    is0 = din("is0", [8, D])
    os0 = din("os0", [8, D])
    w1q = din("w1q", [D, D])
    w1k = din("w1k", [NKV * 128, D])
    w1v = din("w1v", [D, 1024])
    w1o = din("w1o", [D, D])
    w1b = din("w1b", [D, D])
    g1 = din("g1", [128, KC * 8])
    a1 = din("a1", [128, KC * 64])
    b1d = din("b1d", [64, D])
    qbin = din("qb", [1, 2])
    qselin = din("qsel", [64, 8])

    out0 = nc.dram_tensor("out0", [D, TOK], F32, kind="ExternalOutput").ap()
    scr_s0 = dscr("scr_s0", [D, TOK])
    scr_x1 = dscr("scr_x1", [D, TOK])
    scr_g1 = dscr("scr_g1", [D, TOK])
    scr_qh = dscr("scr_qh", [NQH * 128, TOK], BF16)
    scr_ql = dscr("scr_ql", [NQH * 128, TOK], BF16)
    cc_in = nc.dram_tensor("cc_in", [2048, 1024], F32).ap()
    dbg = debug if isinstance(debug, (tuple, list, set)) else ()
    dbg_rope = (nc.dram_tensor("dbg_rope", [64, 5248], F32,
                               kind="ExternalOutput").ap()
                if "dbg_rope" in dbg else None)
    cc_out = dscr("cc_out", [4096, 1024])

    DSEMS = ("dS", "dG", "dW0", "dW1", "dM", "dX0", "dX1", "dB0", "dB1",
             "dK", "dV", "dQ", "dO", "dS0a", "dS0b", "dXW0", "dXW1", "dQW",
             "dGW0", "dGW1", "dCC", "dOut", "dOut1")
    P = Prog()
    for _s in DSEMS:
        P.cnt.setdefault(_s, 0)
    st = ExitStack()
    with st:
        e = st.enter_context

        act_hi = e(nc.sbuf_tensor("act_hi", [128, KC * TOK], BF16))
        act_lo = e(nc.sbuf_tensor("act_lo", [128, KC * TOK], BF16))
        wstage = e(nc.sbuf_tensor("wstage", [128, 2 * QW], F32))
        whl_hi = e(nc.sbuf_tensor("whl_hi", [128, 2 * QW], BF16))
        whl_lo = e(nc.sbuf_tensor("whl_lo", [128, 2 * QW], BF16))
        xst = e(nc.sbuf_tensor("xst", [128, 2 * TOK], F32))
        evac = e(nc.sbuf_tensor("evac", [128, 2 * TOK], F32))
        gbuf = e(nc.sbuf_tensor("gbuf", [128, 2 * TOK], F32))
        ones_f = e(nc.sbuf_tensor("ones_f", [128, 1], F32))
        ones_b = e(nc.sbuf_tensor("ones_b", [128, 1], BF16))
        onesr_f = e(nc.sbuf_tensor("onesr_f", [1, 128], F32))
        onesr_b = e(nc.sbuf_tensor("onesr_b", [1, 128], BF16))
        idm = e(nc.sbuf_tensor("idm", [128, 128], F32))
        idm_i = e(nc.sbuf_tensor("idm_i", [128, 128], I32))
        rt_hi = e(nc.sbuf_tensor("rt_hi", [8, TOK], BF16))
        rt_lo = e(nc.sbuf_tensor("rt_lo", [8, TOK], BF16))
        bcol = e(nc.sbuf_tensor("bcol", [8, 2], F32))
        g0s = e(nc.sbuf_tensor("g0s", [128, KC * 8], F32))
        g0h = e(nc.sbuf_tensor("g0h", [128, KC * 8], BF16))
        g0l = e(nc.sbuf_tensor("g0l", [128, KC * 8], BF16))
        stat = e(nc.sbuf_tensor("stat", [1, 1040], F32))
        b0fs = e(nc.sbuf_tensor("b0fs", [128, KC], F32))
        qbs = e(nc.sbuf_tensor("qbs", [1, 2], F32))
        qbc = e(nc.sbuf_tensor("qbc", [128, 2], F32))
        qsels = e(nc.sbuf_tensor("qsels", [64, 8], F32))
        thr = e(nc.sbuf_tensor("thr", [128, 24], F32))
        thr_i = e(nc.sbuf_tensor("thr_i", [128, 24], I32))
        repm = e(nc.sbuf_tensor("repm", [8, 64], BF16))
        rep8 = e(nc.sbuf_tensor("rep8", [8, 64], I32))
        csts = e(nc.sbuf_tensor("csts", [128, len(CONSTS_F32)], F32))
        cstb = e(nc.sbuf_tensor("cstb", [128, len(CONSTS_BF16)], BF16))

        for i, v in enumerate(CONSTS_F32):
            nc.const_aps.register(F32, v, csts[:, i:i + 1])
        for i, v in enumerate(CONSTS_BF16):
            nc.const_aps.register(BF16, v, cstb[:, i:i + 1])

        pb = [e(nc.psum_tensor(f"pb{i}", [128, TW], F32)) for i in range(8)]

        sems = {}
        for s in DSEMS + ("T", "V", "S", "G", "C"):
            sems[s] = e(nc.semaphore(s))

        def _ap(x):
            return x[:, :] if x.__class__.__name__.endswith("TensorHandle") else x

        def tt_(a, op, b, o, waits=(), eng="vector", inc="V"):
            return P.op(eng, lambda e, a=a, b=b, o=o, op=op:
                        e.tensor_tensor(out=o, in0=a, in1=b, op=op), waits, inc=inc)

        def tsc(in0, s1, op0, o, s2=None, op1=None, waits=(), eng="vector", inc="V"):
            def fn(e, i=in0, s1=s1, s2=s2, o=o, op0=op0, op1=op1):
                kw = {"op1": op1} if op1 is not None else {}
                return e.tensor_scalar(out=o, in0=i, scalar1=s1, scalar2=s2,
                                       op0=op0, **kw)
            return P.op(eng, fn, waits, inc=inc)

        def act(in_, func, o, scale=1.0, bias=0.0, waits=(), inc="S"):
            return P.op("scalar", lambda e, i=in_, o=o, f=func, sc=scale, b=bias:
                        e.activation(out=o, in_=i, func=f, scale=sc, bias=b),
                        waits, inc=inc)

        def scopy(in_, o, scale=None, waits=(), inc="S"):
            if scale is None:
                return P.op("scalar", lambda e, i=in_, o=o: e.copy(out=o, in_=i),
                            waits, inc=inc)
            return P.op("scalar", lambda e, i=in_, o=o, s=scale:
                        e.mul(out=o, in_=i, mul=s), waits, inc=inc)

        def mm(ps, lhsT, rhs, start, stop, waits=(), inc=None):
            return P.op("tensor",
                        lambda e, p=_ap(ps), l=_ap(lhsT), r=_ap(rhs), a=start, b=stop:
                        e.matmul(out=p, lhsT=l, rhs=r, start=a, stop=b),
                        waits, inc=inc)

        def recip(in_, o, waits=()):
            return P.op("vector", lambda e, i=in_, o=o:
                        e.reciprocal(out=o, in_=i), waits, inc="V")

        def iota(o, pattern, base, chm, waits=(), inc="G"):
            return P.op("gpsimd", lambda e, o=o, p=pattern, b=base, c=chm:
                        e.iota(out=o, pattern=p, base=b, channel_multiplier=c),
                        waits, inc=inc)

        def memset(ap, v, waits=(), inc="G"):
            return P.op("gpsimd", lambda e, a=ap, v=v: e.memset(a, v), waits, inc=inc)

        def split(src, hi, lo, waits=()):
            eh = scopy(src, hi, waits=waits)
            el = tt_(src, ALU.subtract, hi, lo, waits=[eh])
            return eh, el

        for i, v in enumerate(CONSTS_F32):
            memset(csts[:, i:i + 1], v)
        for i, v in enumerate(CONSTS_BF16):
            memset(cstb[:, i:i + 1], v)
        memset(ones_f[:, :], 1.0)
        memset(ones_b[:, :], 1.0)
        memset(onesr_f[:, :], 1.0)
        memset(onesr_b[:, :], 1.0)
        iota(idm_i[:, :], [[1, 128]], 0, -1)
        iota(rep8[:, :], [[1, 64]], 0, -8)
        for j in range(24):
            kc = j if j < 8 else (j - 8)
            iota(thr_i[:, j:j + 1], [[1, 1]], 128 * kc, 1)
        memset(bcol[:, :], 0.0)
        P.dma("sync", qbs[:, :], qbin[:, :], sem="dO")
        P.dma("sync", qsels[:, :], qselin[:, :], sem="dO")
        P.dma("sync", b0fs[:, :], b0f[:, :], sem="dO")
        P.dma("sync", g0s[:, :], g0[:, :], sem="dO")
        P.gate()
        tsc(idm_i[:, :], 0.0, ALU.is_equal, idm[:, :])
        tsc(rep8[:, :], 0.0, ALU.is_ge, xst[0:8, 0:64])
        tsc(rep8[:, :], 7.0, ALU.is_le, xst[0:8, 64:128])
        P.gate()
        tt_(xst[0:8, 0:64], ALU.mult, xst[0:8, 64:128], repm[:, :])
        scopy(thr_i[:, :], thr[:, :])
        mm(pb[0][:, 0:1], onesr_f[:, :], qbs[0:1, 0:1], True, True)
        mm(pb[1][:, 0:1], onesr_f[:, :], qbs[0:1, 1:2], True, True, inc="T")
        P.gate()
        scopy(pb[0][:, 0:1], qbc[:, 0:1])
        scopy(pb[1][:, 0:1], qbc[:, 1:2])
        P.gate()
        tsc(thr[:, 0:8], qbc[:, 0:1], ALU.subtract, thr[:, 0:8])
        tsc(thr[:, 8:24], qbc[:, 1:2], ALU.subtract, thr[:, 8:24])
        split(g0s[:, :], g0h[:, :], g0l[:, :])
        P.gate()

        def rhs_act(k0, tt):
            hi = act_hi[:, k0 * TOK + tt * TW: k0 * TOK + (tt + 1) * TW]
            lo = act_lo[:, k0 * TOK + tt * TW: k0 * TOK + (tt + 1) * TW]
            return hi, lo

        def stream_proj(src2d, n_list, ps_pair, epilogue, mask2d=None,
                        sign=False, npass=3, extra_group_mm=None, mstage=None):
            ev_sp, ev_tq, ev_ep, ev_wm = {}, {}, {}, {}
            j = 0
            for ni, n in enumerate(n_list):
                rows = slice(n * 128, (n + 1) * 128)
                psA, psB = ps_pair(ni)
                for qt in range(QT):
                    par = j % 2
                    sl = slice(par * QW, (par + 1) * QW)
                    csl = slice(qt * QW, (qt + 1) * QW)
                    ev_d = P.dma("sync", wstage[:, sl], src2d[rows, csl],
                                 waits=[ev_sp.get(j - 2)], sem="dW%d" % par)
                    if mask2d is not None:
                        ev_dm = P.dma("sync", mstage[:, 0:QW], mask2d[rows, csl],
                                      waits=[ev_wm.get(j - 1)], sem="dM")
                        ev_w = tt_(wstage[:, sl], ALU.mult, mstage[:, 0:QW],
                                   wstage[:, sl], waits=[ev_d, ev_dm,
                                                         ev_tq.get(j - 2)])
                        ev_wm[j] = ev_w
                        src_rdy = [ev_w]
                    else:
                        src_rdy = [ev_d, ev_tq.get(j - 2)]
                    if sign:
                        ev_hi = act(wstage[:, sl], AF.Sign, whl_hi[:, sl],
                                    waits=src_rdy)
                        ev_sp[j] = ev_hi
                    else:
                        ev_hi = scopy(wstage[:, sl], whl_hi[:, sl], waits=src_rdy)
                        ev_lo = tt_(wstage[:, sl], ALU.subtract, whl_hi[:, sl],
                                    whl_lo[:, sl], waits=[ev_hi])
                        ev_sp[j] = ev_lo
                    sem_sp = "S" if sign else "V"
                    last = None
                    for kk in range(KCQ):
                        k0 = qt * KCQ + kk
                        lh = whl_hi[:, par * QW + kk * 128:par * QW + (kk + 1) * 128]
                        ll = whl_lo[:, par * QW + kk * 128:par * QW + (kk + 1) * 128]
                        for tt in range(TT):
                            ps = psA if tt == 0 else psB
                            rh, rl = rhs_act(k0, tt)
                            w0 = []
                            if kk == 0 and tt == 0:
                                w0 = [ev_sp[j]]
                                if qt == 0 and ni >= 2:
                                    w0.append(ev_ep.get(n_list[ni - 2]))
                            first = (qt == 0 and kk == 0)
                            is_last = (qt == QT - 1 and kk == KCQ - 1 and
                                       extra_group_mm is None)
                            qend = (kk == KCQ - 1 and tt == TT - 1)
                            mm(ps, lh, rh, first, False, waits=w0)
                            if npass == 3:
                                mm(ps, lh, rl, False, False)
                                last = mm(ps, ll, rh, False, is_last,
                                          inc="T" if qend else None)
                            else:
                                last = mm(ps, lh, rl, False, is_last,
                                          inc="T" if qend else None)
                    ev_tq[j] = last
                    j += 1
                evT = ev_tq[j - 1]
                if extra_group_mm is not None:
                    evT = extra_group_mm(ni, psA, psB)
                ev_ep[n] = epilogue(ni, evT)
            return ev_ep

        def ln_stats(src2d, m_b, r_b):
            ev_t = {}
            for k0 in range(KC):
                par = k0 % 2
                xs = xst[:, par * TOK:(par + 1) * TOK]
                xq = evac[:, par * TOK:(par + 1) * TOK]
                ev_d = P.dma("sync", xs, src2d[k0 * 128:(k0 + 1) * 128, :],
                             waits=[ev_t.get(k0 - 2)], sem="dX%d" % par)
                ev_s = act(xs, AF.Square, xq, waits=[ev_d, ev_t.get(k0 - 2)])
                first, lastk = (k0 == 0), (k0 == KC - 1)
                mm(pb[4][0:1, :], ones_f[:, :], xs[:, 0:TW], first, lastk,
                   waits=[ev_d])
                mm(pb[5][0:1, :], ones_f[:, :], xs[:, TW:TOK], first, lastk)
                mm(pb[6][0:1, :], ones_f[:, :], xq[:, 0:TW], first, lastk,
                   waits=[ev_s])
                ev_t[k0] = mm(pb[7][0:1, :], ones_f[:, :], xq[:, TW:TOK], first,
                              lastk, inc="T")
            P.gate()
            scopy(pb[4][0:1, :], m_b[0:1, 0:TW], scale=1.0 / 4096.0)
            scopy(pb[5][0:1, :], m_b[0:1, TW:TOK], scale=1.0 / 4096.0)
            scopy(pb[6][0:1, :], stat[0:1, 0:TW], scale=1.0 / 4096.0)
            scopy(pb[7][0:1, :], stat[0:1, TW:TOK], scale=1.0 / 4096.0)
            P.gate()
            tt_(m_b[0:1, 0:TOK], ALU.mult, m_b[0:1, 0:TOK], r_b[0:1, 0:TOK])
            P.gate()
            tt_(stat[0:1, 0:TOK], ALU.subtract, r_b[0:1, 0:TOK],
                stat[0:1, 0:TOK])
            P.gate()
            act(stat[0:1, 0:TOK], AF.Sqrt, stat[0:1, 0:TOK], bias=EPS)
            P.gate()
            recip(stat[0:1, 0:TOK], r_b[0:1, 0:TOK])
            P.gate()
            for tt in range(TT):
                mm(pb[4 + tt], onesr_f[:, :], m_b[0:1, tt * TW:(tt + 1) * TW],
                   True, True)
                mm(pb[6 + tt], onesr_f[:, :],
                   r_b[0:1, tt * TW:(tt + 1) * TW], True, True,
                   inc="T" if tt else None)
            P.gate()
            for tt in range(TT):
                scopy(pb[4 + tt][:, :], m_b[:, tt * TW:(tt + 1) * TW])
                scopy(pb[6 + tt][:, :], r_b[:, tt * TW:(tt + 1) * TW])
            P.gate()

        def ln_apply_and_router(src2d, m_b, r_b):
            ev_t = {}
            for k0 in range(KC):
                par = k0 % 2
                xs = xst[:, par * TOK:(par + 1) * TOK]
                xm = evac[:, par * TOK:(par + 1) * TOK]
                ev_d = P.dma("sync", xs, src2d[k0 * 128:(k0 + 1) * 128, :],
                             waits=[ev_t.get(k0 - 2)], sem="dX%d" % par)
                e1 = tt_(xs, ALU.subtract, m_b, xm, waits=[ev_d, ev_t.get(k0 - 2)])
                e2 = tt_(xm, ALU.mult, r_b, xm, waits=[e1])
                hi = act_hi[:, k0 * TOK:(k0 + 1) * TOK]
                lo = act_lo[:, k0 * TOK:(k0 + 1) * TOK]
                eh, el = split(xm, hi, lo, waits=[e2])
                first, lastk = (k0 == 0), (k0 == KC - 1)
                gh = g0h[:, k0 * 8:(k0 + 1) * 8]
                gl = g0l[:, k0 * 8:(k0 + 1) * 8]
                ev_mm = None
                for tt in range(TT):
                    ps = pb[6][0:8, :] if tt == 0 else pb[7][0:8, :]
                    hs = hi[:, tt * TW:(tt + 1) * TW]
                    ls = lo[:, tt * TW:(tt + 1) * TW]
                    mm(ps, gh, hs, first, False, waits=[el] if tt == 0 else [])
                    mm(ps, gh, ls, False, False)
                    ev_mm = mm(ps, gl, hs, False, lastk,
                               inc="T" if tt == 1 else None)
                ev_t[k0] = ev_mm
            P.gate()

        ph1 = ExitStack()
        mstage = ph1.enter_context(nc.sbuf_tensor("mstage", [128, QW], I32))
        scb_h = ph1.enter_context(nc.sbuf_tensor("scb_h", [8, D], BF16))
        scb_l = ph1.enter_context(nc.sbuf_tensor("scb_l", [8, D], BF16))
        ph1r = ExitStack()
        lgs = ph1r.enter_context(nc.sbuf_tensor("lgs0", [8, TOK], F32))
        rting = lgs

        m_b = gbuf[:, 0:TOK]
        r_b = gbuf[:, TOK:2 * TOK]
        ln_stats(x0, m_b, r_b)
        ln_apply_and_router(x0, m_b, r_b)

        tt_(g0s[:, :], ALU.mult, g0s[:, :], evac[:, 0:KC * 8])
        P.gate()
        for k0 in range(KC):
            mm(pb[0][0:1, 0:8], ones_f[:, :], evac[:, k0 * 8:(k0 + 1) * 8],
               k0 == 0, k0 == KC - 1, inc="T" if k0 == KC - 1 else None)
        P.gate()
        act(pb[0][0:1, 0:8], AF.Sqrt, stat[0:1, 0:8])
        P.gate()
        P.op("vector", lambda e: e.tensor_reduce(out=stat[0:1, 8:9],
                                                 in_=stat[0:1, 0:8],
                                                 axis=AX.X, op=ALU.max), (), inc="V")
        P.gate()
        tsc(stat[0:1, 8:9], -64.0, ALU.mult, stat[0:1, 9:10])
        P.gate()
        mm(pb[2][0:8, 0:1], onesr_f[0:1, 0:8], stat[0:1, 9:10], True, True, inc="T")
        P.gate()
        scopy(pb[2][0:8, 0:1], bcol[:, 0:1])
        scopy(pb[6][0:8, :], lgs[:, 0:TW])
        scopy(pb[7][0:8, :], lgs[:, TW:TOK])
        P.gate()
        act(lgs[:, :], AF.Exp, lgs[:, :], bias=bcol[:, 0:1])
        P.gate()
        for tt in range(TT):
            mm(pb[0 + tt][0:1, :], ones_f[0:8, 0:1], lgs[:, tt * TW:(tt + 1) * TW],
               True, True, inc="T" if tt else None)
        P.gate()
        for tt in range(TT):
            recip(pb[0 + tt][0:1, :], stat[0:1, tt * TW:(tt + 1) * TW])
        P.gate()
        for tt in range(TT):
            mm(pb[2 + tt][0:8, :], onesr_f[0:1, 0:8],
               stat[0:1, tt * TW:(tt + 1) * TW], True, True,
               inc="T" if tt else None)
        P.gate()
        for tt in range(TT):
            tt_(lgs[:, tt * TW:(tt + 1) * TW], ALU.mult, pb[2 + tt][0:8, :],
                rting[:, tt * TW:(tt + 1) * TW])
        P.gate()
        split(rting[:, :], rt_hi[:, :], rt_lo[:, :])
        P.gate()
        ph1r.close()

        for half in range(2):
            P.dma("sync", wstage[0:8, 0:QW], is0[:, half * 2048:half * 2048 + QW])
            P.dma("sync", wstage[0:8, QW:2 * QW],
                  is0[:, half * 2048 + QW:(half + 1) * 2048])
            P.gate()
            split(wstage[0:8, 0:2 * QW],
                  scb_h[:, half * 2048:(half + 1) * 2048],
                  scb_l[:, half * 2048:(half + 1) * 2048])
            P.gate()

        ev_s0dma = {}

        def ep_s0(ni, evT):
            par = ni % 2
            ev1 = scopy(pb[0 + 2 * par][:, :], evac[:, par * TOK:par * TOK + TW],
                        waits=[evT, ev_s0dma.get(ni - 2)])
            ev2 = scopy(pb[1 + 2 * par][:, :],
                        evac[:, par * TOK + TW:(par + 1) * TOK])
            ev_s0dma[ni] = P.dma("gpsimd", scr_s0[ni * 128:(ni + 1) * 128, :],
                                 evac[:, par * TOK:(par + 1) * TOK], waits=[ev2],
                                 sem="dS0a" if par == 0 else "dS0b")
            return ev2

        stream_proj(w0a, list(range(KC)),
                    lambda ni: (pb[0 + 2 * (ni % 2)], pb[1 + 2 * (ni % 2)]),
                    ep_s0, mask2d=m0a, npass=3, mstage=mstage)
        P.gate()

        ev_xs = {}
        for k0 in range(KC):
            par = k0 % 2
            psA, psB = pb[4 + 2 * par], pb[5 + 2 * par]
            ish = scb_h[:, k0 * 128:(k0 + 1) * 128]
            isl = scb_l[:, k0 * 128:(k0 + 1) * 128]
            evm = None
            for tt, ps in ((0, psA), (1, psB)):
                rh = rt_hi[:, tt * TW:(tt + 1) * TW]
                rl = rt_lo[:, tt * TW:(tt + 1) * TW]
                mm(ps, ish, rh, True, False,
                   waits=[ev_xs.get(k0 - 2)] if tt == 0 else [])
                mm(ps, ish, rl, False, False)
                evm = mm(ps, isl, rh, False, True, inc="T" if tt == 1 else None)
            hi = act_hi[:, k0 * TOK:(k0 + 1) * TOK]
            lo = act_lo[:, k0 * TOK:(k0 + 1) * TOK]
            xm = evac[:, par * TOK:(par + 1) * TOK]
            e1 = tt_(hi, ALU.add, lo, xm, waits=[evm])
            e2a = tt_(xm[:, 0:TW], ALU.mult, psA[:, :], xm[:, 0:TW], waits=[e1])
            e2b = tt_(xm[:, TW:TOK], ALU.mult, psB[:, :], xm[:, TW:TOK],
                      waits=[e2a])
            ev_xs[k0] = e2b
            split(xm, hi, lo, waits=[e2b])
        P.gate()

        for half in range(2):
            P.dma("sync", wstage[0:8, 0:QW], os0[:, half * 2048:half * 2048 + QW])
            P.dma("sync", wstage[0:8, QW:2 * QW],
                  os0[:, half * 2048 + QW:(half + 1) * 2048])
            P.gate()
            split(wstage[0:8, 0:2 * QW],
                  scb_h[:, half * 2048:(half + 1) * 2048],
                  scb_l[:, half * 2048:(half + 1) * 2048])
            P.gate()

        ev_x1dma, ev_osc, ev_ep_t1, ev_gcons, ev_scons, ev_xcons = ({} for _ in range(6))

        def ep_g0(ni, evT):
            par = ni % 2
            osh = scb_h[:, ni * 128:(ni + 1) * 128]
            osl = scb_l[:, ni * 128:(ni + 1) * 128]
            evos = None
            for tt in range(TT):
                ps = pb[4 + 2 * par + tt]
                rh = rt_hi[:, tt * TW:(tt + 1) * TW]
                rl = rt_lo[:, tt * TW:(tt + 1) * TW]
                mm(ps, osh, rh, True, False,
                   waits=[ev_osc.get(ni - 2)] if tt == 0 else [])
                mm(ps, osh, rl, False, False)
                evos = mm(ps, osl, rh, False, True, inc="T" if tt == 1 else None)
            gpsA, gpsB = pb[0 + 2 * par], pb[1 + 2 * par]
            t1 = evac[:, par * TOK:(par + 1) * TOK]
            gg = gbuf[:, par * TOK:(par + 1) * TOK]
            evc1 = scopy(pb[4 + 2 * par][:, :], gg[:, 0:TW],
                         waits=[evos, ev_gcons.get(ni - 2)])
            evc2 = scopy(pb[5 + 2 * par][:, :], gg[:, TW:TOK])
            ev_o1 = tt_(gpsA[:, :], ALU.mult, gg[:, 0:TW], t1[:, 0:TW],
                        waits=[evT, evc1, ev_ep_t1.get(ni - 2)])
            ev_o2 = tt_(gpsB[:, :], ALU.mult, gg[:, TW:TOK], t1[:, TW:TOK],
                        waits=[evc2])
            ev_osc[ni] = ev_o2
            ev_g = act(t1, AF.Silu, gg, bias=b0fs[:, ni:ni + 1],
                       waits=[ev_o2])
            xx = xst[:, par * TOK:(par + 1) * TOK]
            ev_sl = P.dma("sync", xx, scr_s0[ni * 128:(ni + 1) * 128, :],
                          waits=[ev_s0dma[ni], ev_scons.get(ni - 2)],
                          sem="dX%d" % par)
            ev_sg = tt_(xx, ALU.mult, gg, gg, waits=[ev_g, ev_sl])
            ev_scons[ni] = ev_sg
            ev_xl = P.dma("sync", xx, x0[ni * 128:(ni + 1) * 128, :],
                          waits=[ev_sg], sem="dX%d" % par)
            ev_x1 = tt_(xx, ALU.add, gg, gg, waits=[ev_xl])
            ev_xcons[ni] = ev_x1
            ev_ep_t1[ni] = ev_x1
            ev_x1dma[ni] = P.dma("gpsimd", scr_x1[ni * 128:(ni + 1) * 128, :],
                                 gg, waits=[ev_x1], sem="dXW%d" % par)
            ev_gcons[ni] = ev_x1dma[ni]
            return ev_o2

        stream_proj(w0f, list(range(KC)),
                    lambda ni: (pb[0 + 2 * (ni % 2)], pb[1 + 2 * (ni % 2)]),
                    ep_g0, sign=True, npass=2)
        P.gate()
        ph1.close()

        ph2 = ExitStack()
        lgs = ph2.enter_context(nc.sbuf_tensor("lgs1", [8, TOK], F32))
        rting = lgs
        t2s = ph2.enter_context(nc.sbuf_tensor("t2s", [128, 256], F32))

        P.dma("sync", g0s[:, :], g1[:, :], sem="dO")
        P.gate()
        split(g0s[:, :], g0h[:, :], g0l[:, :])
        P.gate()
        m_b2 = gbuf[:, 0:TOK]
        r_b2 = gbuf[:, TOK:2 * TOK]
        ln_stats(scr_x1, m_b2, r_b2)
        ln_apply_and_router(scr_x1, m_b2, r_b2)

        scopy(pb[6][0:8, :], lgs[:, 0:TW])
        scopy(pb[7][0:8, :], lgs[:, TW:TOK])
        P.gate()
        for c in range(8):
            P.op("tensor", lambda e, c=c: e.transpose(
                out=pb[0][:, c * 8:(c + 1) * 8], in_=lgs[:, c * 128:(c + 1) * 128],
                identity=idm[0:8, 0:8]), (), inc="T" if c == 7 else None)
        P.gate()
        scopy(pb[0][:, 0:64], t2s[:, 0:64])
        P.gate()
        P.op("vector", lambda e: e.tensor_reduce(
            out=t2s[:, 64:72], in_=t2s[:, 0:64].rearrange("p (c e) -> p c e", e=8),
            axis=AX.X, op=ALU.max), (), inc="V")
        P.gate()
        tsc(t2s[:, 64:72], -1.0, ALU.mult, t2s[:, 72:80])
        P.gate()
        for c in range(8):
            tsc(t2s[:, c * 8:(c + 1) * 8], t2s[:, 64 + c:65 + c], ALU.is_equal,
                t2s[:, 80 + c * 8:88 + c * 8])
        P.gate()
        for c in range(8):
            P.op("vector", lambda e, c=c: e.scalar_tensor_tensor(
                out=t2s[:, 144 + c * 8:152 + c * 8],
                in0=t2s[:, 80 + c * 8:88 + c * 8], scalar=-1e30, op0=ALU.mult,
                in1=t2s[:, c * 8:(c + 1) * 8], op1=ALU.add), (), inc="V")
        P.gate()
        P.op("vector", lambda e: e.tensor_reduce(
            out=t2s[:, 208:216], in_=t2s[:, 144:208].rearrange("p (c e) -> p c e", e=8),
            axis=AX.X, op=ALU.max), (), inc="V")
        P.gate()
        tt_(t2s[:, 208:216], ALU.subtract, t2s[:, 64:72], t2s[:, 216:224])
        P.gate()
        act(t2s[:, 216:224], AF.Exp, t2s[:, 216:224])
        P.gate()
        tsc(t2s[:, 216:224], 1.0, ALU.add, t2s[:, 216:224])
        P.gate()
        recip(t2s[:, 216:224], t2s[:, 224:232])
        P.gate()
        tsc(t2s[:, 224:232], 2.0, ALU.mult, t2s[:, 224:232])
        P.gate()
        for c in range(8):
            act(t2s[:, c * 8:(c + 1) * 8], AF.Exp, t2s[:, 80 + c * 8:88 + c * 8],
                bias=t2s[:, 72 + c:73 + c])
            tsc(t2s[:, c * 8:(c + 1) * 8], t2s[:, 208 + c:209 + c], ALU.is_ge,
                t2s[:, 144 + c * 8:152 + c * 8])
        P.gate()
        for c in range(8):
            tt_(t2s[:, 80 + c * 8:88 + c * 8], ALU.mult,
                t2s[:, 144 + c * 8:152 + c * 8], t2s[:, 80 + c * 8:88 + c * 8])
        P.gate()
        for c in range(8):
            tsc(t2s[:, 80 + c * 8:88 + c * 8], t2s[:, 224 + c:225 + c], ALU.mult,
                t2s[:, 80 + c * 8:88 + c * 8])
        P.gate()
        for c in range(8):
            ps = pb[2] if c < 4 else pb[3]
            P.op("tensor", lambda e, c=c, ps=ps: e.transpose(
                out=ps[0:8, (c % 4) * 128:(c % 4 + 1) * 128],
                in_=t2s[:, 80 + c * 8:88 + c * 8],
                identity=idm[:, :]), (), inc="T" if c == 7 else None)
        P.gate()
        scopy(pb[2][0:8, :], rting[:, 0:TW])
        scopy(pb[3][0:8, :], rting[:, TW:TOK])
        P.gate()
        split(rting[:, :], rt_hi[:, :], rt_lo[:, :])
        P.gate()
        ph2.close()

        ph3h = ExitStack()
        h1h = ph3h.enter_context(nc.sbuf_tensor("h1h", [64, TOK], BF16))
        h1l = ph3h.enter_context(nc.sbuf_tensor("h1l", [64, TOK], BF16))

        ph2t = ExitStack()
        v64 = ph2t.enter_context(nc.sbuf_tensor("v64", [64, 24], F32))
        v64i = ph2t.enter_context(nc.sbuf_tensor("v64i", [64, 1], I32))
        mi64 = ph2t.enter_context(nc.sbuf_tensor("mi64", [64, 512], I32))
        u_i = mi64[0:1, :]
        t64a = xst[0:64, 1024:1536]
        t64b = xst[0:64, 1536:2048]
        cq = ph2t.enter_context(nc.sbuf_tensor("cq", [64, TOK], F32))
        sq_t = ph2t.enter_context(nc.sbuf_tensor("sq_t", [64, TOK], F32))
        ck = ph2t.enter_context(nc.sbuf_tensor("ck", [64, TOK], F32))
        sk_t = ph2t.enter_context(nc.sbuf_tensor("sk_t", [64, TOK], F32))
        qrot = xst[:, 0:TOK]
        r64 = gbuf[0:64, 0:512]
        u_row = gbuf[0:1, 512:1024]
        qsp_h = ph2t.enter_context(nc.sbuf_tensor("qsp_h", [128, TOK], BF16))
        qsp_l = ph2t.enter_context(nc.sbuf_tensor("qsp_l", [128, TOK], BF16))

        P.serial = True
        memset(v64[:, 0:1], 10000.0)
        iota(v64i[:, :], [[1, 1]], 0, 1, inc="G")
        iota(u_i[:, :], [[1, 512]], 0, 0, inc="G")
        act(v64[:, 0:1], AF.Ln, v64[:, 0:1])
        scopy(v64i[:, :], v64[:, 1:2])
        scopy(u_i[:, :], u_row)
        tt_(v64[:, 1:2], ALU.mult, v64[:, 0:1], v64[:, 2:3])
        tsc(v64[:, 2:3], -1.0 / 64.0, ALU.mult, v64[:, 2:3])
        act(v64[:, 2:3], AF.Exp, v64[:, 3:4])
        fvec = v64[:, 3:4]

        def sincos_block(ang_sbuf, out_sin, out_cos):
            for which, out in (("s", out_sin), ("c", out_cos)):
                if which == "c":
                    tsc(ang_sbuf, PIH, ALU.add, ang_sbuf)
                tsc(ang_sbuf, INV2PI, ALU.mult, r64)
                P.op("vector", lambda e, a=r64, o=mi64[:, :]:
                     e.tensor_copy(out=o, in_=a), (), inc="V")
                P.gate()
                scopy(mi64[:, :], r64)
                P.op("vector", lambda e, a=r64, b=ang_sbuf, o=r64:
                     e.scalar_tensor_tensor(out=o, in0=a, scalar=NEG2PI,
                                            op0=ALU.mult, in1=b, op1=ALU.add),
                     (), inc="V")
                P.gate()
                tsc(r64, PI_IN, ALU.min, r64)
                tsc(r64, -PI_IN, ALU.max, r64)
                act(r64, AF.Sin, out)

        for blk in range(2):
            sl = slice(blk * 512, blk * 512 + 512)
            tt_(qbc[0:64, blk:blk + 1], ALU.mult, fvec, v64[:, 8:9])
            mm(pb[4][0:64, :], onesr_f[0:1, 0:64], u_row, True, True, inc="T")
            tsc(pb[4][0:64, :], fvec, ALU.mult, t64a, s2=v64[:, 8:9], op1=ALU.add)
            sincos_block(t64a, sk_t[:, sl], ck[:, sl])
            tsc(ck[:, sl], RSCALE, ALU.mult, cq[:, sl])
            tsc(sk_t[:, sl], RSCALE, ALU.mult, sq_t[:, sl])
        P.serial = False
        P.gate()

        def rope_apply(psA, psB, ctab, stab, out_f32, waits):
            last = None
            for tt, ps in ((0, psA), (1, psB)):
                c = ctab[:, tt * TW:(tt + 1) * TW]
                s = stab[:, tt * TW:(tt + 1) * TW]
                oE = out_f32[0:64, tt * TW:(tt + 1) * TW]
                oO = out_f32[64:128, tt * TW:(tt + 1) * TW]
                w = waits if tt == 0 else [last]
                e1 = tt_(ps[0:64, :], ALU.mult, c, t64a[:, :], waits=w)
                e2 = tt_(ps[64:128, :], ALU.mult, s, t64b[:, :], waits=[e1])
                e3 = tt_(t64a[:, :], ALU.subtract, t64b[:, :], oE, waits=[e2])
                e4 = tt_(ps[0:64, :], ALU.mult, s, t64a[:, :], waits=[e3])
                e5 = tt_(ps[64:128, :], ALU.mult, c, t64b[:, :], waits=[e4])
                last = tt_(t64a[:, :], ALU.add, t64b[:, :], oO, waits=[e5])
            return last

        ev_ccdma = {}
        ev_kshi = {}

        def ep_k(ni, evT):
            ev_r = rope_apply(pb[0 + 2 * (ni % 2)], pb[1 + 2 * (ni % 2)],
                              ck, sk_t, qrot,
                              waits=[evT, ev_ccdma.get(ni - 1)])
            ev_ccdma[ni] = P.dma("gpsimd", cc_in[ni * 128:(ni + 1) * 128, :],
                                 qrot[:, :], waits=[ev_r], sem="dCC")
            return ev_r

        stream_proj(w1k, list(range(NKV)),
                    lambda ni: (pb[0 + 2 * (ni % 2)], pb[1 + 2 * (ni % 2)]),
                    ep_k)
        P.gate()

        ev_vt, ev_vsp = {}, {}
        for dvh in range(2):
            for k0 in range(KC):
                j = dvh * KC + k0
                par = j % 2
                wsl = slice(par * QW, par * QW + TW)
                ev_d = P.dma("sync", wstage[:, wsl],
                             w1v[k0 * 128:(k0 + 1) * 128, dvh * TW:(dvh + 1) * TW],
                             waits=[ev_vsp.get(j - 2)], sem="dW%d" % par)
                eh = scopy(wstage[:, wsl], whl_hi[:, wsl],
                           waits=[ev_d, ev_vt.get(j - 2)])
                el = tt_(wstage[:, wsl], ALU.subtract, whl_hi[:, wsl],
                         whl_lo[:, wsl], waits=[eh])
                ev_vsp[j] = el
                ev = None
                for tc in range(8):
                    ps = pb[tc]
                    lhH = act_hi[:, k0 * TOK + tc * 128:k0 * TOK + (tc + 1) * 128]
                    lhL = act_lo[:, k0 * TOK + tc * 128:k0 * TOK + (tc + 1) * 128]
                    first, lastk = (k0 == 0), (k0 == KC - 1)
                    mm(ps, lhH, whl_hi[:, wsl], first, False,
                       waits=[el] if tc == 0 else [])
                    mm(ps, lhL, whl_hi[:, wsl], False, False)
                    ev = mm(ps, lhH, whl_lo[:, wsl], False, lastk,
                            inc="T" if tc == 7 else None)
                ev_vt[j] = ev
            P.gate()
            for tc in range(8):
                buf = (evac if tc < 4 else gbuf)[:, (tc % 4) * TW:(tc % 4 + 1) * TW]
                evc = scopy(pb[tc][:, :], buf)
                P.dma("gpsimd", cc_in[1024 + tc * 128:1024 + (tc + 1) * 128,
                                      dvh * TW:(dvh + 1) * TW], buf, waits=[evc],
                      sem="dCC")
            P.gate()

        ev_cc = P.op("gpsimd", lambda e: e.collective_compute(
            "AllGather", ALU.bypass,
            replica_groups=[[0, 1], [2, 3], [4, 5], [6, 7]],
            ins=[cc_in[:, :]], outs=[cc_out[:, :]]), (), inc="C")

        ev_qdma = {}

        def ep_q(ni, evT):
            ev_r = rope_apply(pb[0 + 2 * (ni % 2)], pb[1 + 2 * (ni % 2)],
                              cq, sq_t, qrot,
                              waits=[evT, ev_qdma.get(ni - 1)])
            eh, el = split(qrot[:, :], qsp_h[:, :], qsp_l[:, :], waits=[ev_r])
            d1 = P.dma("gpsimd", scr_qh[ni * 128:(ni + 1) * 128, :], qsp_h[:, :],
                       waits=[el], sem="dQW")
            d2 = P.dma("gpsimd", scr_ql[ni * 128:(ni + 1) * 128, :], qsp_l[:, :],
                       waits=[el], sem="dQW")
            ev_qdma[ni] = d2
            return ev_r

        stream_proj(w1q, list(range(NQH)),
                    lambda ni: (pb[0 + 2 * (ni % 2)], pb[1 + 2 * (ni % 2)]),
                    ep_q)
        P.gate()

        P.dma("sync", wstage[:, 0:QW], a1[:, 0:QW], sem="dO")
        P.dma("sync", wstage[:, QW:2 * QW], a1[:, QW:2 * QW], sem="dO")
        P.gate()
        split(wstage[:, :], whl_hi[:, :], whl_lo[:, :])
        P.gate()
        for k0 in range(KC):
            lh = whl_hi[:, k0 * 64:(k0 + 1) * 64]
            ll = whl_lo[:, k0 * 64:(k0 + 1) * 64]
            for tt in range(TT):
                ps = pb[4 + tt][0:64, :]
                rh, rl = rhs_act(k0, tt)
                first, lastk = (k0 == 0), (k0 == KC - 1)
                mm(ps, lh, rh, first, False)
                mm(ps, lh, rl, False, False)
                mm(ps, ll, rh, False, lastk,
                   inc="T" if (lastk and tt == 1) else None)
        P.gate()
        scopy(pb[4][0:64, :], qrot[0:64, 0:TW])
        scopy(pb[5][0:64, :], qrot[0:64, TW:TOK])
        for tt in range(TT):
            ps = pb[6 + tt][0:64, :]
            mm(ps, repm[:, :], rt_hi[:, tt * TW:(tt + 1) * TW], True, False)
            mm(ps, repm[:, :], rt_lo[:, tt * TW:(tt + 1) * TW], False, True,
               inc="T" if tt else None)
        P.gate()
        for tt in range(TT):
            tt_(qrot[0:64, tt * TW:(tt + 1) * TW], ALU.mult, pb[6 + tt][0:64, :],
                qrot[0:64, tt * TW:(tt + 1) * TW])
        P.gate()
        split(qrot[0:64, 0:TOK], h1h[:, :], h1l[:, :])
        P.gate()
        ph2t.close()

        ph2b = ExitStack()
        b1st = ph2b.enter_context(nc.sbuf_tensor("b1st", [64, 2 * 128], F32))
        b1h = ph2b.enter_context(nc.sbuf_tensor("b1h", [64, 2 * 128], BF16))
        b1lo = ph2b.enter_context(nc.sbuf_tensor("b1lo", [64, 2 * 128], BF16))

        ev_g1dma = {}
        ev_b1mm = {}

        def lora_B(ni, psA, psB):
            par = ni % 2
            bsl = slice(par * 128, (par + 1) * 128)
            ev_d = P.dma("sync", b1st[:, bsl],
                         b1d[:, ni * 128:(ni + 1) * 128],
                         waits=[ev_b1sp.get(ni - 2)], sem="dB%d" % par)
            ehb, elb = split(b1st[:, bsl], b1h[:, bsl], b1lo[:, bsl],
                             waits=[ev_d, ev_b1mm.get(ni - 2)])
            ev_b1sp[ni] = elb
            ev = None
            for tt, ps in ((0, psA), (1, psB)):
                mm(ps, b1h[:, bsl], h1h[:, tt * TW:(tt + 1) * TW], False, False,
                   waits=[elb] if tt == 0 else [])
                mm(ps, b1h[:, bsl], h1l[:, tt * TW:(tt + 1) * TW], False, False)
                ev = mm(ps, b1lo[:, bsl], h1h[:, tt * TW:(tt + 1) * TW], False,
                        True, inc="T" if tt == 1 else None)
            ev_b1mm[ni] = ev
            return ev

        ev_b1sp = {}

        def ep_g1(ni, evT):
            par = ni % 2
            gg = gbuf[:, par * TOK:(par + 1) * TOK]
            psA, psB = pb[0 + 2 * par], pb[1 + 2 * par]
            ev1 = act(psA[:, :], AF.Silu, gg[:, 0:TW],
                      waits=[evT, ev_g1dma.get(ni - 2)])
            ev2 = act(psB[:, :], AF.Silu, gg[:, TW:TOK])
            ev_g1dma[ni] = P.dma("gpsimd", scr_g1[ni * 128:(ni + 1) * 128, :],
                                 gg, waits=[ev2], sem="dGW%d" % par)
            return ev2

        stream_proj(w1b, list(range(KC)),
                    lambda ni: (pb[0 + 2 * (ni % 2)], pb[1 + 2 * (ni % 2)]),
                    ep_g1, extra_group_mm=lora_B)
        P.gate()
        ph2b.close()
        ph3h.close()

        ph3 = ExitStack()
        ksb = wstage[:, 0:2048]
        vsb = wstage[:, 0:2048]
        khi = whl_hi[:, 0:2048]
        klo = whl_lo[:, 0:2048]
        vhi = ph3.enter_context(nc.sbuf_tensor("vhi", [128, 2048], BF16))
        vlo = ph3.enter_context(nc.sbuf_tensor("vlo", [128, 2048], BF16))
        qh2 = ph3.enter_context(nc.sbuf_tensor("qh2", [128, TOK], BF16))
        ql2 = ph3.enter_context(nc.sbuf_tensor("ql2", [128, TOK], BF16))
        et = ph3.enter_context(nc.sbuf_tensor("et", [128, 2 * TW], F32))
        ehi = ph3.enter_context(nc.sbuf_tensor("ehi", [128, 2 * TW], BF16))
        elo = ph3.enter_context(nc.sbuf_tensor("elo", [128, 2 * TW], BF16))
        negb = ph3.enter_context(nc.sbuf_tensor("negb", [1, TOK], BF16))
        viota = ph3.enter_context(nc.sbuf_tensor("viota", [128, TW], I32))

        iota(viota[:, :], [[1, TW]], 0, 0, inc="G")
        P.gate()

        RHO = [0, 1, 1, 0]
        LAM = [0, 0, 1, 1]

        for hv in range(NKV):
            kds = []
            for b in range(4):
                r0 = RHO[b] * 2048 + hv * 128
                c0 = LAM[b] * TW
                kds.append(P.dma("sync", ksb[:, b * TW:(b + 1) * TW],
                                 cc_out[r0:r0 + 128, c0:c0 + TW],
                                 waits=[("C", 1)], sem="dK"))
            ehK, elK = split(ksb[:, :], khi[:, :], klo[:, :], waits=kds)
            e1 = tt_(khi[:, :], ALU.add, klo[:, :], vsb[:, :], waits=[elK])
            e2 = act(vsb[:, :], AF.Square, vsb[:, :], waits=[e1])
            evn = None
            for b in range(4):
                evn = mm(pb[5][0:1, :], ones_f[:, :], vsb[:, b * TW:(b + 1) * TW],
                         True, True,
                         waits=[e2] if b == 0 else [evn], inc="T")
                evn = scopy(pb[5][0:1, :], gbuf[0:1, b * TW:(b + 1) * TW],
                            waits=[evn])
            P.gate()
            P.op("vector", lambda e: e.tensor_reduce(
                out=stat[0:1, 1024:1025], in_=gbuf[0:1, 0:2048], axis=AX.X,
                op=ALU.max), (), inc="V")
            P.gate()
            act(stat[0:1, 1024:1025], AF.Sqrt, stat[0:1, 1025:1026])
            P.gate()
            tsc(stat[0:1, 1025:1026], -1.0, ALU.mult, stat[0:1, 1026:1027])
            P.gate()
            vds = []
            for g in range(16):
                b = g // 4
                lc = LAM[b] * 4 + (g % 4)
                r0 = RHO[b] * 2048 + 1024 + lc * 128
                vds.append(P.dma("sync", vsb[:, g * 128:(g + 1) * 128],
                                 cc_out[r0:r0 + 128, hv * 128:(hv + 1) * 128],
                                 sem="dV"))
            ehV, elV = split(vsb[:, :], vhi[:, :], vlo[:, :], waits=vds)
            P.gate()

            for hq in range(hv * 4, hv * 4 + 4):
                par = hq % 2
                qh = qh2[:, :]
                ql = ql2[:, :]
                d1 = P.dma("sync", qh, scr_qh[hq * 128:(hq + 1) * 128, :],
                           sem="dQ")
                d2 = P.dma("sync", ql, scr_ql[hq * 128:(hq + 1) * 128, :],
                           sem="dQ")
                xq = xst[:, par * TOK:(par + 1) * TOK]
                e1 = tt_(qh, ALU.add, ql, xq, waits=[d1, d2])
                e2 = act(xq, AF.Square, xq, waits=[e1])
                evq = None
                for tt in range(TT):
                    evq = mm(pb[5][0:1, :], ones_f[:, :],
                             xq[:, tt * TW:(tt + 1) * TW], True, True,
                             waits=[e2] if tt == 0 else [evq], inc="T")
                    evq = act(pb[5][0:1, :], AF.Sqrt,
                              stat[0:1, tt * TW:(tt + 1) * TW], waits=[evq])
                evb = tsc(stat[0:1, 0:TOK], stat[0:1, 1026:1027], ALU.mult,
                          negb[0:1, :], waits=[evq])
                ev_exp, ev_av, ev_msk = {}, {}, {}
                for qi in range(2):
                    kcnt = KCNT[qi]
                    qsl = slice(qi * TW, (qi + 1) * TW)
                    for kc in range(kcnt):
                        p2 = kc % 2
                        psS = pb[p2]
                        esl = slice(p2 * TW, (p2 + 1) * TW)
                        w0 = [elK, evb, d2, ev_exp.get(kc - 2)]
                        mm(psS, khi[:, kc * 128:(kc + 1) * 128], qh[:, qsl],
                           True, False, waits=w0)
                        mm(psS, klo[:, kc * 128:(kc + 1) * 128], qh[:, qsl],
                           False, False)
                        mm(psS, khi[:, kc * 128:(kc + 1) * 128], ql[:, qsl],
                           False, False)
                        evS = mm(psS, onesr_b[:, :], negb[0:1, qsl], False, True,
                                 inc="T")
                        evE = act(psS[:, :], AF.Exp, et[:, esl],
                                  waits=[evS, ev_av.get(kc - 2)])
                        evEM = P.op("vector", lambda e, esl=esl, col=qi * 8 + kc:
                                    e.scalar_tensor_tensor(
                                        out=et[:, esl], in0=viota[:, :],
                                        scalar=thr[:, col:col + 1], op0=ALU.is_ge,
                                        in1=et[:, esl], op1=ALU.mult),
                                    [evE], inc="V")
                        evH = scopy(et[:, esl], ehi[:, esl], waits=[evEM])
                        evL = tt_(et[:, esl], ALU.subtract, ehi[:, esl],
                                  elo[:, esl], waits=[evH])
                        ev_exp[kc] = evE
                        first, lastk = (kc == 0), (kc == kcnt - 1)
                        mm(pb[3][0:1, :], ones_b[:, :], ehi[:, esl], first, False,
                           waits=[evL])
                        mm(pb[3][0:1, :], ones_b[:, :], elo[:, esl], False, lastk)
                        mm(pb[2], vhi[:, kc * 128:(kc + 1) * 128], ehi[:, esl],
                           first, False, waits=[elV] if first else [])
                        mm(pb[2], vlo[:, kc * 128:(kc + 1) * 128], ehi[:, esl],
                           False, False)
                        ev_av[kc] = mm(pb[2], vhi[:, kc * 128:(kc + 1) * 128],
                                       elo[:, esl], False, lastk, inc="T")
                    evR = recip(pb[3][0:1, :], gbuf[0:1, TW:2 * TW],
                                waits=[ev_av[kcnt - 1]])
                    evB = mm(pb[4], onesr_f[:, :], gbuf[0:1, TW:2 * TW],
                             True, True, waits=[evR], inc="T")
                    o1 = scopy(pb[2][:, :], gbuf[:, 0:TW], waits=[evB])
                    onrm = tt_(gbuf[:, 0:TW], ALU.mult, pb[4][:, :],
                               evac[:, 0:TW], waits=[o1])
                    dsts = act_hi[:, hq * TOK + qi * TW:hq * TOK + (qi + 1) * TW]
                    dstl = act_lo[:, hq * TOK + qi * TW:hq * TOK + (qi + 1) * TW]
                    eh2, el2 = split(evac[:, 0:TW], dsts, dstl, waits=[onrm])
                    P.gate()
        P.gate()
        ph3.close()

        ev_ocons, ev_gcons2, ev_xcons2 = {}, {}, {}

        def ep_out(ni, evT):
            par = ni % 2
            gg = gbuf[:, par * TOK:(par + 1) * TOK]
            xx = xst[:, par * TOK:(par + 1) * TOK]
            oo = evac[:, par * TOK:(par + 1) * TOK]
            ev_gl = P.dma("sync", gg, scr_g1[ni * 128:(ni + 1) * 128, :],
                          waits=[ev_gcons2.get(ni - 2)], sem="dB%d" % par)
            ev_xl = P.dma("sync", xx, scr_x1[ni * 128:(ni + 1) * 128, :],
                          waits=[ev_xcons2.get(ni - 2)], sem="dX%d" % par)
            e1 = tt_(pb[0 + 2 * par][:, :], ALU.mult, gg[:, 0:TW], oo[:, 0:TW],
                     waits=[evT, ev_gl, ev_ocons.get(ni - 2)])
            e2 = tt_(pb[1 + 2 * par][:, :], ALU.mult, gg[:, TW:TOK], oo[:, TW:TOK],
                     waits=[e1])
            ev_gcons2[ni] = e2
            e3 = tt_(xx, ALU.add, oo, oo, waits=[e2, ev_xl])
            ev_xcons2[ni] = e3
            ev_ocons[ni] = P.dma("gpsimd", out0[ni * 128:(ni + 1) * 128, :], oo,
                                 waits=[e3], sem="dOut" if par == 0 else "dOut1")
            return e2

        stream_proj(w1o, list(range(KC)),
                    lambda ni: (pb[0 + 2 * (ni % 2)], pb[1 + 2 * (ni % 2)]),
                    ep_out)
        P.final_wait("gpsimd", ("dOut", P.cnt["dOut"]))
        P.final_wait("gpsimd", ("dOut1", P.cnt["dOut1"]))

        with nc.Block() as block:
            P.emit(block, sems)

    return nc


_PERMQ = np.concatenate([h * 128 + np.concatenate([np.arange(0, 128, 2),
                                                   np.arange(1, 128, 2)])
                         for h in range(NQH)])
_PERMK = np.concatenate([h * 128 + np.concatenate([np.arange(0, 128, 2),
                                                   np.arange(1, 128, 2)])
                         for h in range(NKV)])


def _t4(w):
    return np.ascontiguousarray(
        w.reshape(KC, 128, KC, 128).transpose(2, 1, 0, 3).reshape(D, D))


def _t4k(w):
    return np.ascontiguousarray(
        w.reshape(KC, 128, NKV, 128).transpose(2, 1, 0, 3).reshape(NKV * 128, D))


_NC_CACHE = {}


def _get_nc(debug=False):
    if debug not in _NC_CACHE:
        _NC_CACHE[debug] = build_nc(debug)
    return _NC_CACHE[debug]


def make_in_maps(L0_attn_data, L0_attn_weight, L0_attn_mask, L0_ffn_weight,
                 L0_ffn_bias, L0_ffn_gate_w, L0_ffn_in_channel_scale,
                 L0_ffn_out_channel_scale, L1_attn_wq, L1_attn_wk, L1_attn_wv,
                 L1_attn_wo, L1_ffn_W_base, L1_ffn_gate_W, L1_ffn_A, L1_ffn_B):
    f = np.ascontiguousarray
    w0a = _t4(np.asarray(L0_attn_weight).T)
    m0a = _t4(np.asarray(L0_attn_mask).T.astype(np.int32))
    w0f = _t4(np.asarray(L0_ffn_weight).T)
    b0fv = f(np.asarray(L0_ffn_bias).reshape(KC, 128).T)
    g0v = f(np.asarray(L0_ffn_gate_w).T.reshape(KC, 128, 8)
            .transpose(1, 0, 2).reshape(128, KC * 8))
    is0v = f(np.asarray(L0_ffn_in_channel_scale))
    os0v = f(np.asarray(L0_ffn_out_channel_scale))
    w1qv = _t4(np.asarray(L1_attn_wq)[:, _PERMQ])
    w1kv = _t4k(np.asarray(L1_attn_wk)[:, _PERMK])
    w1vv = f(np.asarray(L1_attn_wv))
    w1ov = _t4(np.asarray(L1_attn_wo))
    w1bv = _t4(np.asarray(L1_ffn_W_base).T)
    g1v = f(np.asarray(L1_ffn_gate_W).T.reshape(KC, 128, 8)
            .transpose(1, 0, 2).reshape(128, KC * 8))
    a1v = f(np.asarray(L1_ffn_A).reshape(64, D).T.reshape(KC, 128, 64)
            .transpose(1, 0, 2).reshape(128, KC * 64))
    b1v = f(np.asarray(L1_ffn_B).transpose(0, 2, 1).reshape(64, D))
    x = np.asarray(L0_attn_data)

    in_maps = []
    for c in range(8):
        seq, role = c // 2, c % 2
        blocks = (0, 3) if role == 0 else (1, 2)
        xc = np.concatenate([x[seq, 512 * b:512 * b + 512, :] for b in blocks], 0)
        x0v = f(xc.T)
        qbv = np.array([[512.0 * blocks[0], 512.0 * blocks[1]]], np.float32)
        qs = np.zeros((64, 8), np.float32)
        qs[:, blocks[0]] = 1.0
        qs[:, 4 + blocks[1]] = 1.0
        in_maps.append({
            "x0": x0v, "w0a": w0a, "m0a": m0a, "w0f": w0f, "b0f": b0fv,
            "g0": g0v, "is0": is0v, "os0": os0v, "w1q": w1qv, "w1k": w1kv,
            "w1v": w1vv, "w1o": w1ov, "w1b": w1bv, "g1": g1v, "a1": a1v,
            "b1d": b1v, "qb": qbv, "qsel": qs,
        })
    return in_maps


def kernel(**inputs):
    in_maps = make_in_maps(**inputs)
    nc = _get_nc(False)
    res = run_bass_kernel_spmd(nc, in_maps, core_ids=list(range(8)))
    out = np.empty((4, 2048, D), np.float32)
    for c in range(8):
        seq, role = c // 2, c % 2
        blocks = (0, 3) if role == 0 else (1, 2)
        o = np.asarray(res.results[c]["out0"])
        for i, b in enumerate(blocks):
            out[seq, 512 * b:512 * b + 512, :] = o[:, 512 * i:512 * i + 512].T
    return out


# version 03
# speedup vs baseline: 1.2362x; 1.2362x over previous
import numpy as np
import concourse.bass as bass
import concourse.mybir as mybir
from concourse.bass_utils import run_bass_kernel_spmd
from contextlib import ExitStack

F32 = mybir.dt.float32
BF16 = mybir.dt.bfloat16
I32 = mybir.dt.int32
AF = mybir.ActivationFunctionType
ALU = mybir.AluOpType
AX = mybir.AxisListType

D = 4096
KC = 32
TOK = 1024
TT = 2
TW = 512
NQH = 32
NKV = 8
HD = 128
RSCALE = float(1.0 / np.float64(np.float32(np.sqrt(np.float64(HD)))))
EPS = 1e-5
QT = 4
QW = 1024
KCQ = QW // 128
KCNT = (8, 16)

ENGINES = ("sync", "scalar", "vector", "tensor", "gpsimd")

PI_IN = 3.1415925
INV2PI = float(1.0 / (2.0 * np.pi))
NEG2PI = float(-2.0 * np.pi)
PIH = float(np.pi / 2.0)
CONSTS_F32 = [-1.0, 2.0, 1.0 / 4096.0, EPS, RSCALE, -RSCALE, 0.5, 7.0,
              64.0, 10000.0, 1.0, 0.0, -1.0 / 64.0, -1e30,
              INV2PI, NEG2PI, PI_IN, -PI_IN, PIH, -64.0]
CONSTS_BF16 = [1.0, -1.0, 2.0, 0.0]


class Prog:
    def __init__(self):
        self.ops = {e: [] for e in ENGINES}
        self.cnt = {}
        self.gate_pending = {e: None for e in ENGINES}
        self.serial = False
        for s in ("dS", "dG", "T", "V", "S", "G", "C"):
            self.cnt[s] = 0

    def op(self, eng, fn, waits=(), inc=None, amt=1):
        w = [x for x in waits if x is not None]
        g = self.gate_pending[eng]
        if g is not None:
            w = list(g.items()) + w
            self.gate_pending[eng] = None
        ev = None
        if inc is not None:
            self.cnt[inc] += amt
            ev = (inc, self.cnt[inc])
        self.ops[eng].append((tuple(w), fn, (inc, amt) if inc else None))
        if self.serial and inc is not None:
            self.gate()
        return ev

    def dma(self, eng, out, in_, waits=(), sem=None):
        if sem is None:
            sem = "dS" if eng == "sync" else "dG"
        self.cnt.setdefault(sem, 0)
        return self.op(eng, lambda e, o=out, i=in_: e.dma_start(out=o, in_=i),
                       waits, inc=sem, amt=16)

    def gate(self):
        snap = {k: v for k, v in self.cnt.items() if v > 0}
        for e in ENGINES:
            self.gate_pending[e] = snap

    def final_wait(self, eng, ev):
        self.ops[eng].append(((ev,), None, None))

    def emit(self, block, sems):
        for eng_name in ENGINES:
            ops = self.ops[eng_name]
            if not ops:
                continue

            def make(ops):
                def body(eng):
                    waited = {}
                    for waits, fn, inc in ops:
                        merged = {}
                        for s, v in waits:
                            if v > merged.get(s, 0):
                                merged[s] = v
                        for s, v in merged.items():
                            if v > 0 and waited.get(s, 0) < v:
                                eng.wait_ge(sems[s], v)
                                waited[s] = v
                        if fn is None:
                            continue
                        instr = fn(eng)
                        if inc is not None:
                            instr.then_inc(sems[inc[0]], inc[1])
                return body

            getattr(block, eng_name)(make(ops))


def build_nc(debug=False):
    nc = bass.Bass("TRN2", target_bir_lowering=False)

    def din(name, shape, dtype=F32):
        return nc.dram_tensor(name, shape, dtype, kind="ExternalInput").ap()

    def dscr(name, shape, dtype=F32):
        dbg = debug if isinstance(debug, (tuple, list, set)) else ()
        kind = "ExternalOutput" if name in dbg else "Internal"
        return nc.dram_tensor(name, shape, dtype, kind=kind).ap()

    x0 = din("x0", [D, TOK])
    w0a = din("w0a", [D, D])
    m0a = din("m0a", [D, D], I32)
    w0f = din("w0f", [D, D])
    b0f = din("b0f", [128, KC])
    g0 = din("g0", [128, KC * 8])
    is0 = din("is0", [8, D])
    os0 = din("os0", [8, D])
    w1q = din("w1q", [D, D])
    w1k = din("w1k", [NKV * 128, D])
    w1v = din("w1v", [D, 1024])
    w1o = din("w1o", [D, D])
    w1b = din("w1b", [D, D])
    g1 = din("g1", [128, KC * 8])
    a1 = din("a1", [128, KC * 64])
    b1d = din("b1d", [64, D])
    qbin = din("qb", [1, 2])
    qselin = din("qsel", [64, 8])

    out0 = nc.dram_tensor("out0", [D, TOK], F32, kind="ExternalOutput").ap()
    scr_s0 = dscr("scr_s0", [D, TOK])
    scr_x1 = dscr("scr_x1", [D, TOK])
    scr_g1 = dscr("scr_g1", [D, TOK])
    scr_qh = dscr("scr_qh", [NQH * 128, TOK], BF16)
    scr_ql = dscr("scr_ql", [NQH * 128, TOK], BF16)
    cc_in = nc.dram_tensor("cc_in", [2048, 1024], F32).ap()
    dbg = debug if isinstance(debug, (tuple, list, set)) else ()
    dbg_rope = (nc.dram_tensor("dbg_rope", [64, 5248], F32,
                               kind="ExternalOutput").ap()
                if "dbg_rope" in dbg else None)
    cc_out = dscr("cc_out", [4096, 1024])

    DSEMS = ("dS", "dG", "dW0", "dW1", "dM", "dX0", "dX1", "dB0", "dB1",
             "dK", "dV", "dQ", "dO", "dS0a", "dS0b", "dXW0", "dXW1", "dQW",
             "dGW0", "dGW1", "dCC", "dOut", "dOut1")
    P = Prog()
    for _s in DSEMS:
        P.cnt.setdefault(_s, 0)
    st = ExitStack()
    with st:
        e = st.enter_context

        act_hi = e(nc.sbuf_tensor("act_hi", [128, KC * TOK], BF16))
        act_lo = e(nc.sbuf_tensor("act_lo", [128, KC * TOK], BF16))
        wstage = e(nc.sbuf_tensor("wstage", [128, 2 * QW], F32))
        whl_hi = e(nc.sbuf_tensor("whl_hi", [128, 2 * QW], BF16))
        whl_lo = e(nc.sbuf_tensor("whl_lo", [128, 2 * QW], BF16))
        xst = e(nc.sbuf_tensor("xst", [128, 2 * TOK], F32))
        evac = e(nc.sbuf_tensor("evac", [128, 2 * TOK], F32))
        gbuf = e(nc.sbuf_tensor("gbuf", [128, 2 * TOK], F32))
        ones_f = e(nc.sbuf_tensor("ones_f", [128, 1], F32))
        ones_b = e(nc.sbuf_tensor("ones_b", [128, 1], BF16))
        onesr_f = e(nc.sbuf_tensor("onesr_f", [1, 128], F32))
        onesr_b = e(nc.sbuf_tensor("onesr_b", [1, 128], BF16))
        idm = e(nc.sbuf_tensor("idm", [128, 128], F32))
        idm_i = e(nc.sbuf_tensor("idm_i", [128, 128], I32))
        rt_hi = e(nc.sbuf_tensor("rt_hi", [8, TOK], BF16))
        rt_lo = e(nc.sbuf_tensor("rt_lo", [8, TOK], BF16))
        bcol = e(nc.sbuf_tensor("bcol", [8, 2], F32))
        g0s = e(nc.sbuf_tensor("g0s", [128, KC * 8], F32))
        g0h = e(nc.sbuf_tensor("g0h", [128, KC * 8], BF16))
        g0l = e(nc.sbuf_tensor("g0l", [128, KC * 8], BF16))
        stat = e(nc.sbuf_tensor("stat", [1, 1040], F32))
        b0fs = e(nc.sbuf_tensor("b0fs", [128, KC], F32))
        qbs = e(nc.sbuf_tensor("qbs", [1, 2], F32))
        qbc = e(nc.sbuf_tensor("qbc", [128, 2], F32))
        qsels = e(nc.sbuf_tensor("qsels", [64, 8], F32))
        thr = e(nc.sbuf_tensor("thr", [128, 24], F32))
        thr_i = e(nc.sbuf_tensor("thr_i", [128, 24], I32))
        repm = e(nc.sbuf_tensor("repm", [8, 64], BF16))
        rep8 = e(nc.sbuf_tensor("rep8", [8, 64], I32))
        csts = e(nc.sbuf_tensor("csts", [128, len(CONSTS_F32)], F32))
        cstb = e(nc.sbuf_tensor("cstb", [128, len(CONSTS_BF16)], BF16))

        for i, v in enumerate(CONSTS_F32):
            nc.const_aps.register(F32, v, csts[:, i:i + 1])
        for i, v in enumerate(CONSTS_BF16):
            nc.const_aps.register(BF16, v, cstb[:, i:i + 1])

        pb = [e(nc.psum_tensor(f"pb{i}", [128, TW], F32)) for i in range(8)]

        sems = {}
        for s in DSEMS + ("T", "V", "S", "G", "C"):
            sems[s] = e(nc.semaphore(s))

        def _ap(x):
            return x[:, :] if x.__class__.__name__.endswith("TensorHandle") else x

        def tt_(a, op, b, o, waits=(), eng="vector", inc="V"):
            return P.op(eng, lambda e, a=a, b=b, o=o, op=op:
                        e.tensor_tensor(out=o, in0=a, in1=b, op=op), waits, inc=inc)

        def tsc(in0, s1, op0, o, s2=None, op1=None, waits=(), eng="vector", inc="V"):
            def fn(e, i=in0, s1=s1, s2=s2, o=o, op0=op0, op1=op1):
                kw = {"op1": op1} if op1 is not None else {}
                return e.tensor_scalar(out=o, in0=i, scalar1=s1, scalar2=s2,
                                       op0=op0, **kw)
            return P.op(eng, fn, waits, inc=inc)

        def act(in_, func, o, scale=1.0, bias=0.0, waits=(), inc="S"):
            return P.op("scalar", lambda e, i=in_, o=o, f=func, sc=scale, b=bias:
                        e.activation(out=o, in_=i, func=f, scale=sc, bias=b),
                        waits, inc=inc)

        def scopy(in_, o, scale=None, waits=(), inc="S"):
            if scale is None:
                return P.op("scalar", lambda e, i=in_, o=o: e.copy(out=o, in_=i),
                            waits, inc=inc)
            return P.op("scalar", lambda e, i=in_, o=o, s=scale:
                        e.mul(out=o, in_=i, mul=s), waits, inc=inc)

        def mm(ps, lhsT, rhs, start, stop, waits=(), inc=None):
            return P.op("tensor",
                        lambda e, p=_ap(ps), l=_ap(lhsT), r=_ap(rhs), a=start, b=stop:
                        e.matmul(out=p, lhsT=l, rhs=r, start=a, stop=b),
                        waits, inc=inc)

        def recip(in_, o, waits=()):
            return P.op("vector", lambda e, i=in_, o=o:
                        e.reciprocal(out=o, in_=i), waits, inc="V")

        def iota(o, pattern, base, chm, waits=(), inc="G"):
            return P.op("gpsimd", lambda e, o=o, p=pattern, b=base, c=chm:
                        e.iota(out=o, pattern=p, base=b, channel_multiplier=c),
                        waits, inc=inc)

        def memset(ap, v, waits=(), inc="G"):
            return P.op("gpsimd", lambda e, a=ap, v=v: e.memset(a, v), waits, inc=inc)

        def split(src, hi, lo, waits=()):
            eh = scopy(src, hi, waits=waits)
            el = tt_(src, ALU.subtract, hi, lo, waits=[eh])
            return eh, el

        for i, v in enumerate(CONSTS_F32):
            memset(csts[:, i:i + 1], v)
        for i, v in enumerate(CONSTS_BF16):
            memset(cstb[:, i:i + 1], v)
        memset(ones_f[:, :], 1.0)
        memset(ones_b[:, :], 1.0)
        memset(onesr_f[:, :], 1.0)
        memset(onesr_b[:, :], 1.0)
        iota(idm_i[:, :], [[1, 128]], 0, -1)
        iota(rep8[:, :], [[1, 64]], 0, -8)
        for j in range(24):
            kc = j if j < 8 else (j - 8)
            iota(thr_i[:, j:j + 1], [[1, 1]], 128 * kc, 1)
        memset(bcol[:, :], 0.0)
        P.dma("sync", qbs[:, :], qbin[:, :], sem="dO")
        P.dma("sync", qsels[:, :], qselin[:, :], sem="dO")
        P.dma("sync", b0fs[:, :], b0f[:, :], sem="dO")
        P.dma("sync", g0s[:, :], g0[:, :], sem="dO")
        P.gate()
        tsc(idm_i[:, :], 0.0, ALU.is_equal, idm[:, :])
        tsc(rep8[:, :], 0.0, ALU.is_ge, xst[0:8, 0:64])
        tsc(rep8[:, :], 7.0, ALU.is_le, xst[0:8, 64:128])
        P.gate()
        tt_(xst[0:8, 0:64], ALU.mult, xst[0:8, 64:128], repm[:, :])
        scopy(thr_i[:, :], thr[:, :])
        mm(pb[0][:, 0:1], onesr_f[:, :], qbs[0:1, 0:1], True, True)
        mm(pb[1][:, 0:1], onesr_f[:, :], qbs[0:1, 1:2], True, True, inc="T")
        P.gate()
        scopy(pb[0][:, 0:1], qbc[:, 0:1])
        scopy(pb[1][:, 0:1], qbc[:, 1:2])
        P.gate()
        tsc(thr[:, 0:8], qbc[:, 0:1], ALU.subtract, thr[:, 0:8])
        tsc(thr[:, 8:24], qbc[:, 1:2], ALU.subtract, thr[:, 8:24])
        split(g0s[:, :], g0h[:, :], g0l[:, :])
        P.gate()

        def rhs_act(k0, tt):
            hi = act_hi[:, k0 * TOK + tt * TW: k0 * TOK + (tt + 1) * TW]
            lo = act_lo[:, k0 * TOK + tt * TW: k0 * TOK + (tt + 1) * TW]
            return hi, lo

        def stream_proj(src2d, n_list, ps_pair, epilogue, mask2d=None,
                        sign=False, npass=3, extra_group_mm=None, mstage=None):
            ev_sp, ev_tq, ev_ep, ev_wm = {}, {}, {}, {}
            j = 0
            for ni, n in enumerate(n_list):
                rows = slice(n * 128, (n + 1) * 128)
                psA, psB = ps_pair(ni)
                for qt in range(QT):
                    par = j % 2
                    sl = slice(par * QW, (par + 1) * QW)
                    csl = slice(qt * QW, (qt + 1) * QW)
                    ev_d = P.dma("sync", wstage[:, sl], src2d[rows, csl],
                                 waits=[ev_sp.get(j - 2)], sem="dW%d" % par)
                    if mask2d is not None:
                        ev_dm = P.dma("sync", mstage[:, 0:QW], mask2d[rows, csl],
                                      waits=[ev_wm.get(j - 1)], sem="dM")
                        ev_w = tt_(wstage[:, sl], ALU.mult, mstage[:, 0:QW],
                                   wstage[:, sl], waits=[ev_d, ev_dm,
                                                         ev_tq.get(j - 2)])
                        ev_wm[j] = ev_w
                        src_rdy = [ev_w]
                    else:
                        src_rdy = [ev_d, ev_tq.get(j - 2)]
                    if sign:
                        ev_hi = act(wstage[:, sl], AF.Sign, whl_hi[:, sl],
                                    waits=src_rdy)
                        ev_sp[j] = ev_hi
                    else:
                        ev_hi = scopy(wstage[:, sl], whl_hi[:, sl], waits=src_rdy)
                        ev_lo = tt_(wstage[:, sl], ALU.subtract, whl_hi[:, sl],
                                    whl_lo[:, sl], waits=[ev_hi])
                        ev_sp[j] = ev_lo
                    sem_sp = "S" if sign else "V"
                    last = None
                    for kk in range(KCQ):
                        k0 = qt * KCQ + kk
                        lh = whl_hi[:, par * QW + kk * 128:par * QW + (kk + 1) * 128]
                        ll = whl_lo[:, par * QW + kk * 128:par * QW + (kk + 1) * 128]
                        rh0, rl0 = rhs_act(k0, 0)
                        rh1, rl1 = rhs_act(k0, 1)
                        w0 = []
                        if kk == 0:
                            w0 = [ev_sp[j]]
                            if qt == 0 and ni >= 2:
                                w0.append(ev_ep.get(n_list[ni - 2]))
                        first = (qt == 0 and kk == 0)
                        is_last = (qt == QT - 1 and kk == KCQ - 1 and
                                   extra_group_mm is None)
                        qend = (kk == KCQ - 1)
                        mm(psA, lh, rh0, first, False, waits=w0)
                        mm(psA, lh, rl0, False, is_last and npass == 2,
                           inc=None)
                        mm(psB, lh, rh1, first, False)
                        last = mm(psB, lh, rl1, False, is_last and npass == 2,
                                  inc="T" if (qend and npass == 2) else None)
                        if npass == 3:
                            mm(psA, ll, rh0, False, is_last)
                            last = mm(psB, ll, rh1, False, is_last,
                                      inc="T" if qend else None)
                    ev_tq[j] = last
                    j += 1
                evT = ev_tq[j - 1]
                if extra_group_mm is not None:
                    evT = extra_group_mm(ni, psA, psB)
                ev_ep[n] = epilogue(ni, evT)
            return ev_ep

        def ln_stats(src2d, m_b, r_b):
            ev_t = {}
            for k0 in range(KC):
                par = k0 % 2
                xs = xst[:, par * TOK:(par + 1) * TOK]
                xq = evac[:, par * TOK:(par + 1) * TOK]
                ev_d = P.dma("sync", xs, src2d[k0 * 128:(k0 + 1) * 128, :],
                             waits=[ev_t.get(k0 - 2)], sem="dX%d" % par)
                ev_s = act(xs, AF.Square, xq, waits=[ev_d, ev_t.get(k0 - 2)])
                first, lastk = (k0 == 0), (k0 == KC - 1)
                mm(pb[4][0:1, :], ones_f[:, :], xs[:, 0:TW], first, lastk,
                   waits=[ev_d])
                mm(pb[5][0:1, :], ones_f[:, :], xs[:, TW:TOK], first, lastk)
                mm(pb[6][0:1, :], ones_f[:, :], xq[:, 0:TW], first, lastk,
                   waits=[ev_s])
                ev_t[k0] = mm(pb[7][0:1, :], ones_f[:, :], xq[:, TW:TOK], first,
                              lastk, inc="T")
            P.gate()
            scopy(pb[4][0:1, :], m_b[0:1, 0:TW], scale=1.0 / 4096.0)
            scopy(pb[5][0:1, :], m_b[0:1, TW:TOK], scale=1.0 / 4096.0)
            scopy(pb[6][0:1, :], stat[0:1, 0:TW], scale=1.0 / 4096.0)
            scopy(pb[7][0:1, :], stat[0:1, TW:TOK], scale=1.0 / 4096.0)
            P.gate()
            tt_(m_b[0:1, 0:TOK], ALU.mult, m_b[0:1, 0:TOK], r_b[0:1, 0:TOK])
            P.gate()
            tt_(stat[0:1, 0:TOK], ALU.subtract, r_b[0:1, 0:TOK],
                stat[0:1, 0:TOK])
            P.gate()
            act(stat[0:1, 0:TOK], AF.Sqrt, stat[0:1, 0:TOK], bias=EPS)
            P.gate()
            recip(stat[0:1, 0:TOK], r_b[0:1, 0:TOK])
            P.gate()
            for tt in range(TT):
                mm(pb[4 + tt], onesr_f[:, :], m_b[0:1, tt * TW:(tt + 1) * TW],
                   True, True)
                mm(pb[6 + tt], onesr_f[:, :],
                   r_b[0:1, tt * TW:(tt + 1) * TW], True, True,
                   inc="T" if tt else None)
            P.gate()
            for tt in range(TT):
                scopy(pb[4 + tt][:, :], m_b[:, tt * TW:(tt + 1) * TW])
                scopy(pb[6 + tt][:, :], r_b[:, tt * TW:(tt + 1) * TW])
            P.gate()

        def ln_apply_and_router(src2d, m_b, r_b):
            ev_t = {}
            for k0 in range(KC):
                par = k0 % 2
                xs = xst[:, par * TOK:(par + 1) * TOK]
                xm = evac[:, par * TOK:(par + 1) * TOK]
                ev_d = P.dma("sync", xs, src2d[k0 * 128:(k0 + 1) * 128, :],
                             waits=[ev_t.get(k0 - 2)], sem="dX%d" % par)
                e1 = tt_(xs, ALU.subtract, m_b, xm, waits=[ev_d, ev_t.get(k0 - 2)])
                e2 = tt_(xm, ALU.mult, r_b, xm, waits=[e1])
                hi = act_hi[:, k0 * TOK:(k0 + 1) * TOK]
                lo = act_lo[:, k0 * TOK:(k0 + 1) * TOK]
                eh, el = split(xm, hi, lo, waits=[e2])
                first, lastk = (k0 == 0), (k0 == KC - 1)
                gh = g0h[:, k0 * 8:(k0 + 1) * 8]
                gl = g0l[:, k0 * 8:(k0 + 1) * 8]
                ev_mm = None
                for tt in range(TT):
                    ps = pb[6][0:8, :] if tt == 0 else pb[7][0:8, :]
                    hs = hi[:, tt * TW:(tt + 1) * TW]
                    ls = lo[:, tt * TW:(tt + 1) * TW]
                    mm(ps, gh, hs, first, False, waits=[el] if tt == 0 else [])
                    mm(ps, gh, ls, False, False)
                    ev_mm = mm(ps, gl, hs, False, lastk,
                               inc="T" if tt == 1 else None)
                ev_t[k0] = ev_mm
            P.gate()

        ph1 = ExitStack()
        mstage = ph1.enter_context(nc.sbuf_tensor("mstage", [128, QW], I32))
        scb_h = ph1.enter_context(nc.sbuf_tensor("scb_h", [8, D], BF16))
        scb_l = ph1.enter_context(nc.sbuf_tensor("scb_l", [8, D], BF16))
        ph1r = ExitStack()
        lgs = ph1r.enter_context(nc.sbuf_tensor("lgs0", [8, TOK], F32))
        rting = lgs

        m_b = gbuf[:, 0:TOK]
        r_b = gbuf[:, TOK:2 * TOK]
        ln_stats(x0, m_b, r_b)
        ln_apply_and_router(x0, m_b, r_b)

        tt_(g0s[:, :], ALU.mult, g0s[:, :], evac[:, 0:KC * 8])
        P.gate()
        for k0 in range(KC):
            mm(pb[0][0:1, 0:8], ones_f[:, :], evac[:, k0 * 8:(k0 + 1) * 8],
               k0 == 0, k0 == KC - 1, inc="T" if k0 == KC - 1 else None)
        P.gate()
        act(pb[0][0:1, 0:8], AF.Sqrt, stat[0:1, 0:8])
        P.gate()
        P.op("vector", lambda e: e.tensor_reduce(out=stat[0:1, 8:9],
                                                 in_=stat[0:1, 0:8],
                                                 axis=AX.X, op=ALU.max), (), inc="V")
        P.gate()
        tsc(stat[0:1, 8:9], -64.0, ALU.mult, stat[0:1, 9:10])
        P.gate()
        mm(pb[2][0:8, 0:1], onesr_f[0:1, 0:8], stat[0:1, 9:10], True, True, inc="T")
        P.gate()
        scopy(pb[2][0:8, 0:1], bcol[:, 0:1])
        scopy(pb[6][0:8, :], lgs[:, 0:TW])
        scopy(pb[7][0:8, :], lgs[:, TW:TOK])
        P.gate()
        act(lgs[:, :], AF.Exp, lgs[:, :], bias=bcol[:, 0:1])
        P.gate()
        for tt in range(TT):
            mm(pb[0 + tt][0:1, :], ones_f[0:8, 0:1], lgs[:, tt * TW:(tt + 1) * TW],
               True, True, inc="T" if tt else None)
        P.gate()
        for tt in range(TT):
            recip(pb[0 + tt][0:1, :], stat[0:1, tt * TW:(tt + 1) * TW])
        P.gate()
        for tt in range(TT):
            mm(pb[2 + tt][0:8, :], onesr_f[0:1, 0:8],
               stat[0:1, tt * TW:(tt + 1) * TW], True, True,
               inc="T" if tt else None)
        P.gate()
        for tt in range(TT):
            tt_(lgs[:, tt * TW:(tt + 1) * TW], ALU.mult, pb[2 + tt][0:8, :],
                rting[:, tt * TW:(tt + 1) * TW])
        P.gate()
        split(rting[:, :], rt_hi[:, :], rt_lo[:, :])
        P.gate()
        ph1r.close()

        for half in range(2):
            P.dma("sync", wstage[0:8, 0:QW], is0[:, half * 2048:half * 2048 + QW])
            P.dma("sync", wstage[0:8, QW:2 * QW],
                  is0[:, half * 2048 + QW:(half + 1) * 2048])
            P.gate()
            split(wstage[0:8, 0:2 * QW],
                  scb_h[:, half * 2048:(half + 1) * 2048],
                  scb_l[:, half * 2048:(half + 1) * 2048])
            P.gate()

        ev_s0dma = {}

        def ep_s0(ni, evT):
            par = ni % 2
            ev1 = scopy(pb[0 + 2 * par][:, :], evac[:, par * TOK:par * TOK + TW],
                        waits=[evT, ev_s0dma.get(ni - 2)])
            ev2 = scopy(pb[1 + 2 * par][:, :],
                        evac[:, par * TOK + TW:(par + 1) * TOK])
            ev_s0dma[ni] = P.dma("gpsimd", scr_s0[ni * 128:(ni + 1) * 128, :],
                                 evac[:, par * TOK:(par + 1) * TOK], waits=[ev2],
                                 sem="dS0a" if par == 0 else "dS0b")
            return ev2

        stream_proj(w0a, list(range(KC)),
                    lambda ni: (pb[0 + 2 * (ni % 2)], pb[1 + 2 * (ni % 2)]),
                    ep_s0, mask2d=m0a, npass=3, mstage=mstage)
        P.gate()

        ev_xs = {}
        for k0 in range(KC):
            par = k0 % 2
            psA, psB = pb[4 + 2 * par], pb[5 + 2 * par]
            ish = scb_h[:, k0 * 128:(k0 + 1) * 128]
            isl = scb_l[:, k0 * 128:(k0 + 1) * 128]
            evm = None
            for tt, ps in ((0, psA), (1, psB)):
                rh = rt_hi[:, tt * TW:(tt + 1) * TW]
                rl = rt_lo[:, tt * TW:(tt + 1) * TW]
                mm(ps, ish, rh, True, False,
                   waits=[ev_xs.get(k0 - 2)] if tt == 0 else [])
                mm(ps, ish, rl, False, False)
                evm = mm(ps, isl, rh, False, True, inc="T" if tt == 1 else None)
            hi = act_hi[:, k0 * TOK:(k0 + 1) * TOK]
            lo = act_lo[:, k0 * TOK:(k0 + 1) * TOK]
            xm = evac[:, par * TOK:(par + 1) * TOK]
            e1 = tt_(hi, ALU.add, lo, xm, waits=[evm])
            e2a = tt_(xm[:, 0:TW], ALU.mult, psA[:, :], xm[:, 0:TW], waits=[e1])
            e2b = tt_(xm[:, TW:TOK], ALU.mult, psB[:, :], xm[:, TW:TOK],
                      waits=[e2a])
            ev_xs[k0] = e2b
            split(xm, hi, lo, waits=[e2b])
        P.gate()

        for half in range(2):
            P.dma("sync", wstage[0:8, 0:QW], os0[:, half * 2048:half * 2048 + QW])
            P.dma("sync", wstage[0:8, QW:2 * QW],
                  os0[:, half * 2048 + QW:(half + 1) * 2048])
            P.gate()
            split(wstage[0:8, 0:2 * QW],
                  scb_h[:, half * 2048:(half + 1) * 2048],
                  scb_l[:, half * 2048:(half + 1) * 2048])
            P.gate()

        ev_x1dma, ev_osc, ev_ep_t1, ev_gcons, ev_scons, ev_xcons = ({} for _ in range(6))

        def ep_g0(ni, evT):
            par = ni % 2
            osh = scb_h[:, ni * 128:(ni + 1) * 128]
            osl = scb_l[:, ni * 128:(ni + 1) * 128]
            evos = None
            for tt in range(TT):
                ps = pb[4 + 2 * par + tt]
                rh = rt_hi[:, tt * TW:(tt + 1) * TW]
                rl = rt_lo[:, tt * TW:(tt + 1) * TW]
                mm(ps, osh, rh, True, False,
                   waits=[ev_osc.get(ni - 2)] if tt == 0 else [])
                mm(ps, osh, rl, False, False)
                evos = mm(ps, osl, rh, False, True, inc="T" if tt == 1 else None)
            gpsA, gpsB = pb[0 + 2 * par], pb[1 + 2 * par]
            t1 = evac[:, par * TOK:(par + 1) * TOK]
            gg = gbuf[:, par * TOK:(par + 1) * TOK]
            evc1 = scopy(pb[4 + 2 * par][:, :], gg[:, 0:TW],
                         waits=[evos, ev_gcons.get(ni - 2)])
            evc2 = scopy(pb[5 + 2 * par][:, :], gg[:, TW:TOK])
            ev_o1 = tt_(gpsA[:, :], ALU.mult, gg[:, 0:TW], t1[:, 0:TW],
                        waits=[evT, evc1, ev_ep_t1.get(ni - 2)])
            ev_o2 = tt_(gpsB[:, :], ALU.mult, gg[:, TW:TOK], t1[:, TW:TOK],
                        waits=[evc2])
            ev_osc[ni] = ev_o2
            ev_g = act(t1, AF.Silu, gg, bias=b0fs[:, ni:ni + 1],
                       waits=[ev_o2])
            xx = xst[:, par * TOK:(par + 1) * TOK]
            ev_sl = P.dma("sync", xx, scr_s0[ni * 128:(ni + 1) * 128, :],
                          waits=[ev_s0dma[ni], ev_scons.get(ni - 2)],
                          sem="dX%d" % par)
            ev_sg = tt_(xx, ALU.mult, gg, gg, waits=[ev_g, ev_sl])
            ev_scons[ni] = ev_sg
            ev_xl = P.dma("sync", xx, x0[ni * 128:(ni + 1) * 128, :],
                          waits=[ev_sg], sem="dX%d" % par)
            ev_x1 = tt_(xx, ALU.add, gg, gg, waits=[ev_xl])
            ev_xcons[ni] = ev_x1
            ev_ep_t1[ni] = ev_x1
            ev_x1dma[ni] = P.dma("gpsimd", scr_x1[ni * 128:(ni + 1) * 128, :],
                                 gg, waits=[ev_x1], sem="dXW%d" % par)
            ev_gcons[ni] = ev_x1dma[ni]
            return ev_o2

        stream_proj(w0f, list(range(KC)),
                    lambda ni: (pb[0 + 2 * (ni % 2)], pb[1 + 2 * (ni % 2)]),
                    ep_g0, sign=True, npass=2)
        P.gate()
        ph1.close()

        ph2 = ExitStack()
        lgs = ph2.enter_context(nc.sbuf_tensor("lgs1", [8, TOK], F32))
        rting = lgs
        t2s = ph2.enter_context(nc.sbuf_tensor("t2s", [128, 256], F32))

        P.dma("sync", g0s[:, :], g1[:, :], sem="dO")
        P.gate()
        split(g0s[:, :], g0h[:, :], g0l[:, :])
        P.gate()
        m_b2 = gbuf[:, 0:TOK]
        r_b2 = gbuf[:, TOK:2 * TOK]
        ln_stats(scr_x1, m_b2, r_b2)
        ln_apply_and_router(scr_x1, m_b2, r_b2)

        scopy(pb[6][0:8, :], lgs[:, 0:TW])
        scopy(pb[7][0:8, :], lgs[:, TW:TOK])
        P.gate()
        for c in range(8):
            P.op("tensor", lambda e, c=c: e.transpose(
                out=pb[0][:, c * 8:(c + 1) * 8], in_=lgs[:, c * 128:(c + 1) * 128],
                identity=idm[0:8, 0:8]), (), inc="T" if c == 7 else None)
        P.gate()
        scopy(pb[0][:, 0:64], t2s[:, 0:64])
        P.gate()
        P.op("vector", lambda e: e.tensor_reduce(
            out=t2s[:, 64:72], in_=t2s[:, 0:64].rearrange("p (c e) -> p c e", e=8),
            axis=AX.X, op=ALU.max), (), inc="V")
        P.gate()
        tsc(t2s[:, 64:72], -1.0, ALU.mult, t2s[:, 72:80])
        P.gate()
        for c in range(8):
            tsc(t2s[:, c * 8:(c + 1) * 8], t2s[:, 64 + c:65 + c], ALU.is_equal,
                t2s[:, 80 + c * 8:88 + c * 8])
        P.gate()
        for c in range(8):
            P.op("vector", lambda e, c=c: e.scalar_tensor_tensor(
                out=t2s[:, 144 + c * 8:152 + c * 8],
                in0=t2s[:, 80 + c * 8:88 + c * 8], scalar=-1e30, op0=ALU.mult,
                in1=t2s[:, c * 8:(c + 1) * 8], op1=ALU.add), (), inc="V")
        P.gate()
        P.op("vector", lambda e: e.tensor_reduce(
            out=t2s[:, 208:216], in_=t2s[:, 144:208].rearrange("p (c e) -> p c e", e=8),
            axis=AX.X, op=ALU.max), (), inc="V")
        P.gate()
        tt_(t2s[:, 208:216], ALU.subtract, t2s[:, 64:72], t2s[:, 216:224])
        P.gate()
        act(t2s[:, 216:224], AF.Exp, t2s[:, 216:224])
        P.gate()
        tsc(t2s[:, 216:224], 1.0, ALU.add, t2s[:, 216:224])
        P.gate()
        recip(t2s[:, 216:224], t2s[:, 224:232])
        P.gate()
        tsc(t2s[:, 224:232], 2.0, ALU.mult, t2s[:, 224:232])
        P.gate()
        for c in range(8):
            act(t2s[:, c * 8:(c + 1) * 8], AF.Exp, t2s[:, 80 + c * 8:88 + c * 8],
                bias=t2s[:, 72 + c:73 + c])
            tsc(t2s[:, c * 8:(c + 1) * 8], t2s[:, 208 + c:209 + c], ALU.is_ge,
                t2s[:, 144 + c * 8:152 + c * 8])
        P.gate()
        for c in range(8):
            tt_(t2s[:, 80 + c * 8:88 + c * 8], ALU.mult,
                t2s[:, 144 + c * 8:152 + c * 8], t2s[:, 80 + c * 8:88 + c * 8])
        P.gate()
        for c in range(8):
            tsc(t2s[:, 80 + c * 8:88 + c * 8], t2s[:, 224 + c:225 + c], ALU.mult,
                t2s[:, 80 + c * 8:88 + c * 8])
        P.gate()
        for c in range(8):
            ps = pb[2] if c < 4 else pb[3]
            P.op("tensor", lambda e, c=c, ps=ps: e.transpose(
                out=ps[0:8, (c % 4) * 128:(c % 4 + 1) * 128],
                in_=t2s[:, 80 + c * 8:88 + c * 8],
                identity=idm[:, :]), (), inc="T" if c == 7 else None)
        P.gate()
        scopy(pb[2][0:8, :], rting[:, 0:TW])
        scopy(pb[3][0:8, :], rting[:, TW:TOK])
        P.gate()
        split(rting[:, :], rt_hi[:, :], rt_lo[:, :])
        P.gate()
        ph2.close()

        ph3h = ExitStack()
        h1h = ph3h.enter_context(nc.sbuf_tensor("h1h", [64, TOK], BF16))
        h1l = ph3h.enter_context(nc.sbuf_tensor("h1l", [64, TOK], BF16))

        ph2t = ExitStack()
        v64 = ph2t.enter_context(nc.sbuf_tensor("v64", [64, 24], F32))
        v64i = ph2t.enter_context(nc.sbuf_tensor("v64i", [64, 1], I32))
        mi64 = ph2t.enter_context(nc.sbuf_tensor("mi64", [64, 512], I32))
        u_i = mi64[0:1, :]
        t64a = xst[0:64, 1024:1536]
        t64b = xst[0:64, 1536:2048]
        cq = ph2t.enter_context(nc.sbuf_tensor("cq", [64, TOK], F32))
        sq_t = ph2t.enter_context(nc.sbuf_tensor("sq_t", [64, TOK], F32))
        ck = ph2t.enter_context(nc.sbuf_tensor("ck", [64, TOK], F32))
        sk_t = ph2t.enter_context(nc.sbuf_tensor("sk_t", [64, TOK], F32))
        qrot = xst[:, 0:TOK]
        r64 = gbuf[0:64, 0:512]
        u_row = gbuf[0:1, 512:1024]
        qsp_h = ph2t.enter_context(nc.sbuf_tensor("qsp_h", [128, TOK], BF16))
        qsp_l = ph2t.enter_context(nc.sbuf_tensor("qsp_l", [128, TOK], BF16))

        P.serial = True
        memset(v64[:, 0:1], 10000.0)
        iota(v64i[:, :], [[1, 1]], 0, 1, inc="G")
        iota(u_i[:, :], [[1, 512]], 0, 0, inc="G")
        act(v64[:, 0:1], AF.Ln, v64[:, 0:1])
        scopy(v64i[:, :], v64[:, 1:2])
        scopy(u_i[:, :], u_row)
        tt_(v64[:, 1:2], ALU.mult, v64[:, 0:1], v64[:, 2:3])
        tsc(v64[:, 2:3], -1.0 / 64.0, ALU.mult, v64[:, 2:3])
        act(v64[:, 2:3], AF.Exp, v64[:, 3:4])
        fvec = v64[:, 3:4]

        def sincos_block(ang_sbuf, out_sin, out_cos):
            for which, out in (("s", out_sin), ("c", out_cos)):
                if which == "c":
                    tsc(ang_sbuf, PIH, ALU.add, ang_sbuf)
                tsc(ang_sbuf, INV2PI, ALU.mult, r64)
                P.op("vector", lambda e, a=r64, o=mi64[:, :]:
                     e.tensor_copy(out=o, in_=a), (), inc="V")
                P.gate()
                scopy(mi64[:, :], r64)
                P.op("vector", lambda e, a=r64, b=ang_sbuf, o=r64:
                     e.scalar_tensor_tensor(out=o, in0=a, scalar=NEG2PI,
                                            op0=ALU.mult, in1=b, op1=ALU.add),
                     (), inc="V")
                P.gate()
                tsc(r64, PI_IN, ALU.min, r64)
                tsc(r64, -PI_IN, ALU.max, r64)
                act(r64, AF.Sin, out)

        for blk in range(2):
            sl = slice(blk * 512, blk * 512 + 512)
            tt_(qbc[0:64, blk:blk + 1], ALU.mult, fvec, v64[:, 8:9])
            mm(pb[4][0:64, :], onesr_f[0:1, 0:64], u_row, True, True, inc="T")
            tsc(pb[4][0:64, :], fvec, ALU.mult, t64a, s2=v64[:, 8:9], op1=ALU.add)
            sincos_block(t64a, sk_t[:, sl], ck[:, sl])
            tsc(ck[:, sl], RSCALE, ALU.mult, cq[:, sl])
            tsc(sk_t[:, sl], RSCALE, ALU.mult, sq_t[:, sl])
        P.serial = False
        P.gate()

        def rope_apply(psA, psB, ctab, stab, out_f32, waits):
            last = None
            for tt, ps in ((0, psA), (1, psB)):
                c = ctab[:, tt * TW:(tt + 1) * TW]
                s = stab[:, tt * TW:(tt + 1) * TW]
                oE = out_f32[0:64, tt * TW:(tt + 1) * TW]
                oO = out_f32[64:128, tt * TW:(tt + 1) * TW]
                w = waits if tt == 0 else [last]
                e1 = tt_(ps[0:64, :], ALU.mult, c, t64a[:, :], waits=w)
                e2 = tt_(ps[64:128, :], ALU.mult, s, t64b[:, :], waits=[e1])
                e3 = tt_(t64a[:, :], ALU.subtract, t64b[:, :], oE, waits=[e2])
                e4 = tt_(ps[0:64, :], ALU.mult, s, t64a[:, :], waits=[e3])
                e5 = tt_(ps[64:128, :], ALU.mult, c, t64b[:, :], waits=[e4])
                last = tt_(t64a[:, :], ALU.add, t64b[:, :], oO, waits=[e5])
            return last

        ev_ccdma = {}
        ev_kshi = {}

        def ep_k(ni, evT):
            ev_r = rope_apply(pb[0 + 2 * (ni % 2)], pb[1 + 2 * (ni % 2)],
                              ck, sk_t, qrot,
                              waits=[evT, ev_ccdma.get(ni - 1)])
            ev_ccdma[ni] = P.dma("gpsimd", cc_in[ni * 128:(ni + 1) * 128, :],
                                 qrot[:, :], waits=[ev_r], sem="dCC")
            return ev_r

        stream_proj(w1k, list(range(NKV)),
                    lambda ni: (pb[0 + 2 * (ni % 2)], pb[1 + 2 * (ni % 2)]),
                    ep_k)
        P.gate()

        ev_vt, ev_vsp = {}, {}
        for dvh in range(2):
            for k0 in range(KC):
                j = dvh * KC + k0
                par = j % 2
                wsl = slice(par * QW, par * QW + TW)
                ev_d = P.dma("sync", wstage[:, wsl],
                             w1v[k0 * 128:(k0 + 1) * 128, dvh * TW:(dvh + 1) * TW],
                             waits=[ev_vsp.get(j - 2)], sem="dW%d" % par)
                eh = scopy(wstage[:, wsl], whl_hi[:, wsl],
                           waits=[ev_d, ev_vt.get(j - 2)])
                el = tt_(wstage[:, wsl], ALU.subtract, whl_hi[:, wsl],
                         whl_lo[:, wsl], waits=[eh])
                ev_vsp[j] = el
                ev = None
                for tc in range(8):
                    ps = pb[tc]
                    lhH = act_hi[:, k0 * TOK + tc * 128:k0 * TOK + (tc + 1) * 128]
                    lhL = act_lo[:, k0 * TOK + tc * 128:k0 * TOK + (tc + 1) * 128]
                    first, lastk = (k0 == 0), (k0 == KC - 1)
                    mm(ps, lhH, whl_hi[:, wsl], first, False,
                       waits=[el] if tc == 0 else [])
                    mm(ps, lhH, whl_lo[:, wsl], False, False)
                    ev = mm(ps, lhL, whl_hi[:, wsl], False, lastk,
                            inc="T" if tc == 7 else None)
                ev_vt[j] = ev
            P.gate()
            for tc in range(8):
                buf = (evac if tc < 4 else gbuf)[:, (tc % 4) * TW:(tc % 4 + 1) * TW]
                evc = scopy(pb[tc][:, :], buf)
                P.dma("gpsimd", cc_in[1024 + tc * 128:1024 + (tc + 1) * 128,
                                      dvh * TW:(dvh + 1) * TW], buf, waits=[evc],
                      sem="dCC")
            P.gate()

        ev_cc = P.op("gpsimd", lambda e: e.collective_compute(
            "AllGather", ALU.bypass,
            replica_groups=[[0, 1], [2, 3], [4, 5], [6, 7]],
            ins=[cc_in[:, :]], outs=[cc_out[:, :]]), (), inc="C")

        ev_qdma = {}

        def ep_q(ni, evT):
            ev_r = rope_apply(pb[0 + 2 * (ni % 2)], pb[1 + 2 * (ni % 2)],
                              cq, sq_t, qrot,
                              waits=[evT, ev_qdma.get(ni - 1)])
            eh, el = split(qrot[:, :], qsp_h[:, :], qsp_l[:, :], waits=[ev_r])
            d1 = P.dma("gpsimd", scr_qh[ni * 128:(ni + 1) * 128, :], qsp_h[:, :],
                       waits=[el], sem="dQW")
            d2 = P.dma("gpsimd", scr_ql[ni * 128:(ni + 1) * 128, :], qsp_l[:, :],
                       waits=[el], sem="dQW")
            ev_qdma[ni] = d2
            return ev_r

        stream_proj(w1q, list(range(NQH)),
                    lambda ni: (pb[0 + 2 * (ni % 2)], pb[1 + 2 * (ni % 2)]),
                    ep_q)
        P.gate()

        P.dma("sync", wstage[:, 0:QW], a1[:, 0:QW], sem="dO")
        P.dma("sync", wstage[:, QW:2 * QW], a1[:, QW:2 * QW], sem="dO")
        P.gate()
        split(wstage[:, :], whl_hi[:, :], whl_lo[:, :])
        P.gate()
        for k0 in range(KC):
            lh = whl_hi[:, k0 * 64:(k0 + 1) * 64]
            ll = whl_lo[:, k0 * 64:(k0 + 1) * 64]
            for tt in range(TT):
                ps = pb[4 + tt][0:64, :]
                rh, rl = rhs_act(k0, tt)
                first, lastk = (k0 == 0), (k0 == KC - 1)
                mm(ps, lh, rh, first, False)
                mm(ps, lh, rl, False, False)
                mm(ps, ll, rh, False, lastk,
                   inc="T" if (lastk and tt == 1) else None)
        P.gate()
        scopy(pb[4][0:64, :], qrot[0:64, 0:TW])
        scopy(pb[5][0:64, :], qrot[0:64, TW:TOK])
        for tt in range(TT):
            ps = pb[6 + tt][0:64, :]
            mm(ps, repm[:, :], rt_hi[:, tt * TW:(tt + 1) * TW], True, False)
            mm(ps, repm[:, :], rt_lo[:, tt * TW:(tt + 1) * TW], False, True,
               inc="T" if tt else None)
        P.gate()
        for tt in range(TT):
            tt_(qrot[0:64, tt * TW:(tt + 1) * TW], ALU.mult, pb[6 + tt][0:64, :],
                qrot[0:64, tt * TW:(tt + 1) * TW])
        P.gate()
        split(qrot[0:64, 0:TOK], h1h[:, :], h1l[:, :])
        P.gate()
        ph2t.close()

        ph2b = ExitStack()
        b1st = ph2b.enter_context(nc.sbuf_tensor("b1st", [64, 2 * 128], F32))
        b1h = ph2b.enter_context(nc.sbuf_tensor("b1h", [64, 2 * 128], BF16))
        b1lo = ph2b.enter_context(nc.sbuf_tensor("b1lo", [64, 2 * 128], BF16))

        ev_g1dma = {}
        ev_b1mm = {}

        def lora_B(ni, psA, psB):
            par = ni % 2
            bsl = slice(par * 128, (par + 1) * 128)
            ev_d = P.dma("sync", b1st[:, bsl],
                         b1d[:, ni * 128:(ni + 1) * 128],
                         waits=[ev_b1sp.get(ni - 2)], sem="dB%d" % par)
            ehb, elb = split(b1st[:, bsl], b1h[:, bsl], b1lo[:, bsl],
                             waits=[ev_d, ev_b1mm.get(ni - 2)])
            ev_b1sp[ni] = elb
            ev = None
            for tt, ps in ((0, psA), (1, psB)):
                mm(ps, b1h[:, bsl], h1h[:, tt * TW:(tt + 1) * TW], False, False,
                   waits=[elb] if tt == 0 else [])
                mm(ps, b1h[:, bsl], h1l[:, tt * TW:(tt + 1) * TW], False, False)
                ev = mm(ps, b1lo[:, bsl], h1h[:, tt * TW:(tt + 1) * TW], False,
                        True, inc="T" if tt == 1 else None)
            ev_b1mm[ni] = ev
            return ev

        ev_b1sp = {}

        def ep_g1(ni, evT):
            par = ni % 2
            gg = gbuf[:, par * TOK:(par + 1) * TOK]
            psA, psB = pb[0 + 2 * par], pb[1 + 2 * par]
            ev1 = act(psA[:, :], AF.Silu, gg[:, 0:TW],
                      waits=[evT, ev_g1dma.get(ni - 2)])
            ev2 = act(psB[:, :], AF.Silu, gg[:, TW:TOK])
            ev_g1dma[ni] = P.dma("gpsimd", scr_g1[ni * 128:(ni + 1) * 128, :],
                                 gg, waits=[ev2], sem="dGW%d" % par)
            return ev2

        stream_proj(w1b, list(range(KC)),
                    lambda ni: (pb[0 + 2 * (ni % 2)], pb[1 + 2 * (ni % 2)]),
                    ep_g1, extra_group_mm=lora_B)
        P.gate()
        ph2b.close()
        ph3h.close()

        ph3 = ExitStack()
        ksb = wstage[:, 0:2048]
        vsb = wstage[:, 0:2048]
        khi = whl_hi[:, 0:2048]
        klo = whl_lo[:, 0:2048]
        vhi = ph3.enter_context(nc.sbuf_tensor("vhi", [128, 2048], BF16))
        vlo = ph3.enter_context(nc.sbuf_tensor("vlo", [128, 2048], BF16))
        qh2 = ph3.enter_context(nc.sbuf_tensor("qh2", [128, TOK], BF16))
        ql2 = ph3.enter_context(nc.sbuf_tensor("ql2", [128, TOK], BF16))
        et = ph3.enter_context(nc.sbuf_tensor("et", [128, 2 * TW], F32))
        ehi = ph3.enter_context(nc.sbuf_tensor("ehi", [128, 2 * TW], BF16))
        elo = ph3.enter_context(nc.sbuf_tensor("elo", [128, 2 * TW], BF16))
        negb = ph3.enter_context(nc.sbuf_tensor("negb", [1, TOK], BF16))
        viota = ph3.enter_context(nc.sbuf_tensor("viota", [128, TW], I32))

        iota(viota[:, :], [[1, TW]], 0, 0, inc="G")
        P.gate()

        RHO = [0, 1, 1, 0]
        LAM = [0, 0, 1, 1]

        for hv in range(NKV):
            kds = []
            for b in range(4):
                r0 = RHO[b] * 2048 + hv * 128
                c0 = LAM[b] * TW
                kds.append(P.dma("sync", ksb[:, b * TW:(b + 1) * TW],
                                 cc_out[r0:r0 + 128, c0:c0 + TW],
                                 waits=[("C", 1)], sem="dK"))
            ehK, elK = split(ksb[:, :], khi[:, :], klo[:, :], waits=kds)
            e1 = tt_(khi[:, :], ALU.add, klo[:, :], vsb[:, :], waits=[elK])
            e2 = act(vsb[:, :], AF.Square, vsb[:, :], waits=[e1])
            evn = {}
            for b in range(4):
                ps = pb[5 + (b % 2)]
                ev = mm(ps[0:1, :], ones_f[:, :], vsb[:, b * TW:(b + 1) * TW],
                        True, True,
                        waits=([e2] if b < 2 else []) + [evn.get(b - 2)],
                        inc="T")
                evn[b] = scopy(ps[0:1, :], gbuf[0:1, b * TW:(b + 1) * TW],
                               waits=[ev])
            P.gate()
            P.op("vector", lambda e: e.tensor_reduce(
                out=stat[0:1, 1024:1025], in_=gbuf[0:1, 0:2048], axis=AX.X,
                op=ALU.max), (), inc="V")
            P.gate()
            act(stat[0:1, 1024:1025], AF.Sqrt, stat[0:1, 1025:1026])
            P.gate()
            tsc(stat[0:1, 1025:1026], -1.0, ALU.mult, stat[0:1, 1026:1027])
            P.gate()
            vds = []
            for g in range(16):
                b = g // 4
                lc = LAM[b] * 4 + (g % 4)
                r0 = RHO[b] * 2048 + 1024 + lc * 128
                vds.append(P.dma("sync", vsb[:, g * 128:(g + 1) * 128],
                                 cc_out[r0:r0 + 128, hv * 128:(hv + 1) * 128],
                                 sem="dV"))
            ehV, elV = split(vsb[:, :], vhi[:, :], vlo[:, :], waits=vds)
            P.gate()

            ev_qn = {}
            for hq in range(hv * 4, hv * 4 + 4):
                par = hq % 2
                qh = qh2[:, :]
                ql = ql2[:, :]
                d1 = P.dma("sync", qh, scr_qh[hq * 128:(hq + 1) * 128, :],
                           sem="dQ")
                d2 = P.dma("sync", ql, scr_ql[hq * 128:(hq + 1) * 128, :],
                           sem="dQ")
                xq = xst[:, par * TOK:(par + 1) * TOK]
                e1 = tt_(qh, ALU.add, ql, xq, waits=[d1, d2])
                e2 = act(xq, AF.Square, xq, waits=[e1])
                evq = None
                for tt in range(TT):
                    evq = mm(pb[5 + tt][0:1, :], ones_f[:, :],
                             xq[:, tt * TW:(tt + 1) * TW], True, True,
                             waits=[e2, ev_qn.get(tt)], inc="T")
                    evq = act(pb[5 + tt][0:1, :], AF.Sqrt,
                              stat[0:1, tt * TW:(tt + 1) * TW], waits=[evq])
                    ev_qn[tt] = evq
                evb = tsc(stat[0:1, 0:TOK], stat[0:1, 1026:1027], ALU.mult,
                          negb[0:1, :], waits=[evq])
                for qi in range(2):
                    ev_exp, ev_av = {}, {}
                    kcnt = KCNT[qi]
                    qsl = slice(qi * TW, (qi + 1) * TW)

                    def emit_S(kc):
                        p2 = kc % 2
                        psS = pb[p2]
                        w0 = [elK, evb, d2, ev_exp.get(kc - 2)]
                        mm(psS, khi[:, kc * 128:(kc + 1) * 128], qh[:, qsl],
                           True, False, waits=w0)
                        mm(psS, khi[:, kc * 128:(kc + 1) * 128], ql[:, qsl],
                           False, False)
                        mm(psS, klo[:, kc * 128:(kc + 1) * 128], qh[:, qsl],
                           False, False)
                        return mm(psS, onesr_b[:, :], negb[0:1, qsl], False,
                                  True, inc="T")

                    def emit_EV(kc, evS):
                        p2 = kc % 2
                        psS = pb[p2]
                        esl = slice(p2 * TW, (p2 + 1) * TW)
                        evE = act(psS[:, :], AF.Exp, et[:, esl],
                                  waits=[evS, ev_av.get(kc - 2)])
                        evEM = P.op("vector", lambda e, esl=esl, col=qi * 8 + kc:
                                    e.scalar_tensor_tensor(
                                        out=et[:, esl], in0=viota[:, :],
                                        scalar=thr[:, col:col + 1], op0=ALU.is_ge,
                                        in1=et[:, esl], op1=ALU.mult),
                                    [evE, ev_av.get(kc - 2)], inc="V")
                        evH = scopy(et[:, esl], ehi[:, esl], waits=[evEM])
                        evL = tt_(et[:, esl], ALU.subtract, ehi[:, esl],
                                  elo[:, esl], waits=[evH])
                        ev_exp[kc] = evE
                        return evL

                    def emit_AV(kc, evL):
                        p2 = kc % 2
                        esl = slice(p2 * TW, (p2 + 1) * TW)
                        first, lastk = (kc == 0), (kc == kcnt - 1)
                        mm(pb[3][0:1, :], ones_b[:, :], ehi[:, esl], first,
                           False, waits=[evL])
                        mm(pb[3][0:1, :], ones_b[:, :], elo[:, esl], False,
                           lastk)
                        mm(pb[2], vhi[:, kc * 128:(kc + 1) * 128], ehi[:, esl],
                           first, False, waits=[elV] if first else [])
                        mm(pb[2], vhi[:, kc * 128:(kc + 1) * 128], elo[:, esl],
                           False, False)
                        ev_av[kc] = mm(pb[2], vlo[:, kc * 128:(kc + 1) * 128],
                                       ehi[:, esl], False, lastk, inc="T")

                    pend = None
                    for kc in range(kcnt):
                        evS = emit_S(kc)
                        if pend is not None:
                            emit_AV(*pend)
                        evL = emit_EV(kc, evS)
                        pend = (kc, evL)
                    emit_AV(*pend)
                    evR = recip(pb[3][0:1, :], gbuf[0:1, TW:2 * TW],
                                waits=[ev_av[kcnt - 1]])
                    evB = mm(pb[4], onesr_f[:, :], gbuf[0:1, TW:2 * TW],
                             True, True, waits=[evR], inc="T")
                    o1 = scopy(pb[2][:, :], gbuf[:, 0:TW], waits=[evB])
                    onrm = tt_(gbuf[:, 0:TW], ALU.mult, pb[4][:, :],
                               evac[:, 0:TW], waits=[o1])
                    dsts = act_hi[:, hq * TOK + qi * TW:hq * TOK + (qi + 1) * TW]
                    dstl = act_lo[:, hq * TOK + qi * TW:hq * TOK + (qi + 1) * TW]
                    eh2, el2 = split(evac[:, 0:TW], dsts, dstl, waits=[onrm])
                    P.gate()
        P.gate()
        ph3.close()

        ev_ocons, ev_gcons2, ev_xcons2 = {}, {}, {}

        def ep_out(ni, evT):
            par = ni % 2
            gg = gbuf[:, par * TOK:(par + 1) * TOK]
            xx = xst[:, par * TOK:(par + 1) * TOK]
            oo = evac[:, par * TOK:(par + 1) * TOK]
            ev_gl = P.dma("sync", gg, scr_g1[ni * 128:(ni + 1) * 128, :],
                          waits=[ev_gcons2.get(ni - 2)], sem="dB%d" % par)
            ev_xl = P.dma("sync", xx, scr_x1[ni * 128:(ni + 1) * 128, :],
                          waits=[ev_xcons2.get(ni - 2)], sem="dX%d" % par)
            e1 = tt_(pb[0 + 2 * par][:, :], ALU.mult, gg[:, 0:TW], oo[:, 0:TW],
                     waits=[evT, ev_gl, ev_ocons.get(ni - 2)])
            e2 = tt_(pb[1 + 2 * par][:, :], ALU.mult, gg[:, TW:TOK], oo[:, TW:TOK],
                     waits=[e1])
            ev_gcons2[ni] = e2
            e3 = tt_(xx, ALU.add, oo, oo, waits=[e2, ev_xl])
            ev_xcons2[ni] = e3
            ev_ocons[ni] = P.dma("gpsimd", out0[ni * 128:(ni + 1) * 128, :], oo,
                                 waits=[e3], sem="dOut" if par == 0 else "dOut1")
            return e2

        stream_proj(w1o, list(range(KC)),
                    lambda ni: (pb[0 + 2 * (ni % 2)], pb[1 + 2 * (ni % 2)]),
                    ep_out)
        P.final_wait("gpsimd", ("dOut", P.cnt["dOut"]))
        P.final_wait("gpsimd", ("dOut1", P.cnt["dOut1"]))

        with nc.Block() as block:
            P.emit(block, sems)

    return nc


_PERMQ = np.concatenate([h * 128 + np.concatenate([np.arange(0, 128, 2),
                                                   np.arange(1, 128, 2)])
                         for h in range(NQH)])
_PERMK = np.concatenate([h * 128 + np.concatenate([np.arange(0, 128, 2),
                                                   np.arange(1, 128, 2)])
                         for h in range(NKV)])


def _t4(w):
    return np.ascontiguousarray(
        w.reshape(KC, 128, KC, 128).transpose(2, 1, 0, 3).reshape(D, D))


def _t4k(w):
    return np.ascontiguousarray(
        w.reshape(KC, 128, NKV, 128).transpose(2, 1, 0, 3).reshape(NKV * 128, D))


_NC_CACHE = {}


def _get_nc(debug=False):
    if debug not in _NC_CACHE:
        _NC_CACHE[debug] = build_nc(debug)
    return _NC_CACHE[debug]


def make_in_maps(L0_attn_data, L0_attn_weight, L0_attn_mask, L0_ffn_weight,
                 L0_ffn_bias, L0_ffn_gate_w, L0_ffn_in_channel_scale,
                 L0_ffn_out_channel_scale, L1_attn_wq, L1_attn_wk, L1_attn_wv,
                 L1_attn_wo, L1_ffn_W_base, L1_ffn_gate_W, L1_ffn_A, L1_ffn_B):
    f = np.ascontiguousarray
    w0a = _t4(np.asarray(L0_attn_weight).T)
    m0a = _t4(np.asarray(L0_attn_mask).T.astype(np.int32))
    w0f = _t4(np.asarray(L0_ffn_weight).T)
    b0fv = f(np.asarray(L0_ffn_bias).reshape(KC, 128).T)
    g0v = f(np.asarray(L0_ffn_gate_w).T.reshape(KC, 128, 8)
            .transpose(1, 0, 2).reshape(128, KC * 8))
    is0v = f(np.asarray(L0_ffn_in_channel_scale))
    os0v = f(np.asarray(L0_ffn_out_channel_scale))
    w1qv = _t4(np.asarray(L1_attn_wq)[:, _PERMQ])
    w1kv = _t4k(np.asarray(L1_attn_wk)[:, _PERMK])
    w1vv = f(np.asarray(L1_attn_wv))
    w1ov = _t4(np.asarray(L1_attn_wo))
    w1bv = _t4(np.asarray(L1_ffn_W_base).T)
    g1v = f(np.asarray(L1_ffn_gate_W).T.reshape(KC, 128, 8)
            .transpose(1, 0, 2).reshape(128, KC * 8))
    a1v = f(np.asarray(L1_ffn_A).reshape(64, D).T.reshape(KC, 128, 64)
            .transpose(1, 0, 2).reshape(128, KC * 64))
    b1v = f(np.asarray(L1_ffn_B).transpose(0, 2, 1).reshape(64, D))
    x = np.asarray(L0_attn_data)

    in_maps = []
    for c in range(8):
        seq, role = c // 2, c % 2
        blocks = (0, 3) if role == 0 else (1, 2)
        xc = np.concatenate([x[seq, 512 * b:512 * b + 512, :] for b in blocks], 0)
        x0v = f(xc.T)
        qbv = np.array([[512.0 * blocks[0], 512.0 * blocks[1]]], np.float32)
        qs = np.zeros((64, 8), np.float32)
        qs[:, blocks[0]] = 1.0
        qs[:, 4 + blocks[1]] = 1.0
        in_maps.append({
            "x0": x0v, "w0a": w0a, "m0a": m0a, "w0f": w0f, "b0f": b0fv,
            "g0": g0v, "is0": is0v, "os0": os0v, "w1q": w1qv, "w1k": w1kv,
            "w1v": w1vv, "w1o": w1ov, "w1b": w1bv, "g1": g1v, "a1": a1v,
            "b1d": b1v, "qb": qbv, "qsel": qs,
        })
    return in_maps


def kernel(**inputs):
    in_maps = make_in_maps(**inputs)
    nc = _get_nc(False)
    res = run_bass_kernel_spmd(nc, in_maps, core_ids=list(range(8)))
    out = np.empty((4, 2048, D), np.float32)
    for c in range(8):
        seq, role = c // 2, c % 2
        blocks = (0, 3) if role == 0 else (1, 2)
        o = np.asarray(res.results[c]["out0"])
        for i, b in enumerate(blocks):
            out[seq, 512 * b:512 * b + 512, :] = o[:, 512 * i:512 * i + 512].T
    return out


# version 04
# speedup vs baseline: 1.2967x; 1.0489x over previous
import numpy as np
import concourse.bass as bass
import concourse.mybir as mybir
from concourse.bass_utils import run_bass_kernel_spmd
from contextlib import ExitStack

F32 = mybir.dt.float32
BF16 = mybir.dt.bfloat16
I32 = mybir.dt.int32
AF = mybir.ActivationFunctionType
ALU = mybir.AluOpType
AX = mybir.AxisListType

D = 4096
KC = 32
TOK = 1024
TT = 2
TW = 512
NQH = 32
NKV = 8
HD = 128
RSCALE = float(1.0 / np.float64(np.float32(np.sqrt(np.float64(HD)))))
EPS = 1e-5
QT = 4
QW = 1024
KCQ = QW // 128
KCNT = (8, 16)

ENGINES = ("sync", "scalar", "vector", "tensor", "gpsimd")

PI_IN = 3.1415925
INV2PI = float(1.0 / (2.0 * np.pi))
NEG2PI = float(-2.0 * np.pi)
PIH = float(np.pi / 2.0)
CONSTS_F32 = [-1.0, 2.0, 1.0 / 4096.0, EPS, RSCALE, -RSCALE, 0.5, 7.0,
              64.0, 10000.0, 1.0, 0.0, -1.0 / 64.0, -1e30,
              INV2PI, NEG2PI, PI_IN, -PI_IN, PIH, -64.0]
CONSTS_BF16 = [1.0, -1.0, 2.0, 0.0]


class Prog:
    def __init__(self):
        self.ops = {e: [] for e in ENGINES}
        self.cnt = {}
        self.gate_pending = {e: None for e in ENGINES}
        self.serial = False
        for s in ("dS", "dG", "T", "V", "S", "G", "C"):
            self.cnt[s] = 0

    def op(self, eng, fn, waits=(), inc=None, amt=1):
        w = [x for x in waits if x is not None]
        g = self.gate_pending[eng]
        if g is not None:
            w = list(g.items()) + w
            self.gate_pending[eng] = None
        ev = None
        if inc is not None:
            self.cnt[inc] += amt
            ev = (inc, self.cnt[inc])
        self.ops[eng].append((tuple(w), fn, (inc, amt) if inc else None))
        if self.serial and inc is not None:
            self.gate()
        return ev

    def dma(self, eng, out, in_, waits=(), sem=None):
        if sem is None:
            sem = "dS" if eng == "sync" else "dG"
        self.cnt.setdefault(sem, 0)
        return self.op(eng, lambda e, o=out, i=in_: e.dma_start(out=o, in_=i),
                       waits, inc=sem, amt=16)

    def gate(self):
        snap = {k: v for k, v in self.cnt.items() if v > 0}
        for e in ENGINES:
            self.gate_pending[e] = snap

    def final_wait(self, eng, ev):
        self.ops[eng].append(((ev,), None, None))

    def emit(self, block, sems):
        for eng_name in ENGINES:
            ops = self.ops[eng_name]
            if not ops:
                continue

            def make(ops):
                def body(eng):
                    waited = {}
                    for waits, fn, inc in ops:
                        merged = {}
                        for s, v in waits:
                            if v > merged.get(s, 0):
                                merged[s] = v
                        for s, v in merged.items():
                            if v > 0 and waited.get(s, 0) < v:
                                eng.wait_ge(sems[s], v)
                                waited[s] = v
                        if fn is None:
                            continue
                        instr = fn(eng)
                        if inc is not None:
                            instr.then_inc(sems[inc[0]], inc[1])
                return body

            getattr(block, eng_name)(make(ops))


def build_nc(debug=False):
    nc = bass.Bass("TRN2", target_bir_lowering=False)

    def din(name, shape, dtype=F32):
        return nc.dram_tensor(name, shape, dtype, kind="ExternalInput").ap()

    def dscr(name, shape, dtype=F32):
        dbg = debug if isinstance(debug, (tuple, list, set)) else ()
        kind = "ExternalOutput" if name in dbg else "Internal"
        return nc.dram_tensor(name, shape, dtype, kind=kind).ap()

    x0 = din("x0", [D, TOK])
    w0a = din("w0a", [D, D])
    m0a = din("m0a", [D, D], I32)
    w0f = din("w0f", [D, D])
    b0f = din("b0f", [128, KC])
    g0 = din("g0", [128, KC * 8])
    is0 = din("is0", [8, D])
    os0 = din("os0", [8, D])
    w1q = din("w1q", [D, D])
    w1k = din("w1k", [NKV * 128, D])
    w1v = din("w1v", [D, 1024])
    w1o = din("w1o", [D, D])
    w1b = din("w1b", [D, D])
    g1 = din("g1", [128, KC * 8])
    a1 = din("a1", [128, KC * 64])
    b1d = din("b1d", [64, D])
    qbin = din("qb", [1, 2])
    qselin = din("qsel", [64, 8])

    out0 = nc.dram_tensor("out0", [D, TOK], F32, kind="ExternalOutput").ap()
    scr_s0 = dscr("scr_s0", [D, TOK])
    scr_x1 = dscr("scr_x1", [D, TOK])
    scr_g1 = dscr("scr_g1", [D, TOK])
    scr_qh = dscr("scr_qh", [NQH * 128, TOK], BF16)
    scr_ql = dscr("scr_ql", [NQH * 128, TOK], BF16)
    cc_in = nc.dram_tensor("cc_in", [2048, 1024], F32).ap()
    dbg = debug if isinstance(debug, (tuple, list, set)) else ()
    dbg_rope = (nc.dram_tensor("dbg_rope", [64, 5248], F32,
                               kind="ExternalOutput").ap()
                if "dbg_rope" in dbg else None)
    cc_out = dscr("cc_out", [4096, 1024])

    DSEMS = ("dS", "dG", "dW0", "dW1", "dM", "dX0", "dX1", "dB0", "dB1",
             "dK", "dV", "dQ", "dO", "dS0a", "dS0b", "dXW0", "dXW1", "dQW",
             "dGW0", "dGW1", "dCC", "dOut", "dOut1")
    P = Prog()
    for _s in DSEMS:
        P.cnt.setdefault(_s, 0)
    st = ExitStack()
    with st:
        e = st.enter_context

        act_hi = e(nc.sbuf_tensor("act_hi", [128, KC * TOK], BF16))
        act_lo = e(nc.sbuf_tensor("act_lo", [128, KC * TOK], BF16))
        wstage = e(nc.sbuf_tensor("wstage", [128, 2 * QW], F32))
        whl_hi = e(nc.sbuf_tensor("whl_hi", [128, 2 * QW], BF16))
        whl_lo = e(nc.sbuf_tensor("whl_lo", [128, 2 * QW], BF16))
        xst = e(nc.sbuf_tensor("xst", [128, 2 * TOK], F32))
        evac = e(nc.sbuf_tensor("evac", [128, 2 * TOK], F32))
        gbuf = e(nc.sbuf_tensor("gbuf", [128, 2 * TOK], F32))
        ones_f = e(nc.sbuf_tensor("ones_f", [128, 1], F32))
        ones_b = e(nc.sbuf_tensor("ones_b", [128, 1], BF16))
        onesr_f = e(nc.sbuf_tensor("onesr_f", [1, 128], F32))
        onesr_b = e(nc.sbuf_tensor("onesr_b", [1, 128], BF16))
        idm = e(nc.sbuf_tensor("idm", [128, 128], F32))
        idm_i = e(nc.sbuf_tensor("idm_i", [128, 128], I32))
        rt_hi = e(nc.sbuf_tensor("rt_hi", [8, TOK], BF16))
        rt_lo = e(nc.sbuf_tensor("rt_lo", [8, TOK], BF16))
        bcol = e(nc.sbuf_tensor("bcol", [8, 2], F32))
        g0s = e(nc.sbuf_tensor("g0s", [128, KC * 8], F32))
        g0h = e(nc.sbuf_tensor("g0h", [128, KC * 8], BF16))
        g0l = e(nc.sbuf_tensor("g0l", [128, KC * 8], BF16))
        stat = e(nc.sbuf_tensor("stat", [1, 1040], F32))
        b0fs = e(nc.sbuf_tensor("b0fs", [128, KC], F32))
        qbs = e(nc.sbuf_tensor("qbs", [1, 2], F32))
        qbc = e(nc.sbuf_tensor("qbc", [128, 2], F32))
        qsels = e(nc.sbuf_tensor("qsels", [64, 8], F32))
        thr = e(nc.sbuf_tensor("thr", [128, 24], F32))
        thr_i = e(nc.sbuf_tensor("thr_i", [128, 24], I32))
        repm = e(nc.sbuf_tensor("repm", [8, 64], BF16))
        rep8 = e(nc.sbuf_tensor("rep8", [8, 64], I32))
        csts = e(nc.sbuf_tensor("csts", [128, len(CONSTS_F32)], F32))
        cstb = e(nc.sbuf_tensor("cstb", [128, len(CONSTS_BF16)], BF16))

        for i, v in enumerate(CONSTS_F32):
            nc.const_aps.register(F32, v, csts[:, i:i + 1])
        for i, v in enumerate(CONSTS_BF16):
            nc.const_aps.register(BF16, v, cstb[:, i:i + 1])

        pb = [e(nc.psum_tensor(f"pb{i}", [128, TW], F32)) for i in range(8)]

        sems = {}
        for s in DSEMS + ("T", "V", "S", "G", "C"):
            sems[s] = e(nc.semaphore(s))

        def _ap(x):
            return x[:, :] if x.__class__.__name__.endswith("TensorHandle") else x

        def tt_(a, op, b, o, waits=(), eng="vector", inc="V"):
            return P.op(eng, lambda e, a=a, b=b, o=o, op=op:
                        e.tensor_tensor(out=o, in0=a, in1=b, op=op), waits, inc=inc)

        def tsc(in0, s1, op0, o, s2=None, op1=None, waits=(), eng="vector", inc="V"):
            def fn(e, i=in0, s1=s1, s2=s2, o=o, op0=op0, op1=op1):
                kw = {"op1": op1} if op1 is not None else {}
                return e.tensor_scalar(out=o, in0=i, scalar1=s1, scalar2=s2,
                                       op0=op0, **kw)
            return P.op(eng, fn, waits, inc=inc)

        def act(in_, func, o, scale=1.0, bias=0.0, waits=(), inc="S"):
            return P.op("scalar", lambda e, i=in_, o=o, f=func, sc=scale, b=bias:
                        e.activation(out=o, in_=i, func=f, scale=sc, bias=b),
                        waits, inc=inc)

        def scopy(in_, o, scale=None, waits=(), inc="S"):
            if scale is None:
                return P.op("scalar", lambda e, i=in_, o=o: e.copy(out=o, in_=i),
                            waits, inc=inc)
            return P.op("scalar", lambda e, i=in_, o=o, s=scale:
                        e.mul(out=o, in_=i, mul=s), waits, inc=inc)

        def mm(ps, lhsT, rhs, start, stop, waits=(), inc=None):
            return P.op("tensor",
                        lambda e, p=_ap(ps), l=_ap(lhsT), r=_ap(rhs), a=start, b=stop:
                        e.matmul(out=p, lhsT=l, rhs=r, start=a, stop=b),
                        waits, inc=inc)

        def recip(in_, o, waits=()):
            return P.op("vector", lambda e, i=in_, o=o:
                        e.reciprocal(out=o, in_=i), waits, inc="V")

        def iota(o, pattern, base, chm, waits=(), inc="G"):
            return P.op("gpsimd", lambda e, o=o, p=pattern, b=base, c=chm:
                        e.iota(out=o, pattern=p, base=b, channel_multiplier=c),
                        waits, inc=inc)

        def memset(ap, v, waits=(), inc="G"):
            return P.op("gpsimd", lambda e, a=ap, v=v: e.memset(a, v), waits, inc=inc)

        def split(src, hi, lo, waits=()):
            eh = scopy(src, hi, waits=waits)
            el = tt_(src, ALU.subtract, hi, lo, waits=[eh])
            return eh, el

        for i, v in enumerate(CONSTS_F32):
            memset(csts[:, i:i + 1], v)
        for i, v in enumerate(CONSTS_BF16):
            memset(cstb[:, i:i + 1], v)
        memset(ones_f[:, :], 1.0)
        memset(ones_b[:, :], 1.0)
        memset(onesr_f[:, :], 1.0)
        memset(onesr_b[:, :], 1.0)
        iota(idm_i[:, :], [[1, 128]], 0, -1)
        iota(rep8[:, :], [[1, 64]], 0, -8)
        for j in range(24):
            kc = j if j < 8 else (j - 8)
            iota(thr_i[:, j:j + 1], [[1, 1]], 128 * kc, 1)
        memset(bcol[:, :], 0.0)
        P.dma("sync", qbs[:, :], qbin[:, :], sem="dO")
        P.dma("sync", qsels[:, :], qselin[:, :], sem="dO")
        P.dma("sync", b0fs[:, :], b0f[:, :], sem="dO")
        P.dma("sync", g0s[:, :], g0[:, :], sem="dO")
        P.gate()
        tsc(idm_i[:, :], 0.0, ALU.is_equal, idm[:, :])
        tsc(rep8[:, :], 0.0, ALU.is_ge, xst[0:8, 0:64])
        tsc(rep8[:, :], 7.0, ALU.is_le, xst[0:8, 64:128])
        P.gate()
        tt_(xst[0:8, 0:64], ALU.mult, xst[0:8, 64:128], repm[:, :])
        scopy(thr_i[:, :], thr[:, :])
        mm(pb[0][:, 0:1], onesr_f[:, :], qbs[0:1, 0:1], True, True)
        mm(pb[1][:, 0:1], onesr_f[:, :], qbs[0:1, 1:2], True, True, inc="T")
        P.gate()
        scopy(pb[0][:, 0:1], qbc[:, 0:1])
        scopy(pb[1][:, 0:1], qbc[:, 1:2])
        P.gate()
        tsc(thr[:, 0:8], qbc[:, 0:1], ALU.subtract, thr[:, 0:8])
        tsc(thr[:, 8:24], qbc[:, 1:2], ALU.subtract, thr[:, 8:24])
        split(g0s[:, :], g0h[:, :], g0l[:, :])
        P.gate()

        def rhs_act(k0, tt):
            hi = act_hi[:, k0 * TOK + tt * TW: k0 * TOK + (tt + 1) * TW]
            lo = act_lo[:, k0 * TOK + tt * TW: k0 * TOK + (tt + 1) * TW]
            return hi, lo

        def stream_proj(src2d, n_list, ps_pair, epilogue, mask2d=None,
                        sign=False, npass=3, extra_group_mm=None, mstage=None):
            ev_sp, ev_tq, ev_ep, ev_wm = {}, {}, {}, {}
            j = 0
            for ni, n in enumerate(n_list):
                rows = slice(n * 128, (n + 1) * 128)
                psA, psB = ps_pair(ni)
                for qt in range(QT):
                    par = j % 2
                    sl = slice(par * QW, (par + 1) * QW)
                    csl = slice(qt * QW, (qt + 1) * QW)
                    ev_d = P.dma("sync", wstage[:, sl], src2d[rows, csl],
                                 waits=[ev_sp.get(j - 2)], sem="dW%d" % par)
                    if mask2d is not None:
                        ev_dm = P.dma("sync", mstage[:, 0:QW], mask2d[rows, csl],
                                      waits=[ev_wm.get(j - 1)], sem="dM")
                        ev_w = tt_(wstage[:, sl], ALU.mult, mstage[:, 0:QW],
                                   wstage[:, sl], waits=[ev_d, ev_dm,
                                                         ev_tq.get(j - 2)])
                        ev_wm[j] = ev_w
                        src_rdy = [ev_w]
                    else:
                        src_rdy = [ev_d, ev_tq.get(j - 2)]
                    if sign:
                        ev_hi = act(wstage[:, sl], AF.Sign, whl_hi[:, sl],
                                    waits=src_rdy)
                        ev_sp[j] = ev_hi
                    else:
                        ev_hi = scopy(wstage[:, sl], whl_hi[:, sl], waits=src_rdy)
                        ev_lo = tt_(wstage[:, sl], ALU.subtract, whl_hi[:, sl],
                                    whl_lo[:, sl], waits=[ev_hi])
                        ev_sp[j] = ev_lo
                    sem_sp = "S" if sign else "V"
                    last = None
                    for kk in range(KCQ):
                        k0 = qt * KCQ + kk
                        lh = whl_hi[:, par * QW + kk * 128:par * QW + (kk + 1) * 128]
                        ll = whl_lo[:, par * QW + kk * 128:par * QW + (kk + 1) * 128]
                        rh0, rl0 = rhs_act(k0, 0)
                        rh1, rl1 = rhs_act(k0, 1)
                        w0 = []
                        if kk == 0:
                            w0 = [ev_sp[j]]
                            if qt == 0 and ni >= 2:
                                w0.append(ev_ep.get(n_list[ni - 2]))
                        first = (qt == 0 and kk == 0)
                        is_last = (qt == QT - 1 and kk == KCQ - 1 and
                                   extra_group_mm is None)
                        qend = (kk == KCQ - 1)
                        mm(psA, lh, rh0, first, False, waits=w0)
                        mm(psA, lh, rl0, False, is_last and npass == 2,
                           inc=None)
                        mm(psB, lh, rh1, first, False)
                        last = mm(psB, lh, rl1, False, is_last and npass == 2,
                                  inc="T" if (qend and npass == 2) else None)
                        if npass == 3:
                            mm(psA, ll, rh0, False, is_last)
                            last = mm(psB, ll, rh1, False, is_last,
                                      inc="T" if qend else None)
                    ev_tq[j] = last
                    j += 1
                evT = ev_tq[j - 1]
                if extra_group_mm is not None:
                    evT = extra_group_mm(ni, psA, psB)
                ev_ep[n] = epilogue(ni, evT)
            return ev_ep

        def ln_stats(src2d, m_b, r_b):
            ev_t = {}
            for k0 in range(KC):
                par = k0 % 2
                xs = xst[:, par * TOK:(par + 1) * TOK]
                xq = evac[:, par * TOK:(par + 1) * TOK]
                ev_d = P.dma("sync", xs, src2d[k0 * 128:(k0 + 1) * 128, :],
                             waits=[ev_t.get(k0 - 2)], sem="dX%d" % par)
                ev_s = act(xs, AF.Square, xq, waits=[ev_d, ev_t.get(k0 - 2)])
                first, lastk = (k0 == 0), (k0 == KC - 1)
                mm(pb[4][0:1, :], ones_f[:, :], xs[:, 0:TW], first, lastk,
                   waits=[ev_d])
                mm(pb[5][0:1, :], ones_f[:, :], xs[:, TW:TOK], first, lastk)
                mm(pb[6][0:1, :], ones_f[:, :], xq[:, 0:TW], first, lastk,
                   waits=[ev_s])
                ev_t[k0] = mm(pb[7][0:1, :], ones_f[:, :], xq[:, TW:TOK], first,
                              lastk, inc="T")
            P.gate()
            scopy(pb[4][0:1, :], m_b[0:1, 0:TW], scale=1.0 / 4096.0)
            scopy(pb[5][0:1, :], m_b[0:1, TW:TOK], scale=1.0 / 4096.0)
            scopy(pb[6][0:1, :], stat[0:1, 0:TW], scale=1.0 / 4096.0)
            scopy(pb[7][0:1, :], stat[0:1, TW:TOK], scale=1.0 / 4096.0)
            P.gate()
            tt_(m_b[0:1, 0:TOK], ALU.mult, m_b[0:1, 0:TOK], r_b[0:1, 0:TOK])
            P.gate()
            tt_(stat[0:1, 0:TOK], ALU.subtract, r_b[0:1, 0:TOK],
                stat[0:1, 0:TOK])
            P.gate()
            act(stat[0:1, 0:TOK], AF.Sqrt, stat[0:1, 0:TOK], bias=EPS)
            P.gate()
            recip(stat[0:1, 0:TOK], r_b[0:1, 0:TOK])
            P.gate()
            for tt in range(TT):
                mm(pb[4 + tt], onesr_f[:, :], m_b[0:1, tt * TW:(tt + 1) * TW],
                   True, True)
                mm(pb[6 + tt], onesr_f[:, :],
                   r_b[0:1, tt * TW:(tt + 1) * TW], True, True,
                   inc="T" if tt else None)
            P.gate()
            for tt in range(TT):
                scopy(pb[4 + tt][:, :], m_b[:, tt * TW:(tt + 1) * TW])
                scopy(pb[6 + tt][:, :], r_b[:, tt * TW:(tt + 1) * TW])
            P.gate()

        def ln_apply_and_router(src2d, m_b, r_b):
            ev_t = {}
            for k0 in range(KC):
                par = k0 % 2
                xs = xst[:, par * TOK:(par + 1) * TOK]
                xm = evac[:, par * TOK:(par + 1) * TOK]
                ev_d = P.dma("sync", xs, src2d[k0 * 128:(k0 + 1) * 128, :],
                             waits=[ev_t.get(k0 - 2)], sem="dX%d" % par)
                e1 = tt_(xs, ALU.subtract, m_b, xm, waits=[ev_d, ev_t.get(k0 - 2)])
                e2 = tt_(xm, ALU.mult, r_b, xm, waits=[e1])
                hi = act_hi[:, k0 * TOK:(k0 + 1) * TOK]
                lo = act_lo[:, k0 * TOK:(k0 + 1) * TOK]
                eh, el = split(xm, hi, lo, waits=[e2])
                first, lastk = (k0 == 0), (k0 == KC - 1)
                gh = g0h[:, k0 * 8:(k0 + 1) * 8]
                gl = g0l[:, k0 * 8:(k0 + 1) * 8]
                ev_mm = None
                for tt in range(TT):
                    ps = pb[6][0:8, :] if tt == 0 else pb[7][0:8, :]
                    hs = hi[:, tt * TW:(tt + 1) * TW]
                    ls = lo[:, tt * TW:(tt + 1) * TW]
                    mm(ps, gh, hs, first, False, waits=[el] if tt == 0 else [])
                    mm(ps, gh, ls, False, False)
                    ev_mm = mm(ps, gl, hs, False, lastk,
                               inc="T" if tt == 1 else None)
                ev_t[k0] = ev_mm
            P.gate()

        ph1 = ExitStack()
        mstage = ph1.enter_context(nc.sbuf_tensor("mstage", [128, QW], I32))
        scb_h = ph1.enter_context(nc.sbuf_tensor("scb_h", [8, D], BF16))
        scb_l = ph1.enter_context(nc.sbuf_tensor("scb_l", [8, D], BF16))
        ph1r = ExitStack()
        lgs = ph1r.enter_context(nc.sbuf_tensor("lgs0", [8, TOK], F32))
        rting = lgs

        m_b = gbuf[:, 0:TOK]
        r_b = gbuf[:, TOK:2 * TOK]
        ln_stats(x0, m_b, r_b)
        ln_apply_and_router(x0, m_b, r_b)

        tt_(g0s[:, :], ALU.mult, g0s[:, :], evac[:, 0:KC * 8])
        P.gate()
        for k0 in range(KC):
            mm(pb[0][0:1, 0:8], ones_f[:, :], evac[:, k0 * 8:(k0 + 1) * 8],
               k0 == 0, k0 == KC - 1, inc="T" if k0 == KC - 1 else None)
        P.gate()
        act(pb[0][0:1, 0:8], AF.Sqrt, stat[0:1, 0:8])
        P.gate()
        P.op("vector", lambda e: e.tensor_reduce(out=stat[0:1, 8:9],
                                                 in_=stat[0:1, 0:8],
                                                 axis=AX.X, op=ALU.max), (), inc="V")
        P.gate()
        tsc(stat[0:1, 8:9], -64.0, ALU.mult, stat[0:1, 9:10])
        P.gate()
        mm(pb[2][0:8, 0:1], onesr_f[0:1, 0:8], stat[0:1, 9:10], True, True, inc="T")
        P.gate()
        scopy(pb[2][0:8, 0:1], bcol[:, 0:1])
        scopy(pb[6][0:8, :], lgs[:, 0:TW])
        scopy(pb[7][0:8, :], lgs[:, TW:TOK])
        P.gate()
        act(lgs[:, :], AF.Exp, lgs[:, :], bias=bcol[:, 0:1])
        P.gate()
        for tt in range(TT):
            mm(pb[0 + tt][0:1, :], ones_f[0:8, 0:1], lgs[:, tt * TW:(tt + 1) * TW],
               True, True, inc="T" if tt else None)
        P.gate()
        for tt in range(TT):
            recip(pb[0 + tt][0:1, :], stat[0:1, tt * TW:(tt + 1) * TW])
        P.gate()
        for tt in range(TT):
            mm(pb[2 + tt][0:8, :], onesr_f[0:1, 0:8],
               stat[0:1, tt * TW:(tt + 1) * TW], True, True,
               inc="T" if tt else None)
        P.gate()
        for tt in range(TT):
            tt_(lgs[:, tt * TW:(tt + 1) * TW], ALU.mult, pb[2 + tt][0:8, :],
                rting[:, tt * TW:(tt + 1) * TW])
        P.gate()
        split(rting[:, :], rt_hi[:, :], rt_lo[:, :])
        P.gate()
        ph1r.close()

        for half in range(2):
            P.dma("sync", wstage[0:8, 0:QW], is0[:, half * 2048:half * 2048 + QW])
            P.dma("sync", wstage[0:8, QW:2 * QW],
                  is0[:, half * 2048 + QW:(half + 1) * 2048])
            P.gate()
            split(wstage[0:8, 0:2 * QW],
                  scb_h[:, half * 2048:(half + 1) * 2048],
                  scb_l[:, half * 2048:(half + 1) * 2048])
            P.gate()

        ev_s0dma = {}

        def ep_s0(ni, evT):
            par = ni % 2
            ev1 = scopy(pb[0 + 2 * par][:, :], evac[:, par * TOK:par * TOK + TW],
                        waits=[evT, ev_s0dma.get(ni - 2)])
            ev2 = scopy(pb[1 + 2 * par][:, :],
                        evac[:, par * TOK + TW:(par + 1) * TOK])
            ev_s0dma[ni] = P.dma("gpsimd", scr_s0[ni * 128:(ni + 1) * 128, :],
                                 evac[:, par * TOK:(par + 1) * TOK], waits=[ev2],
                                 sem="dS0a" if par == 0 else "dS0b")
            return ev2

        stream_proj(w0a, list(range(KC)),
                    lambda ni: (pb[0 + 2 * (ni % 2)], pb[1 + 2 * (ni % 2)]),
                    ep_s0, mask2d=m0a, npass=3, mstage=mstage)
        P.gate()

        ev_xs = {}
        for k0 in range(KC):
            par = k0 % 2
            psA, psB = pb[4 + 2 * par], pb[5 + 2 * par]
            ish = scb_h[:, k0 * 128:(k0 + 1) * 128]
            isl = scb_l[:, k0 * 128:(k0 + 1) * 128]
            evm = None
            for tt, ps in ((0, psA), (1, psB)):
                rh = rt_hi[:, tt * TW:(tt + 1) * TW]
                rl = rt_lo[:, tt * TW:(tt + 1) * TW]
                mm(ps, ish, rh, True, False,
                   waits=[ev_xs.get(k0 - 2)] if tt == 0 else [])
                mm(ps, ish, rl, False, False)
                evm = mm(ps, isl, rh, False, True, inc="T" if tt == 1 else None)
            hi = act_hi[:, k0 * TOK:(k0 + 1) * TOK]
            lo = act_lo[:, k0 * TOK:(k0 + 1) * TOK]
            xm = evac[:, par * TOK:(par + 1) * TOK]
            e1 = tt_(hi, ALU.add, lo, xm, waits=[evm])
            e2a = tt_(xm[:, 0:TW], ALU.mult, psA[:, :], xm[:, 0:TW], waits=[e1])
            e2b = tt_(xm[:, TW:TOK], ALU.mult, psB[:, :], xm[:, TW:TOK],
                      waits=[e2a])
            ev_xs[k0] = e2b
            split(xm, hi, lo, waits=[e2b])
        P.gate()

        for half in range(2):
            P.dma("sync", wstage[0:8, 0:QW], os0[:, half * 2048:half * 2048 + QW])
            P.dma("sync", wstage[0:8, QW:2 * QW],
                  os0[:, half * 2048 + QW:(half + 1) * 2048])
            P.gate()
            split(wstage[0:8, 0:2 * QW],
                  scb_h[:, half * 2048:(half + 1) * 2048],
                  scb_l[:, half * 2048:(half + 1) * 2048])
            P.gate()

        ev_x1dma, ev_osc, ev_ep_t1, ev_gcons, ev_scons, ev_xcons = ({} for _ in range(6))

        def ep_g0(ni, evT):
            par = ni % 2
            osh = scb_h[:, ni * 128:(ni + 1) * 128]
            osl = scb_l[:, ni * 128:(ni + 1) * 128]
            evos = None
            for tt in range(TT):
                ps = pb[4 + 2 * par + tt]
                rh = rt_hi[:, tt * TW:(tt + 1) * TW]
                rl = rt_lo[:, tt * TW:(tt + 1) * TW]
                mm(ps, osh, rh, True, False,
                   waits=[ev_osc.get(ni - 2)] if tt == 0 else [])
                mm(ps, osh, rl, False, False)
                evos = mm(ps, osl, rh, False, True, inc="T" if tt == 1 else None)
            gpsA, gpsB = pb[0 + 2 * par], pb[1 + 2 * par]
            t1 = evac[:, par * TOK:(par + 1) * TOK]
            gg = gbuf[:, par * TOK:(par + 1) * TOK]
            evc1 = scopy(pb[4 + 2 * par][:, :], gg[:, 0:TW],
                         waits=[evos, ev_gcons.get(ni - 2)])
            evc2 = scopy(pb[5 + 2 * par][:, :], gg[:, TW:TOK])
            ev_o1 = tt_(gpsA[:, :], ALU.mult, gg[:, 0:TW], t1[:, 0:TW],
                        waits=[evT, evc1, ev_ep_t1.get(ni - 2)])
            ev_o2 = tt_(gpsB[:, :], ALU.mult, gg[:, TW:TOK], t1[:, TW:TOK],
                        waits=[evc2])
            ev_osc[ni] = ev_o2
            ev_g = act(t1, AF.Silu, gg, bias=b0fs[:, ni:ni + 1],
                       waits=[ev_o2])
            xx = xst[:, par * TOK:(par + 1) * TOK]
            ev_sl = P.dma("sync", xx, scr_s0[ni * 128:(ni + 1) * 128, :],
                          waits=[ev_s0dma[ni], ev_scons.get(ni - 2)],
                          sem="dX%d" % par)
            ev_sg = tt_(xx, ALU.mult, gg, gg, waits=[ev_g, ev_sl])
            ev_scons[ni] = ev_sg
            ev_xl = P.dma("sync", xx, x0[ni * 128:(ni + 1) * 128, :],
                          waits=[ev_sg], sem="dX%d" % par)
            ev_x1 = tt_(xx, ALU.add, gg, gg, waits=[ev_xl])
            ev_xcons[ni] = ev_x1
            ev_ep_t1[ni] = ev_x1
            ev_x1dma[ni] = P.dma("gpsimd", scr_x1[ni * 128:(ni + 1) * 128, :],
                                 gg, waits=[ev_x1], sem="dXW%d" % par)
            ev_gcons[ni] = ev_x1dma[ni]
            return ev_o2

        stream_proj(w0f, list(range(KC)),
                    lambda ni: (pb[0 + 2 * (ni % 2)], pb[1 + 2 * (ni % 2)]),
                    ep_g0, sign=True, npass=2)
        P.gate()
        ph1.close()

        ph2 = ExitStack()
        lgs = ph2.enter_context(nc.sbuf_tensor("lgs1", [8, TOK], F32))
        rting = lgs
        t2s = ph2.enter_context(nc.sbuf_tensor("t2s", [128, 256], F32))

        P.dma("sync", g0s[:, :], g1[:, :], sem="dO")
        P.gate()
        split(g0s[:, :], g0h[:, :], g0l[:, :])
        P.gate()
        m_b2 = gbuf[:, 0:TOK]
        r_b2 = gbuf[:, TOK:2 * TOK]
        ln_stats(scr_x1, m_b2, r_b2)
        ln_apply_and_router(scr_x1, m_b2, r_b2)

        scopy(pb[6][0:8, :], lgs[:, 0:TW])
        scopy(pb[7][0:8, :], lgs[:, TW:TOK])
        P.gate()
        for c in range(8):
            P.op("tensor", lambda e, c=c: e.transpose(
                out=pb[0][:, c * 8:(c + 1) * 8], in_=lgs[:, c * 128:(c + 1) * 128],
                identity=idm[0:8, 0:8]), (), inc="T" if c == 7 else None)
        P.gate()
        scopy(pb[0][:, 0:64], t2s[:, 0:64])
        P.gate()
        P.op("vector", lambda e: e.tensor_reduce(
            out=t2s[:, 64:72], in_=t2s[:, 0:64].rearrange("p (c e) -> p c e", e=8),
            axis=AX.X, op=ALU.max), (), inc="V")
        P.gate()
        tsc(t2s[:, 64:72], -1.0, ALU.mult, t2s[:, 72:80])
        P.gate()
        for c in range(8):
            tsc(t2s[:, c * 8:(c + 1) * 8], t2s[:, 64 + c:65 + c], ALU.is_equal,
                t2s[:, 80 + c * 8:88 + c * 8])
        P.gate()
        for c in range(8):
            P.op("vector", lambda e, c=c: e.scalar_tensor_tensor(
                out=t2s[:, 144 + c * 8:152 + c * 8],
                in0=t2s[:, 80 + c * 8:88 + c * 8], scalar=-1e30, op0=ALU.mult,
                in1=t2s[:, c * 8:(c + 1) * 8], op1=ALU.add), (), inc="V")
        P.gate()
        P.op("vector", lambda e: e.tensor_reduce(
            out=t2s[:, 208:216], in_=t2s[:, 144:208].rearrange("p (c e) -> p c e", e=8),
            axis=AX.X, op=ALU.max), (), inc="V")
        P.gate()
        tt_(t2s[:, 208:216], ALU.subtract, t2s[:, 64:72], t2s[:, 216:224])
        P.gate()
        act(t2s[:, 216:224], AF.Exp, t2s[:, 216:224])
        P.gate()
        tsc(t2s[:, 216:224], 1.0, ALU.add, t2s[:, 216:224])
        P.gate()
        recip(t2s[:, 216:224], t2s[:, 224:232])
        P.gate()
        tsc(t2s[:, 224:232], 2.0, ALU.mult, t2s[:, 224:232])
        P.gate()
        for c in range(8):
            act(t2s[:, c * 8:(c + 1) * 8], AF.Exp, t2s[:, 80 + c * 8:88 + c * 8],
                bias=t2s[:, 72 + c:73 + c])
            tsc(t2s[:, c * 8:(c + 1) * 8], t2s[:, 208 + c:209 + c], ALU.is_ge,
                t2s[:, 144 + c * 8:152 + c * 8])
        P.gate()
        for c in range(8):
            tt_(t2s[:, 80 + c * 8:88 + c * 8], ALU.mult,
                t2s[:, 144 + c * 8:152 + c * 8], t2s[:, 80 + c * 8:88 + c * 8])
        P.gate()
        for c in range(8):
            tsc(t2s[:, 80 + c * 8:88 + c * 8], t2s[:, 224 + c:225 + c], ALU.mult,
                t2s[:, 80 + c * 8:88 + c * 8])
        P.gate()
        for c in range(8):
            ps = pb[2] if c < 4 else pb[3]
            P.op("tensor", lambda e, c=c, ps=ps: e.transpose(
                out=ps[0:8, (c % 4) * 128:(c % 4 + 1) * 128],
                in_=t2s[:, 80 + c * 8:88 + c * 8],
                identity=idm[:, :]), (), inc="T" if c == 7 else None)
        P.gate()
        scopy(pb[2][0:8, :], rting[:, 0:TW])
        scopy(pb[3][0:8, :], rting[:, TW:TOK])
        P.gate()
        split(rting[:, :], rt_hi[:, :], rt_lo[:, :])
        P.gate()
        ph2.close()

        ph3h = ExitStack()
        h1h = ph3h.enter_context(nc.sbuf_tensor("h1h", [64, TOK], BF16))
        h1l = ph3h.enter_context(nc.sbuf_tensor("h1l", [64, TOK], BF16))

        ph2t = ExitStack()
        v64 = ph2t.enter_context(nc.sbuf_tensor("v64", [64, 24], F32))
        v64i = ph2t.enter_context(nc.sbuf_tensor("v64i", [64, 1], I32))
        mi64 = ph2t.enter_context(nc.sbuf_tensor("mi64", [64, 512], I32))
        u_i = mi64[0:1, :]
        t64a = xst[0:64, 1024:1536]
        t64b = xst[0:64, 1536:2048]
        cq = ph2t.enter_context(nc.sbuf_tensor("cq", [64, TOK], F32))
        sq_t = ph2t.enter_context(nc.sbuf_tensor("sq_t", [64, TOK], F32))
        ck = ph2t.enter_context(nc.sbuf_tensor("ck", [64, TOK], F32))
        sk_t = ph2t.enter_context(nc.sbuf_tensor("sk_t", [64, TOK], F32))
        qrot = xst[:, 0:TOK]
        r64 = gbuf[0:64, 0:512]
        u_row = gbuf[0:1, 512:1024]
        qsp_h = ph2t.enter_context(nc.sbuf_tensor("qsp_h", [128, TOK], BF16))
        qsp_l = ph2t.enter_context(nc.sbuf_tensor("qsp_l", [128, TOK], BF16))

        P.serial = True
        memset(v64[:, 0:1], 10000.0)
        iota(v64i[:, :], [[1, 1]], 0, 1, inc="G")
        iota(u_i[:, :], [[1, 512]], 0, 0, inc="G")
        act(v64[:, 0:1], AF.Ln, v64[:, 0:1])
        scopy(v64i[:, :], v64[:, 1:2])
        scopy(u_i[:, :], u_row)
        tt_(v64[:, 1:2], ALU.mult, v64[:, 0:1], v64[:, 2:3])
        tsc(v64[:, 2:3], -1.0 / 64.0, ALU.mult, v64[:, 2:3])
        act(v64[:, 2:3], AF.Exp, v64[:, 3:4])
        fvec = v64[:, 3:4]

        def sincos_block(ang_sbuf, out_sin, out_cos):
            for which, out in (("s", out_sin), ("c", out_cos)):
                if which == "c":
                    tsc(ang_sbuf, PIH, ALU.add, ang_sbuf)
                tsc(ang_sbuf, INV2PI, ALU.mult, r64)
                P.op("vector", lambda e, a=r64, o=mi64[:, :]:
                     e.tensor_copy(out=o, in_=a), (), inc="V")
                P.gate()
                scopy(mi64[:, :], r64)
                P.op("vector", lambda e, a=r64, b=ang_sbuf, o=r64:
                     e.scalar_tensor_tensor(out=o, in0=a, scalar=NEG2PI,
                                            op0=ALU.mult, in1=b, op1=ALU.add),
                     (), inc="V")
                P.gate()
                tsc(r64, PI_IN, ALU.min, r64)
                tsc(r64, -PI_IN, ALU.max, r64)
                act(r64, AF.Sin, out)

        for blk in range(2):
            sl = slice(blk * 512, blk * 512 + 512)
            tt_(qbc[0:64, blk:blk + 1], ALU.mult, fvec, v64[:, 8:9])
            mm(pb[4][0:64, :], onesr_f[0:1, 0:64], u_row, True, True, inc="T")
            tsc(pb[4][0:64, :], fvec, ALU.mult, t64a, s2=v64[:, 8:9], op1=ALU.add)
            sincos_block(t64a, sk_t[:, sl], ck[:, sl])
            tsc(ck[:, sl], RSCALE, ALU.mult, cq[:, sl])
            tsc(sk_t[:, sl], RSCALE, ALU.mult, sq_t[:, sl])
        P.serial = False
        P.gate()

        def rope_apply(psA, psB, ctab, stab, out_f32, waits):
            last = None
            for tt, ps in ((0, psA), (1, psB)):
                c = ctab[:, tt * TW:(tt + 1) * TW]
                s = stab[:, tt * TW:(tt + 1) * TW]
                oE = out_f32[0:64, tt * TW:(tt + 1) * TW]
                oO = out_f32[64:128, tt * TW:(tt + 1) * TW]
                w = waits if tt == 0 else [last]
                e1 = tt_(ps[0:64, :], ALU.mult, c, t64a[:, :], waits=w)
                e2 = tt_(ps[64:128, :], ALU.mult, s, t64b[:, :], waits=[e1])
                e3 = tt_(t64a[:, :], ALU.subtract, t64b[:, :], oE, waits=[e2])
                e4 = tt_(ps[0:64, :], ALU.mult, s, t64a[:, :], waits=[e3])
                e5 = tt_(ps[64:128, :], ALU.mult, c, t64b[:, :], waits=[e4])
                last = tt_(t64a[:, :], ALU.add, t64b[:, :], oO, waits=[e5])
            return last

        ev_ccdma = {}
        ev_kshi = {}

        def ep_k(ni, evT):
            ev_r = rope_apply(pb[0 + 2 * (ni % 2)], pb[1 + 2 * (ni % 2)],
                              ck, sk_t, qrot,
                              waits=[evT, ev_ccdma.get(ni - 1)])
            ev_ccdma[ni] = P.dma("gpsimd", cc_in[ni * 128:(ni + 1) * 128, :],
                                 qrot[:, :], waits=[ev_r], sem="dCC")
            return ev_r

        stream_proj(w1k, list(range(NKV)),
                    lambda ni: (pb[0 + 2 * (ni % 2)], pb[1 + 2 * (ni % 2)]),
                    ep_k)
        P.gate()

        ev_vt, ev_vsp = {}, {}
        for dvh in range(2):
            for k0 in range(KC):
                j = dvh * KC + k0
                par = j % 2
                wsl = slice(par * QW, par * QW + TW)
                ev_d = P.dma("sync", wstage[:, wsl],
                             w1v[k0 * 128:(k0 + 1) * 128, dvh * TW:(dvh + 1) * TW],
                             waits=[ev_vsp.get(j - 2)], sem="dW%d" % par)
                eh = scopy(wstage[:, wsl], whl_hi[:, wsl],
                           waits=[ev_d, ev_vt.get(j - 2)])
                el = tt_(wstage[:, wsl], ALU.subtract, whl_hi[:, wsl],
                         whl_lo[:, wsl], waits=[eh])
                ev_vsp[j] = el
                ev = None
                for tc in range(8):
                    ps = pb[tc]
                    lhH = act_hi[:, k0 * TOK + tc * 128:k0 * TOK + (tc + 1) * 128]
                    lhL = act_lo[:, k0 * TOK + tc * 128:k0 * TOK + (tc + 1) * 128]
                    first, lastk = (k0 == 0), (k0 == KC - 1)
                    mm(ps, lhH, whl_hi[:, wsl], first, False,
                       waits=[el] if tc == 0 else [])
                    mm(ps, lhH, whl_lo[:, wsl], False, False)
                    ev = mm(ps, lhL, whl_hi[:, wsl], False, lastk,
                            inc="T" if tc == 7 else None)
                ev_vt[j] = ev
            P.gate()
            for tc in range(8):
                buf = (evac if tc < 4 else gbuf)[:, (tc % 4) * TW:(tc % 4 + 1) * TW]
                evc = scopy(pb[tc][:, :], buf)
                P.dma("gpsimd", cc_in[1024 + tc * 128:1024 + (tc + 1) * 128,
                                      dvh * TW:(dvh + 1) * TW], buf, waits=[evc],
                      sem="dCC")
            P.gate()

        ev_cc = P.op("gpsimd", lambda e: e.collective_compute(
            "AllGather", ALU.bypass,
            replica_groups=[[0, 1], [2, 3], [4, 5], [6, 7]],
            ins=[cc_in[:, :]], outs=[cc_out[:, :]]), (), inc="C")

        ev_qdma = {}

        def ep_q(ni, evT):
            ev_r = rope_apply(pb[0 + 2 * (ni % 2)], pb[1 + 2 * (ni % 2)],
                              cq, sq_t, qrot,
                              waits=[evT, ev_qdma.get(ni - 1)])
            eh, el = split(qrot[:, :], qsp_h[:, :], qsp_l[:, :], waits=[ev_r])
            d1 = P.dma("gpsimd", scr_qh[ni * 128:(ni + 1) * 128, :], qsp_h[:, :],
                       waits=[el], sem="dQW")
            d2 = P.dma("gpsimd", scr_ql[ni * 128:(ni + 1) * 128, :], qsp_l[:, :],
                       waits=[el], sem="dQW")
            ev_qdma[ni] = d2
            return ev_r

        stream_proj(w1q, list(range(NQH)),
                    lambda ni: (pb[0 + 2 * (ni % 2)], pb[1 + 2 * (ni % 2)]),
                    ep_q)
        P.gate()

        P.dma("sync", wstage[:, 0:QW], a1[:, 0:QW], sem="dO")
        P.dma("sync", wstage[:, QW:2 * QW], a1[:, QW:2 * QW], sem="dO")
        P.gate()
        split(wstage[:, :], whl_hi[:, :], whl_lo[:, :])
        P.gate()
        for k0 in range(KC):
            lh = whl_hi[:, k0 * 64:(k0 + 1) * 64]
            ll = whl_lo[:, k0 * 64:(k0 + 1) * 64]
            for tt in range(TT):
                ps = pb[4 + tt][0:64, :]
                rh, rl = rhs_act(k0, tt)
                first, lastk = (k0 == 0), (k0 == KC - 1)
                mm(ps, lh, rh, first, False)
                mm(ps, lh, rl, False, False)
                mm(ps, ll, rh, False, lastk,
                   inc="T" if (lastk and tt == 1) else None)
        P.gate()
        scopy(pb[4][0:64, :], qrot[0:64, 0:TW])
        scopy(pb[5][0:64, :], qrot[0:64, TW:TOK])
        for tt in range(TT):
            ps = pb[6 + tt][0:64, :]
            mm(ps, repm[:, :], rt_hi[:, tt * TW:(tt + 1) * TW], True, False)
            mm(ps, repm[:, :], rt_lo[:, tt * TW:(tt + 1) * TW], False, True,
               inc="T" if tt else None)
        P.gate()
        for tt in range(TT):
            tt_(qrot[0:64, tt * TW:(tt + 1) * TW], ALU.mult, pb[6 + tt][0:64, :],
                qrot[0:64, tt * TW:(tt + 1) * TW])
        P.gate()
        split(qrot[0:64, 0:TOK], h1h[:, :], h1l[:, :])
        P.gate()
        ph2t.close()

        ph2b = ExitStack()
        b1st = ph2b.enter_context(nc.sbuf_tensor("b1st", [64, 2 * 128], F32))
        b1h = ph2b.enter_context(nc.sbuf_tensor("b1h", [64, 2 * 128], BF16))
        b1lo = ph2b.enter_context(nc.sbuf_tensor("b1lo", [64, 2 * 128], BF16))

        ev_g1dma = {}
        ev_b1mm = {}

        def lora_B(ni, psA, psB):
            par = ni % 2
            bsl = slice(par * 128, (par + 1) * 128)
            ev_d = P.dma("sync", b1st[:, bsl],
                         b1d[:, ni * 128:(ni + 1) * 128],
                         waits=[ev_b1sp.get(ni - 2)], sem="dB%d" % par)
            ehb, elb = split(b1st[:, bsl], b1h[:, bsl], b1lo[:, bsl],
                             waits=[ev_d, ev_b1mm.get(ni - 2)])
            ev_b1sp[ni] = elb
            ev = None
            for tt, ps in ((0, psA), (1, psB)):
                mm(ps, b1h[:, bsl], h1h[:, tt * TW:(tt + 1) * TW], False, False,
                   waits=[elb] if tt == 0 else [])
                mm(ps, b1h[:, bsl], h1l[:, tt * TW:(tt + 1) * TW], False, False)
                ev = mm(ps, b1lo[:, bsl], h1h[:, tt * TW:(tt + 1) * TW], False,
                        True, inc="T" if tt == 1 else None)
            ev_b1mm[ni] = ev
            return ev

        ev_b1sp = {}

        def ep_g1(ni, evT):
            par = ni % 2
            gg = gbuf[:, par * TOK:(par + 1) * TOK]
            psA, psB = pb[0 + 2 * par], pb[1 + 2 * par]
            ev1 = act(psA[:, :], AF.Silu, gg[:, 0:TW],
                      waits=[evT, ev_g1dma.get(ni - 2)])
            ev2 = act(psB[:, :], AF.Silu, gg[:, TW:TOK])
            ev_g1dma[ni] = P.dma("gpsimd", scr_g1[ni * 128:(ni + 1) * 128, :],
                                 gg, waits=[ev2], sem="dGW%d" % par)
            return ev2

        stream_proj(w1b, list(range(KC)),
                    lambda ni: (pb[0 + 2 * (ni % 2)], pb[1 + 2 * (ni % 2)]),
                    ep_g1, extra_group_mm=lora_B)
        P.gate()
        ph2b.close()
        ph3h.close()

        ph3 = ExitStack()
        ksb = wstage[:, 0:2048]
        vsb = wstage[:, 0:2048]
        khi = whl_hi[:, 0:2048]
        klo = whl_lo[:, 0:2048]
        vhi = ph3.enter_context(nc.sbuf_tensor("vhi", [128, 2048], BF16))
        vlo = ph3.enter_context(nc.sbuf_tensor("vlo", [128, 2048], BF16))
        qh2 = ph3.enter_context(nc.sbuf_tensor("qh2", [128, TOK], BF16))
        ql2 = ph3.enter_context(nc.sbuf_tensor("ql2", [128, TOK], BF16))
        et = ph3.enter_context(nc.sbuf_tensor("et", [128, 2 * TW], F32))
        ehi = ph3.enter_context(nc.sbuf_tensor("ehi", [128, 2 * TW], BF16))
        elo = ph3.enter_context(nc.sbuf_tensor("elo", [128, 2 * TW], BF16))
        negb = ph3.enter_context(nc.sbuf_tensor("negb", [1, TOK], BF16))
        viota = ph3.enter_context(nc.sbuf_tensor("viota", [128, TW], I32))

        iota(viota[:, :], [[1, TW]], 0, 0, inc="G")
        P.gate()

        RHO = [0, 1, 1, 0]
        LAM = [0, 0, 1, 1]

        C_exp = {}
        C_av = {0: None, 1: None}
        C_lastS = None
        C_rsfree = None
        C_ofree = None
        C_bfree = None
        C_obuf = None
        C_pq = {0: None, 1: None}
        C_xq = {0: None, 1: None}
        C_negb = None

        for hv in range(NKV):
            kds = []
            for b in range(4):
                r0 = RHO[b] * 2048 + hv * 128
                c0 = LAM[b] * TW
                kds.append(P.dma("sync", ksb[:, b * TW:(b + 1) * TW],
                                 cc_out[r0:r0 + 128, c0:c0 + TW],
                                 waits=[("C", 1), C_lastS], sem="dK"))
            ehK, elK = split(ksb[:, :], khi[:, :], klo[:, :],
                             waits=kds + [C_lastS])
            e1 = tt_(khi[:, :], ALU.add, klo[:, :], vsb[:, :], waits=[elK])
            e2 = act(vsb[:, :], AF.Square, vsb[:, :], waits=[e1])
            evn = {}
            ev_kn = None
            for b in range(4):
                ps = pb[5 + (b % 2)]
                ev_kn = mm(ps[0:1, :], ones_f[:, :], vsb[:, b * TW:(b + 1) * TW],
                           True, True,
                           waits=([e2] if b < 2 else []) + [evn.get(b - 2),
                                                            C_pq[b % 2]],
                           inc="T")
                evn[b] = scopy(ps[0:1, :], gbuf[0:1, b * TW:(b + 1) * TW],
                               waits=[ev_kn, C_bfree])
            C_pq = {0: evn[2], 1: evn[3]}
            ev_red = P.op("vector", lambda e: e.tensor_reduce(
                out=stat[0:1, 1024:1025], in_=gbuf[0:1, 0:2048], axis=AX.X,
                op=ALU.max), [evn[2], evn[3]], inc="V")
            ev_sq = act(stat[0:1, 1024:1025], AF.Sqrt, stat[0:1, 1025:1026],
                        waits=[ev_red])
            ev_nk = tsc(stat[0:1, 1025:1026], -1.0, ALU.mult,
                        stat[0:1, 1026:1027], waits=[ev_sq])
            vds = []
            for g in range(16):
                b = g // 4
                lc = LAM[b] * 4 + (g % 4)
                r0 = RHO[b] * 2048 + 1024 + lc * 128
                vds.append(P.dma("sync", vsb[:, g * 128:(g + 1) * 128],
                                 cc_out[r0:r0 + 128, hv * 128:(hv + 1) * 128],
                                 waits=[ev_kn, C_av[0], C_av[1]], sem="dV"))
            ehV, elV = split(vsb[:, :], vhi[:, :], vlo[:, :], waits=vds)

            for hq in range(hv * 4, hv * 4 + 4):
                par = hq % 2
                qh = qh2[:, :]
                ql = ql2[:, :]
                d1 = P.dma("sync", qh, scr_qh[hq * 128:(hq + 1) * 128, :],
                           waits=[C_lastS], sem="dQ")
                d2 = P.dma("sync", ql, scr_ql[hq * 128:(hq + 1) * 128, :],
                           sem="dQ")
                xq = xst[:, par * TOK:(par + 1) * TOK]
                e1 = tt_(qh, ALU.add, ql, xq, waits=[d1, d2, C_xq[par]])
                e2 = act(xq, AF.Square, xq, waits=[e1])
                evq = None
                for tt in range(TT):
                    evq = mm(pb[5 + tt][0:1, :], ones_f[:, :],
                             xq[:, tt * TW:(tt + 1) * TW], True, True,
                             waits=[e2, C_pq[tt]], inc="T")
                    evq = act(pb[5 + tt][0:1, :], AF.Sqrt,
                              stat[0:1, tt * TW:(tt + 1) * TW],
                              waits=[evq, C_negb])
                    C_pq[tt] = evq
                C_xq[par] = evq
                evb = tsc(stat[0:1, 0:TOK], stat[0:1, 1026:1027], ALU.mult,
                          negb[0:1, :], waits=[evq, ev_nk, C_lastS])
                C_negb = evb

                for qi in range(2):
                    ev_exp, ev_av = dict(C_exp), dict(C_av)
                    kcnt = KCNT[qi]
                    qsl = slice(qi * TW, (qi + 1) * TW)

                    def emit_S(kc):
                        p2 = kc % 2
                        psS = pb[p2]
                        w0 = [elK, evb, d2, ev_exp.get(kc - 2, ev_exp.get(p2))]
                        mm(psS, khi[:, kc * 128:(kc + 1) * 128], qh[:, qsl],
                           True, False, waits=w0)
                        mm(psS, khi[:, kc * 128:(kc + 1) * 128], ql[:, qsl],
                           False, False)
                        mm(psS, klo[:, kc * 128:(kc + 1) * 128], qh[:, qsl],
                           False, False)
                        return mm(psS, onesr_b[:, :], negb[0:1, qsl], False,
                                  True, inc="T")

                    def emit_EV(kc, evS):
                        p2 = kc % 2
                        psS = pb[p2]
                        esl = slice(p2 * TW, (p2 + 1) * TW)
                        avg = ev_av.get(kc - 2, ev_av.get(p2))
                        evE = act(psS[:, :], AF.Exp, et[:, esl],
                                  waits=[evS, avg])
                        evEM = P.op("vector", lambda e, esl=esl, col=qi * 8 + kc:
                                    e.scalar_tensor_tensor(
                                        out=et[:, esl], in0=viota[:, :],
                                        scalar=thr[:, col:col + 1], op0=ALU.is_ge,
                                        in1=et[:, esl], op1=ALU.mult),
                                    [evE, avg], inc="V")
                        evH = scopy(et[:, esl], ehi[:, esl], waits=[evEM])
                        evL = tt_(et[:, esl], ALU.subtract, ehi[:, esl],
                                  elo[:, esl], waits=[evH])
                        ev_exp[kc] = evE
                        return evL

                    def emit_AV(kc, evL):
                        p2 = kc % 2
                        esl = slice(p2 * TW, (p2 + 1) * TW)
                        first, lastk = (kc == 0), (kc == kcnt - 1)
                        w_rs = [evL] + ([C_rsfree] if first else [])
                        mm(pb[3][0:1, :], ones_b[:, :], ehi[:, esl], first,
                           False, waits=w_rs)
                        mm(pb[3][0:1, :], ones_b[:, :], elo[:, esl], False,
                           lastk)
                        w_o = [elV, C_ofree] if first else []
                        mm(pb[2], vhi[:, kc * 128:(kc + 1) * 128], ehi[:, esl],
                           first, False, waits=w_o)
                        mm(pb[2], vhi[:, kc * 128:(kc + 1) * 128], elo[:, esl],
                           False, False)
                        ev_av[kc] = mm(pb[2], vlo[:, kc * 128:(kc + 1) * 128],
                                       ehi[:, esl], False, lastk, inc="T")
                        return ev_av[kc]

                    pend = []
                    evS_last = None
                    for kc in range(kcnt):
                        evS_last = emit_S(kc)
                        if len(pend) >= 2:
                            emit_AV(*pend.pop(0))
                        evL = emit_EV(kc, evS_last)
                        pend.append((kc, evL))
                    for p_ in pend:
                        emit_AV(*p_)
                    C_lastS = evS_last
                    lk = kcnt - 1
                    C_exp = {0: ev_exp[lk if lk % 2 == 0 else lk - 1],
                             1: ev_exp[lk if lk % 2 == 1 else lk - 1]}
                    C_av = {0: ev_av[lk if lk % 2 == 0 else lk - 1],
                            1: ev_av[lk if lk % 2 == 1 else lk - 1]}

                    evR = recip(pb[3][0:1, :], gbuf[0:1, TW:2 * TW],
                                waits=[ev_av[kcnt - 1], C_bfree])
                    C_rsfree = evR
                    evB = mm(pb[4], onesr_f[:, :], gbuf[0:1, TW:2 * TW],
                             True, True, waits=[evR], inc="T")
                    o1 = scopy(pb[2][:, :], gbuf[:, 0:TW],
                               waits=[evB, C_bfree])
                    C_ofree = o1
                    onrm = tt_(gbuf[:, 0:TW], ALU.mult, pb[4][:, :],
                               evac[:, 0:TW], waits=[o1, C_obuf])
                    C_bfree = onrm
                    dsts = act_hi[:, hq * TOK + qi * TW:hq * TOK + (qi + 1) * TW]
                    dstl = act_lo[:, hq * TOK + qi * TW:hq * TOK + (qi + 1) * TW]
                    eh2, el2 = split(evac[:, 0:TW], dsts, dstl, waits=[onrm])
                    C_obuf = el2
        P.gate()
        ph3.close()

        ev_ocons, ev_gcons2, ev_xcons2 = {}, {}, {}

        def ep_out(ni, evT):
            par = ni % 2
            gg = gbuf[:, par * TOK:(par + 1) * TOK]
            xx = xst[:, par * TOK:(par + 1) * TOK]
            oo = evac[:, par * TOK:(par + 1) * TOK]
            ev_gl = P.dma("sync", gg, scr_g1[ni * 128:(ni + 1) * 128, :],
                          waits=[ev_gcons2.get(ni - 2)], sem="dB%d" % par)
            ev_xl = P.dma("sync", xx, scr_x1[ni * 128:(ni + 1) * 128, :],
                          waits=[ev_xcons2.get(ni - 2)], sem="dX%d" % par)
            e1 = tt_(pb[0 + 2 * par][:, :], ALU.mult, gg[:, 0:TW], oo[:, 0:TW],
                     waits=[evT, ev_gl, ev_ocons.get(ni - 2)])
            e2 = tt_(pb[1 + 2 * par][:, :], ALU.mult, gg[:, TW:TOK], oo[:, TW:TOK],
                     waits=[e1])
            ev_gcons2[ni] = e2
            e3 = tt_(xx, ALU.add, oo, oo, waits=[e2, ev_xl])
            ev_xcons2[ni] = e3
            ev_ocons[ni] = P.dma("gpsimd", out0[ni * 128:(ni + 1) * 128, :], oo,
                                 waits=[e3], sem="dOut" if par == 0 else "dOut1")
            return e2

        stream_proj(w1o, list(range(KC)),
                    lambda ni: (pb[0 + 2 * (ni % 2)], pb[1 + 2 * (ni % 2)]),
                    ep_out)
        P.final_wait("gpsimd", ("dOut", P.cnt["dOut"]))
        P.final_wait("gpsimd", ("dOut1", P.cnt["dOut1"]))

        with nc.Block() as block:
            P.emit(block, sems)

    return nc


_PERMQ = np.concatenate([h * 128 + np.concatenate([np.arange(0, 128, 2),
                                                   np.arange(1, 128, 2)])
                         for h in range(NQH)])
_PERMK = np.concatenate([h * 128 + np.concatenate([np.arange(0, 128, 2),
                                                   np.arange(1, 128, 2)])
                         for h in range(NKV)])


def _t4(w):
    return np.ascontiguousarray(
        w.reshape(KC, 128, KC, 128).transpose(2, 1, 0, 3).reshape(D, D))


def _t4k(w):
    return np.ascontiguousarray(
        w.reshape(KC, 128, NKV, 128).transpose(2, 1, 0, 3).reshape(NKV * 128, D))


_NC_CACHE = {}


def _get_nc(debug=False):
    if debug not in _NC_CACHE:
        _NC_CACHE[debug] = build_nc(debug)
    return _NC_CACHE[debug]


def make_in_maps(L0_attn_data, L0_attn_weight, L0_attn_mask, L0_ffn_weight,
                 L0_ffn_bias, L0_ffn_gate_w, L0_ffn_in_channel_scale,
                 L0_ffn_out_channel_scale, L1_attn_wq, L1_attn_wk, L1_attn_wv,
                 L1_attn_wo, L1_ffn_W_base, L1_ffn_gate_W, L1_ffn_A, L1_ffn_B):
    f = np.ascontiguousarray
    w0a = _t4(np.asarray(L0_attn_weight).T)
    m0a = _t4(np.asarray(L0_attn_mask).T.astype(np.int32))
    w0f = _t4(np.asarray(L0_ffn_weight).T)
    b0fv = f(np.asarray(L0_ffn_bias).reshape(KC, 128).T)
    g0v = f(np.asarray(L0_ffn_gate_w).T.reshape(KC, 128, 8)
            .transpose(1, 0, 2).reshape(128, KC * 8))
    is0v = f(np.asarray(L0_ffn_in_channel_scale))
    os0v = f(np.asarray(L0_ffn_out_channel_scale))
    w1qv = _t4(np.asarray(L1_attn_wq)[:, _PERMQ])
    w1kv = _t4k(np.asarray(L1_attn_wk)[:, _PERMK])
    w1vv = f(np.asarray(L1_attn_wv))
    w1ov = _t4(np.asarray(L1_attn_wo))
    w1bv = _t4(np.asarray(L1_ffn_W_base).T)
    g1v = f(np.asarray(L1_ffn_gate_W).T.reshape(KC, 128, 8)
            .transpose(1, 0, 2).reshape(128, KC * 8))
    a1v = f(np.asarray(L1_ffn_A).reshape(64, D).T.reshape(KC, 128, 64)
            .transpose(1, 0, 2).reshape(128, KC * 64))
    b1v = f(np.asarray(L1_ffn_B).transpose(0, 2, 1).reshape(64, D))
    x = np.asarray(L0_attn_data)

    in_maps = []
    for c in range(8):
        seq, role = c // 2, c % 2
        blocks = (0, 3) if role == 0 else (1, 2)
        xc = np.concatenate([x[seq, 512 * b:512 * b + 512, :] for b in blocks], 0)
        x0v = f(xc.T)
        qbv = np.array([[512.0 * blocks[0], 512.0 * blocks[1]]], np.float32)
        qs = np.zeros((64, 8), np.float32)
        qs[:, blocks[0]] = 1.0
        qs[:, 4 + blocks[1]] = 1.0
        in_maps.append({
            "x0": x0v, "w0a": w0a, "m0a": m0a, "w0f": w0f, "b0f": b0fv,
            "g0": g0v, "is0": is0v, "os0": os0v, "w1q": w1qv, "w1k": w1kv,
            "w1v": w1vv, "w1o": w1ov, "w1b": w1bv, "g1": g1v, "a1": a1v,
            "b1d": b1v, "qb": qbv, "qsel": qs,
        })
    return in_maps


def kernel(**inputs):
    in_maps = make_in_maps(**inputs)
    nc = _get_nc(False)
    res = run_bass_kernel_spmd(nc, in_maps, core_ids=list(range(8)))
    out = np.empty((4, 2048, D), np.float32)
    for c in range(8):
        seq, role = c // 2, c % 2
        blocks = (0, 3) if role == 0 else (1, 2)
        o = np.asarray(res.results[c]["out0"])
        for i, b in enumerate(blocks):
            out[seq, 512 * b:512 * b + 512, :] = o[:, 512 * i:512 * i + 512].T
    return out


# version 05 (final)
# speedup vs baseline: 1.2995x; 1.0022x over previous
import numpy as np
import concourse.bass as bass
import concourse.mybir as mybir
from concourse.bass_utils import run_bass_kernel_spmd
from contextlib import ExitStack

F32 = mybir.dt.float32
BF16 = mybir.dt.bfloat16
I32 = mybir.dt.int32
AF = mybir.ActivationFunctionType
ALU = mybir.AluOpType
AX = mybir.AxisListType

D = 4096
KC = 32
TOK = 1024
TT = 2
TW = 512
NQH = 32
NKV = 8
HD = 128
RSCALE = float(1.0 / np.float64(np.float32(np.sqrt(np.float64(HD)))))
EPS = 1e-5
QT = 4
QW = 1024
KCQ = QW // 128
KCNT = (8, 16)

ENGINES = ("sync", "scalar", "vector", "tensor", "gpsimd")

PI_IN = 3.1415925
INV2PI = float(1.0 / (2.0 * np.pi))
NEG2PI = float(-2.0 * np.pi)
PIH = float(np.pi / 2.0)
CONSTS_F32 = [-1.0, 2.0, 1.0 / 4096.0, EPS, RSCALE, -RSCALE, 0.5, 7.0,
              64.0, 10000.0, 1.0, 0.0, -1.0 / 64.0, -1e30,
              INV2PI, NEG2PI, PI_IN, -PI_IN, PIH, -64.0]
CONSTS_BF16 = [1.0, -1.0, 2.0, 0.0]


class Prog:
    def __init__(self):
        self.ops = {e: [] for e in ENGINES}
        self.cnt = {}
        self.gate_pending = {e: None for e in ENGINES}
        self.serial = False
        for s in ("dS", "dG", "T", "V", "S", "G", "C"):
            self.cnt[s] = 0

    def op(self, eng, fn, waits=(), inc=None, amt=1):
        w = [x for x in waits if x is not None]
        g = self.gate_pending[eng]
        if g is not None:
            w = list(g.items()) + w
            self.gate_pending[eng] = None
        ev = None
        if inc is not None:
            self.cnt[inc] += amt
            ev = (inc, self.cnt[inc])
        self.ops[eng].append((tuple(w), fn, (inc, amt) if inc else None))
        if self.serial and inc is not None:
            self.gate(keys=("V", "S", "G"), engines=("vector", "scalar", "gpsimd"))
        return ev

    def dma(self, eng, out, in_, waits=(), sem=None):
        if sem is None:
            sem = "dS" if eng == "sync" else "dG"
        self.cnt.setdefault(sem, 0)
        return self.op(eng, lambda e, o=out, i=in_: e.dma_start(out=o, in_=i),
                       waits, inc=sem, amt=16)

    def gate(self, keys=None, engines=None):
        snap = {k: v for k, v in self.cnt.items()
                if v > 0 and (keys is None or k in keys)}
        for e in (engines or ENGINES):
            old = self.gate_pending[e] or {}
            merged = dict(old)
            for k, v in snap.items():
                if v > merged.get(k, 0):
                    merged[k] = v
            self.gate_pending[e] = merged

    def final_wait(self, eng, ev):
        self.ops[eng].append(((ev,), None, None))

    def emit(self, block, sems):
        for eng_name in ENGINES:
            ops = self.ops[eng_name]
            if not ops:
                continue

            def make(ops):
                def body(eng):
                    waited = {}
                    for waits, fn, inc in ops:
                        merged = {}
                        for s, v in waits:
                            if v > merged.get(s, 0):
                                merged[s] = v
                        for s, v in merged.items():
                            if v > 0 and waited.get(s, 0) < v:
                                eng.wait_ge(sems[s], v)
                                waited[s] = v
                        if fn is None:
                            continue
                        instr = fn(eng)
                        if inc is not None:
                            instr.then_inc(sems[inc[0]], inc[1])
                return body

            getattr(block, eng_name)(make(ops))


def build_nc(debug=False):
    nc = bass.Bass("TRN2", target_bir_lowering=False)

    def din(name, shape, dtype=F32):
        return nc.dram_tensor(name, shape, dtype, kind="ExternalInput").ap()

    def dscr(name, shape, dtype=F32):
        dbg = debug if isinstance(debug, (tuple, list, set)) else ()
        kind = "ExternalOutput" if name in dbg else "Internal"
        return nc.dram_tensor(name, shape, dtype, kind=kind).ap()

    x0 = din("x0", [D, TOK])
    w0a = din("w0a", [D, D])
    m0a = din("m0a", [D, D], I32)
    w0f = din("w0f", [D, D])
    b0f = din("b0f", [128, KC])
    g0 = din("g0", [128, KC * 8])
    is0 = din("is0", [8, D])
    os0 = din("os0", [8, D])
    w1q = din("w1q", [D, D])
    w1k = din("w1k", [NKV * 128, D])
    w1v = din("w1v", [D, 1024])
    w1o = din("w1o", [D, D])
    w1b = din("w1b", [D, D])
    g1 = din("g1", [128, KC * 8])
    a1 = din("a1", [128, KC * 64])
    b1d = din("b1d", [64, D])
    qbin = din("qb", [1, 2])
    qselin = din("qsel", [64, 8])

    out0 = nc.dram_tensor("out0", [D, TOK], F32, kind="ExternalOutput").ap()
    scr_s0 = dscr("scr_s0", [D, TOK])
    scr_x1 = dscr("scr_x1", [D, TOK])
    scr_g1 = dscr("scr_g1", [D, TOK])
    scr_qh = dscr("scr_qh", [NQH * 128, TOK], BF16)
    scr_ql = dscr("scr_ql", [NQH * 128, TOK], BF16)
    cc_in = nc.dram_tensor("cc_in", [2048, 1024], F32).ap()
    dbg = debug if isinstance(debug, (tuple, list, set)) else ()
    dbg_rope = (nc.dram_tensor("dbg_rope", [64, 5248], F32,
                               kind="ExternalOutput").ap()
                if "dbg_rope" in dbg else None)
    cc_out = dscr("cc_out", [4096, 1024])

    DSEMS = ("dS", "dG", "dW0", "dW1", "dM", "dX0", "dX1", "dB0", "dB1",
             "dK", "dV", "dQ", "dO", "dS0a", "dS0b", "dXW0", "dXW1", "dQW",
             "dGW0", "dGW1", "dCC", "dOut", "dOut1")
    P = Prog()
    for _s in DSEMS:
        P.cnt.setdefault(_s, 0)
    st = ExitStack()
    with st:
        e = st.enter_context

        act_hi = e(nc.sbuf_tensor("act_hi", [128, KC * TOK], BF16))
        act_lo = e(nc.sbuf_tensor("act_lo", [128, KC * TOK], BF16))
        wstage = e(nc.sbuf_tensor("wstage", [128, 2 * QW], F32))
        whl_hi = e(nc.sbuf_tensor("whl_hi", [128, 2 * QW], BF16))
        whl_lo = e(nc.sbuf_tensor("whl_lo", [128, 2 * QW], BF16))
        xst = e(nc.sbuf_tensor("xst", [128, 2 * TOK], F32))
        evac = e(nc.sbuf_tensor("evac", [128, 2 * TOK], F32))
        gbuf = e(nc.sbuf_tensor("gbuf", [128, 2 * TOK], F32))
        ones_f = e(nc.sbuf_tensor("ones_f", [128, 1], F32))
        ones_b = e(nc.sbuf_tensor("ones_b", [128, 1], BF16))
        onesr_f = e(nc.sbuf_tensor("onesr_f", [1, 128], F32))
        onesr_b = e(nc.sbuf_tensor("onesr_b", [1, 128], BF16))
        idm = e(nc.sbuf_tensor("idm", [128, 128], F32))
        idm_i = e(nc.sbuf_tensor("idm_i", [128, 128], I32))
        rt_hi = e(nc.sbuf_tensor("rt_hi", [8, TOK], BF16))
        rt_lo = e(nc.sbuf_tensor("rt_lo", [8, TOK], BF16))
        bcol = e(nc.sbuf_tensor("bcol", [8, 2], F32))
        g0s = e(nc.sbuf_tensor("g0s", [128, KC * 8], F32))
        g0h = e(nc.sbuf_tensor("g0h", [128, KC * 8], BF16))
        g0l = e(nc.sbuf_tensor("g0l", [128, KC * 8], BF16))
        stat = e(nc.sbuf_tensor("stat", [1, 1040], F32))
        b0fs = e(nc.sbuf_tensor("b0fs", [128, KC], F32))
        qbs = e(nc.sbuf_tensor("qbs", [1, 2], F32))
        qbc = e(nc.sbuf_tensor("qbc", [128, 2], F32))
        qsels = e(nc.sbuf_tensor("qsels", [64, 8], F32))
        thr = e(nc.sbuf_tensor("thr", [128, 24], F32))
        thr_i = e(nc.sbuf_tensor("thr_i", [128, 24], I32))
        repm = e(nc.sbuf_tensor("repm", [8, 64], BF16))
        rep8 = e(nc.sbuf_tensor("rep8", [8, 64], I32))
        csts = e(nc.sbuf_tensor("csts", [128, len(CONSTS_F32)], F32))
        cstb = e(nc.sbuf_tensor("cstb", [128, len(CONSTS_BF16)], BF16))

        for i, v in enumerate(CONSTS_F32):
            nc.const_aps.register(F32, v, csts[:, i:i + 1])
        for i, v in enumerate(CONSTS_BF16):
            nc.const_aps.register(BF16, v, cstb[:, i:i + 1])

        pb = [e(nc.psum_tensor(f"pb{i}", [128, TW], F32)) for i in range(8)]

        sems = {}
        for s in DSEMS + ("T", "V", "S", "G", "C"):
            sems[s] = e(nc.semaphore(s))

        def _ap(x):
            return x[:, :] if x.__class__.__name__.endswith("TensorHandle") else x

        def tt_(a, op, b, o, waits=(), eng="vector", inc="V"):
            return P.op(eng, lambda e, a=a, b=b, o=o, op=op:
                        e.tensor_tensor(out=o, in0=a, in1=b, op=op), waits, inc=inc)

        def tsc(in0, s1, op0, o, s2=None, op1=None, waits=(), eng="vector", inc="V"):
            def fn(e, i=in0, s1=s1, s2=s2, o=o, op0=op0, op1=op1):
                kw = {"op1": op1} if op1 is not None else {}
                return e.tensor_scalar(out=o, in0=i, scalar1=s1, scalar2=s2,
                                       op0=op0, **kw)
            return P.op(eng, fn, waits, inc=inc)

        def act(in_, func, o, scale=1.0, bias=0.0, waits=(), inc="S"):
            return P.op("scalar", lambda e, i=in_, o=o, f=func, sc=scale, b=bias:
                        e.activation(out=o, in_=i, func=f, scale=sc, bias=b),
                        waits, inc=inc)

        def scopy(in_, o, scale=None, waits=(), inc="S"):
            if scale is None:
                return P.op("scalar", lambda e, i=in_, o=o: e.copy(out=o, in_=i),
                            waits, inc=inc)
            return P.op("scalar", lambda e, i=in_, o=o, s=scale:
                        e.mul(out=o, in_=i, mul=s), waits, inc=inc)

        def mm(ps, lhsT, rhs, start, stop, waits=(), inc=None):
            return P.op("tensor",
                        lambda e, p=_ap(ps), l=_ap(lhsT), r=_ap(rhs), a=start, b=stop:
                        e.matmul(out=p, lhsT=l, rhs=r, start=a, stop=b),
                        waits, inc=inc)

        def recip(in_, o, waits=()):
            return P.op("vector", lambda e, i=in_, o=o:
                        e.reciprocal(out=o, in_=i), waits, inc="V")

        def iota(o, pattern, base, chm, waits=(), inc="G"):
            return P.op("gpsimd", lambda e, o=o, p=pattern, b=base, c=chm:
                        e.iota(out=o, pattern=p, base=b, channel_multiplier=c),
                        waits, inc=inc)

        def memset(ap, v, waits=(), inc="G"):
            return P.op("gpsimd", lambda e, a=ap, v=v: e.memset(a, v), waits, inc=inc)

        def split(src, hi, lo, waits=()):
            eh = scopy(src, hi, waits=waits)
            el = tt_(src, ALU.subtract, hi, lo, waits=[eh])
            return eh, el

        for i, v in enumerate(CONSTS_F32):
            memset(csts[:, i:i + 1], v)
        for i, v in enumerate(CONSTS_BF16):
            memset(cstb[:, i:i + 1], v)
        memset(ones_f[:, :], 1.0)
        memset(ones_b[:, :], 1.0)
        memset(onesr_f[:, :], 1.0)
        memset(onesr_b[:, :], 1.0)
        iota(idm_i[:, :], [[1, 128]], 0, -1)
        iota(rep8[:, :], [[1, 64]], 0, -8)
        for j in range(24):
            kc = j if j < 8 else (j - 8)
            iota(thr_i[:, j:j + 1], [[1, 1]], 128 * kc, 1)
        memset(bcol[:, :], 0.0)
        P.dma("sync", qbs[:, :], qbin[:, :], sem="dO")
        P.dma("sync", qsels[:, :], qselin[:, :], sem="dO")
        P.dma("sync", b0fs[:, :], b0f[:, :], sem="dO")
        P.dma("sync", g0s[:, :], g0[:, :], sem="dO")
        P.gate()
        tsc(idm_i[:, :], 0.0, ALU.is_equal, idm[:, :])
        tsc(rep8[:, :], 0.0, ALU.is_ge, xst[0:8, 0:64])
        tsc(rep8[:, :], 7.0, ALU.is_le, xst[0:8, 64:128])
        P.gate()
        tt_(xst[0:8, 0:64], ALU.mult, xst[0:8, 64:128], repm[:, :])
        scopy(thr_i[:, :], thr[:, :])
        mm(pb[0][:, 0:1], onesr_f[:, :], qbs[0:1, 0:1], True, True)
        mm(pb[1][:, 0:1], onesr_f[:, :], qbs[0:1, 1:2], True, True, inc="T")
        P.gate()
        scopy(pb[0][:, 0:1], qbc[:, 0:1])
        scopy(pb[1][:, 0:1], qbc[:, 1:2])
        P.gate()
        tsc(thr[:, 0:8], qbc[:, 0:1], ALU.subtract, thr[:, 0:8])
        tsc(thr[:, 8:24], qbc[:, 1:2], ALU.subtract, thr[:, 8:24])
        split(g0s[:, :], g0h[:, :], g0l[:, :])
        P.gate()

        def rhs_act(k0, tt):
            hi = act_hi[:, k0 * TOK + tt * TW: k0 * TOK + (tt + 1) * TW]
            lo = act_lo[:, k0 * TOK + tt * TW: k0 * TOK + (tt + 1) * TW]
            return hi, lo

        def stream_proj(src2d, n_list, ps_pair, epilogue, mask2d=None,
                        sign=False, npass=3, extra_group_mm=None, mstage=None):
            ev_sp, ev_tq, ev_ep, ev_wm = {}, {}, {}, {}
            j = 0
            for ni, n in enumerate(n_list):
                rows = slice(n * 128, (n + 1) * 128)
                psA, psB = ps_pair(ni)
                for qt in range(QT):
                    par = j % 2
                    sl = slice(par * QW, (par + 1) * QW)
                    csl = slice(qt * QW, (qt + 1) * QW)
                    ev_d = P.dma("sync", wstage[:, sl], src2d[rows, csl],
                                 waits=[ev_sp.get(j - 2)], sem="dW%d" % par)
                    if mask2d is not None:
                        ev_dm = P.dma("sync", mstage[:, 0:QW], mask2d[rows, csl],
                                      waits=[ev_wm.get(j - 1)], sem="dM")
                        ev_w = tt_(wstage[:, sl], ALU.mult, mstage[:, 0:QW],
                                   wstage[:, sl], waits=[ev_d, ev_dm,
                                                         ev_tq.get(j - 2)])
                        ev_wm[j] = ev_w
                        src_rdy = [ev_w]
                    else:
                        src_rdy = [ev_d, ev_tq.get(j - 2)]
                    if sign:
                        ev_hi = act(wstage[:, sl], AF.Sign, whl_hi[:, sl],
                                    waits=src_rdy)
                        ev_sp[j] = ev_hi
                    else:
                        ev_hi = scopy(wstage[:, sl], whl_hi[:, sl], waits=src_rdy)
                        ev_lo = tt_(wstage[:, sl], ALU.subtract, whl_hi[:, sl],
                                    whl_lo[:, sl], waits=[ev_hi])
                        ev_sp[j] = ev_lo
                    sem_sp = "S" if sign else "V"
                    last = None
                    for kk in range(KCQ):
                        k0 = qt * KCQ + kk
                        lh = whl_hi[:, par * QW + kk * 128:par * QW + (kk + 1) * 128]
                        ll = whl_lo[:, par * QW + kk * 128:par * QW + (kk + 1) * 128]
                        rh0, rl0 = rhs_act(k0, 0)
                        rh1, rl1 = rhs_act(k0, 1)
                        w0 = []
                        if kk == 0:
                            w0 = [ev_sp[j]]
                            if qt == 0 and ni >= 2:
                                w0.append(ev_ep.get(n_list[ni - 2]))
                        first = (qt == 0 and kk == 0)
                        is_last = (qt == QT - 1 and kk == KCQ - 1 and
                                   extra_group_mm is None)
                        qend = (kk == KCQ - 1)
                        mm(psA, lh, rh0, first, False, waits=w0)
                        mm(psA, lh, rl0, False, is_last and npass == 2,
                           inc=None)
                        mm(psB, lh, rh1, first, False)
                        last = mm(psB, lh, rl1, False, is_last and npass == 2,
                                  inc="T" if (qend and npass == 2) else None)
                        if npass == 3:
                            mm(psA, ll, rh0, False, is_last)
                            last = mm(psB, ll, rh1, False, is_last,
                                      inc="T" if qend else None)
                    ev_tq[j] = last
                    j += 1
                evT = ev_tq[j - 1]
                if extra_group_mm is not None:
                    evT = extra_group_mm(ni, psA, psB)
                ev_ep[n] = epilogue(ni, evT)
            return ev_ep

        def ln_stats(src2d, m_b, r_b, first_waits=()):
            ev_t = {}
            for k0 in range(KC):
                par = k0 % 2
                xs = xst[:, par * TOK:(par + 1) * TOK]
                xq = evac[:, par * TOK:(par + 1) * TOK]
                ev_d = P.dma("sync", xs, src2d[k0 * 128:(k0 + 1) * 128, :],
                             waits=[ev_t.get(k0 - 2)], sem="dX%d" % par)
                ev_s = act(xs, AF.Square, xq, waits=[ev_d, ev_t.get(k0 - 2)])
                first, lastk = (k0 == 0), (k0 == KC - 1)
                mm(pb[4][0:1, :], ones_f[:, :], xs[:, 0:TW], first, lastk,
                   waits=[ev_d] + (list(first_waits) if first else []))
                mm(pb[5][0:1, :], ones_f[:, :], xs[:, TW:TOK], first, lastk)
                mm(pb[6][0:1, :], ones_f[:, :], xq[:, 0:TW], first, lastk,
                   waits=[ev_s])
                ev_t[k0] = mm(pb[7][0:1, :], ones_f[:, :], xq[:, TW:TOK], first,
                              lastk, inc="T")
            P.gate()
            scopy(pb[4][0:1, :], m_b[0:1, 0:TW], scale=1.0 / 4096.0)
            scopy(pb[5][0:1, :], m_b[0:1, TW:TOK], scale=1.0 / 4096.0)
            scopy(pb[6][0:1, :], stat[0:1, 0:TW], scale=1.0 / 4096.0)
            scopy(pb[7][0:1, :], stat[0:1, TW:TOK], scale=1.0 / 4096.0)
            P.gate()
            tt_(m_b[0:1, 0:TOK], ALU.mult, m_b[0:1, 0:TOK], r_b[0:1, 0:TOK])
            P.gate()
            tt_(stat[0:1, 0:TOK], ALU.subtract, r_b[0:1, 0:TOK],
                stat[0:1, 0:TOK])
            P.gate()
            act(stat[0:1, 0:TOK], AF.Sqrt, stat[0:1, 0:TOK], bias=EPS)
            P.gate()
            recip(stat[0:1, 0:TOK], r_b[0:1, 0:TOK])
            P.gate()
            for tt in range(TT):
                mm(pb[4 + tt], onesr_f[:, :], m_b[0:1, tt * TW:(tt + 1) * TW],
                   True, True)
                mm(pb[6 + tt], onesr_f[:, :],
                   r_b[0:1, tt * TW:(tt + 1) * TW], True, True,
                   inc="T" if tt else None)
            P.gate()
            for tt in range(TT):
                scopy(pb[4 + tt][:, :], m_b[:, tt * TW:(tt + 1) * TW])
                scopy(pb[6 + tt][:, :], r_b[:, tt * TW:(tt + 1) * TW])
            P.gate()

        def ln_apply_and_router(src2d, m_b, r_b):
            ev_t = {}
            for k0 in range(KC):
                par = k0 % 2
                xs = xst[:, par * TOK:(par + 1) * TOK]
                xm = evac[:, par * TOK:(par + 1) * TOK]
                ev_d = P.dma("sync", xs, src2d[k0 * 128:(k0 + 1) * 128, :],
                             waits=[ev_t.get(k0 - 2)], sem="dX%d" % par)
                e1 = tt_(xs, ALU.subtract, m_b, xm, waits=[ev_d, ev_t.get(k0 - 2)])
                e2 = tt_(xm, ALU.mult, r_b, xm, waits=[e1])
                hi = act_hi[:, k0 * TOK:(k0 + 1) * TOK]
                lo = act_lo[:, k0 * TOK:(k0 + 1) * TOK]
                eh, el = split(xm, hi, lo, waits=[e2])
                first, lastk = (k0 == 0), (k0 == KC - 1)
                gh = g0h[:, k0 * 8:(k0 + 1) * 8]
                gl = g0l[:, k0 * 8:(k0 + 1) * 8]
                ev_mm = None
                for tt in range(TT):
                    ps = pb[6][0:8, :] if tt == 0 else pb[7][0:8, :]
                    hs = hi[:, tt * TW:(tt + 1) * TW]
                    ls = lo[:, tt * TW:(tt + 1) * TW]
                    mm(ps, gh, hs, first, False, waits=[el] if tt == 0 else [])
                    mm(ps, gh, ls, False, False)
                    ev_mm = mm(ps, gl, hs, False, lastk,
                               inc="T" if tt == 1 else None)
                ev_t[k0] = ev_mm
            P.gate()

        ph1 = ExitStack()
        mstage = ph1.enter_context(nc.sbuf_tensor("mstage", [128, QW], I32))
        scb_h = ph1.enter_context(nc.sbuf_tensor("scb_h", [8, D], BF16))
        scb_l = ph1.enter_context(nc.sbuf_tensor("scb_l", [8, D], BF16))
        ph1r = ExitStack()
        lgs = ph1r.enter_context(nc.sbuf_tensor("lgs0", [8, TOK], F32))
        rting = lgs

        m_b = gbuf[:, 0:TOK]
        r_b = gbuf[:, TOK:2 * TOK]
        ln_stats(x0, m_b, r_b)
        ln_apply_and_router(x0, m_b, r_b)

        tt_(g0s[:, :], ALU.mult, g0s[:, :], evac[:, 0:KC * 8])
        P.gate()
        for k0 in range(KC):
            mm(pb[0][0:1, 0:8], ones_f[:, :], evac[:, k0 * 8:(k0 + 1) * 8],
               k0 == 0, k0 == KC - 1, inc="T" if k0 == KC - 1 else None)
        P.gate()
        act(pb[0][0:1, 0:8], AF.Sqrt, stat[0:1, 0:8])
        P.gate()
        P.op("vector", lambda e: e.tensor_reduce(out=stat[0:1, 8:9],
                                                 in_=stat[0:1, 0:8],
                                                 axis=AX.X, op=ALU.max), (), inc="V")
        P.gate()
        tsc(stat[0:1, 8:9], -64.0, ALU.mult, stat[0:1, 9:10])
        P.gate()
        mm(pb[2][0:8, 0:1], onesr_f[0:1, 0:8], stat[0:1, 9:10], True, True, inc="T")
        P.gate()
        scopy(pb[2][0:8, 0:1], bcol[:, 0:1])
        scopy(pb[6][0:8, :], lgs[:, 0:TW])
        scopy(pb[7][0:8, :], lgs[:, TW:TOK])
        P.gate()
        act(lgs[:, :], AF.Exp, lgs[:, :], bias=bcol[:, 0:1])
        P.gate()
        for tt in range(TT):
            mm(pb[0 + tt][0:1, :], ones_f[0:8, 0:1], lgs[:, tt * TW:(tt + 1) * TW],
               True, True, inc="T" if tt else None)
        P.gate()
        for tt in range(TT):
            recip(pb[0 + tt][0:1, :], stat[0:1, tt * TW:(tt + 1) * TW])
        P.gate()
        for tt in range(TT):
            mm(pb[2 + tt][0:8, :], onesr_f[0:1, 0:8],
               stat[0:1, tt * TW:(tt + 1) * TW], True, True,
               inc="T" if tt else None)
        P.gate()
        for tt in range(TT):
            tt_(lgs[:, tt * TW:(tt + 1) * TW], ALU.mult, pb[2 + tt][0:8, :],
                rting[:, tt * TW:(tt + 1) * TW])
        P.gate()
        split(rting[:, :], rt_hi[:, :], rt_lo[:, :])
        P.gate()
        ph1r.close()

        for half in range(2):
            P.dma("sync", wstage[0:8, 0:QW], is0[:, half * 2048:half * 2048 + QW])
            P.dma("sync", wstage[0:8, QW:2 * QW],
                  is0[:, half * 2048 + QW:(half + 1) * 2048])
            P.gate()
            split(wstage[0:8, 0:2 * QW],
                  scb_h[:, half * 2048:(half + 1) * 2048],
                  scb_l[:, half * 2048:(half + 1) * 2048])
            P.gate()

        ev_s0dma = {}

        def ep_s0(ni, evT):
            par = ni % 2
            ev1 = scopy(pb[0 + 2 * par][:, :], evac[:, par * TOK:par * TOK + TW],
                        waits=[evT, ev_s0dma.get(ni - 2)])
            ev2 = scopy(pb[1 + 2 * par][:, :],
                        evac[:, par * TOK + TW:(par + 1) * TOK])
            ev_s0dma[ni] = P.dma("gpsimd", scr_s0[ni * 128:(ni + 1) * 128, :],
                                 evac[:, par * TOK:(par + 1) * TOK], waits=[ev2],
                                 sem="dS0a" if par == 0 else "dS0b")
            return ev2

        stream_proj(w0a, list(range(KC)),
                    lambda ni: (pb[0 + 2 * (ni % 2)], pb[1 + 2 * (ni % 2)]),
                    ep_s0, mask2d=m0a, npass=3, mstage=mstage)
        P.gate()

        ev_xs = {}
        for k0 in range(KC):
            par = k0 % 2
            psA, psB = pb[4 + 2 * par], pb[5 + 2 * par]
            ish = scb_h[:, k0 * 128:(k0 + 1) * 128]
            isl = scb_l[:, k0 * 128:(k0 + 1) * 128]
            evm = None
            for tt, ps in ((0, psA), (1, psB)):
                rh = rt_hi[:, tt * TW:(tt + 1) * TW]
                rl = rt_lo[:, tt * TW:(tt + 1) * TW]
                mm(ps, ish, rh, True, False,
                   waits=[ev_xs.get(k0 - 2)] if tt == 0 else [])
                mm(ps, ish, rl, False, False)
                evm = mm(ps, isl, rh, False, True, inc="T" if tt == 1 else None)
            hi = act_hi[:, k0 * TOK:(k0 + 1) * TOK]
            lo = act_lo[:, k0 * TOK:(k0 + 1) * TOK]
            xm = evac[:, par * TOK:(par + 1) * TOK]
            e1 = tt_(hi, ALU.add, lo, xm, waits=[evm])
            e2a = tt_(xm[:, 0:TW], ALU.mult, psA[:, :], xm[:, 0:TW], waits=[e1])
            e2b = tt_(xm[:, TW:TOK], ALU.mult, psB[:, :], xm[:, TW:TOK],
                      waits=[e2a])
            ev_xs[k0] = e2b
            split(xm, hi, lo, waits=[e2b])
        P.gate()

        for half in range(2):
            P.dma("sync", wstage[0:8, 0:QW], os0[:, half * 2048:half * 2048 + QW])
            P.dma("sync", wstage[0:8, QW:2 * QW],
                  os0[:, half * 2048 + QW:(half + 1) * 2048])
            P.gate()
            split(wstage[0:8, 0:2 * QW],
                  scb_h[:, half * 2048:(half + 1) * 2048],
                  scb_l[:, half * 2048:(half + 1) * 2048])
            P.gate()

        ev_x1dma, ev_osc, ev_ep_t1, ev_gcons, ev_scons, ev_xcons = ({} for _ in range(6))

        def ep_g0(ni, evT):
            par = ni % 2
            osh = scb_h[:, ni * 128:(ni + 1) * 128]
            osl = scb_l[:, ni * 128:(ni + 1) * 128]
            evos = None
            for tt in range(TT):
                ps = pb[4 + 2 * par + tt]
                rh = rt_hi[:, tt * TW:(tt + 1) * TW]
                rl = rt_lo[:, tt * TW:(tt + 1) * TW]
                mm(ps, osh, rh, True, False,
                   waits=[ev_osc.get(ni - 2)] if tt == 0 else [])
                mm(ps, osh, rl, False, False)
                evos = mm(ps, osl, rh, False, True, inc="T" if tt == 1 else None)
            gpsA, gpsB = pb[0 + 2 * par], pb[1 + 2 * par]
            t1 = evac[:, par * TOK:(par + 1) * TOK]
            gg = gbuf[:, par * TOK:(par + 1) * TOK]
            evc1 = scopy(pb[4 + 2 * par][:, :], gg[:, 0:TW],
                         waits=[evos, ev_gcons.get(ni - 2)])
            evc2 = scopy(pb[5 + 2 * par][:, :], gg[:, TW:TOK])
            ev_o1 = tt_(gpsA[:, :], ALU.mult, gg[:, 0:TW], t1[:, 0:TW],
                        waits=[evT, evc1, ev_ep_t1.get(ni - 2)])
            ev_o2 = tt_(gpsB[:, :], ALU.mult, gg[:, TW:TOK], t1[:, TW:TOK],
                        waits=[evc2])
            ev_osc[ni] = ev_o2
            ev_g = act(t1, AF.Silu, gg, bias=b0fs[:, ni:ni + 1],
                       waits=[ev_o2])
            xx = xst[:, par * TOK:(par + 1) * TOK]
            ev_sl = P.dma("sync", xx, scr_s0[ni * 128:(ni + 1) * 128, :],
                          waits=[ev_s0dma[ni], ev_scons.get(ni - 2)],
                          sem="dX%d" % par)
            ev_sg = tt_(xx, ALU.mult, gg, gg, waits=[ev_g, ev_sl])
            ev_scons[ni] = ev_sg
            ev_xl = P.dma("sync", xx, x0[ni * 128:(ni + 1) * 128, :],
                          waits=[ev_sg], sem="dX%d" % par)
            ev_x1 = tt_(xx, ALU.add, gg, gg, waits=[ev_xl])
            ev_xcons[ni] = ev_x1
            ev_ep_t1[ni] = ev_x1
            ev_x1dma[ni] = P.dma("gpsimd", scr_x1[ni * 128:(ni + 1) * 128, :],
                                 gg, waits=[ev_x1], sem="dXW%d" % par)
            ev_gcons[ni] = ev_x1dma[ni]
            return ev_o2

        stream_proj(w0f, list(range(KC)),
                    lambda ni: (pb[0 + 2 * (ni % 2)], pb[1 + 2 * (ni % 2)]),
                    ep_g0, sign=True, npass=2)
        P.gate()
        ph1.close()

        ph2t = ExitStack()
        v64 = ph2t.enter_context(nc.sbuf_tensor("v64", [64, 24], F32))
        v64i = ph2t.enter_context(nc.sbuf_tensor("v64i", [64, 1], I32))
        mi64 = ph2t.enter_context(nc.sbuf_tensor("mi64", [64, 512], I32))
        u_i = mi64[0:1, :]
        cq = ph2t.enter_context(nc.sbuf_tensor("cq", [64, TOK], F32))
        sq_t = ph2t.enter_context(nc.sbuf_tensor("sq_t", [64, TOK], F32))
        ck = ph2t.enter_context(nc.sbuf_tensor("ck", [64, TOK], F32))
        sk_t = ph2t.enter_context(nc.sbuf_tensor("sk_t", [64, TOK], F32))
        qrot = xst[:, 0:TOK]
        t64a = xst[0:64, 1024:1536]
        t64b = xst[0:64, 1536:2048]
        u_row = stat[0:1, 0:512]

        P.serial = True
        memset(v64[:, 0:1], 10000.0)
        iota(v64i[:, :], [[1, 1]], 0, 1, inc="G")
        iota(u_i[:, :], [[1, 512]], 0, 0, inc="G")
        act(v64[:, 0:1], AF.Ln, v64[:, 0:1])
        scopy(v64i[:, :], v64[:, 1:2])
        ev_urow = scopy(u_i[:, :], u_row)
        tt_(v64[:, 1:2], ALU.mult, v64[:, 0:1], v64[:, 2:3])
        tsc(v64[:, 2:3], -1.0 / 64.0, ALU.mult, v64[:, 2:3])
        act(v64[:, 2:3], AF.Exp, v64[:, 3:4])
        fvec = v64[:, 3:4]
        ev_angs = []
        for blk in range(2):
            sl = slice(blk * 512, blk * 512 + 512)
            cqs, sqs = cq[:, sl], sq_t[:, sl]
            cks, sks = ck[:, sl], sk_t[:, sl]
            tt_(qbc[0:64, blk:blk + 1], ALU.mult, fvec, v64[:, 8:9])
            ev_mm = mm(pb[4 + blk][0:64, :], onesr_f[0:1, 0:64], u_row,
                       True, True, waits=[ev_urow], inc="T")
            ev_ang = tsc(pb[4 + blk][0:64, :], fvec, ALU.mult, cqs,
                         s2=v64[:, 8:9], op1=ALU.add, waits=[ev_mm])
            ev_angs.append(ev_ang)
            tsc(cqs, PIH, ALU.add, cks)
            tsc(cks, INV2PI, ALU.mult, sks)
            P.op("vector", lambda e, a=sks: e.tensor_copy(out=mi64[:, :], in_=a),
                 (), inc="V")
            P.gate(keys=("V", "S", "G"), engines=("vector", "scalar", "gpsimd"))
            scopy(mi64[:, :], sks)
            P.op("vector", lambda e, a=sks, b=cks: e.scalar_tensor_tensor(
                out=b, in0=a, scalar=NEG2PI, op0=ALU.mult, in1=b, op1=ALU.add),
                (), inc="V")
            P.gate(keys=("V", "S", "G"), engines=("vector", "scalar", "gpsimd"))
            tsc(cks, PI_IN, ALU.min, cks)
            tsc(cks, -PI_IN, ALU.max, cks)
            act(cks, AF.Sin, cks)
            tsc(cqs, INV2PI, ALU.mult, sks)
            P.op("vector", lambda e, a=sks: e.tensor_copy(out=mi64[:, :], in_=a),
                 (), inc="V")
            P.gate(keys=("V", "S", "G"), engines=("vector", "scalar", "gpsimd"))
            scopy(mi64[:, :], sks)
            P.op("vector", lambda e, a=sks, b=cqs: e.scalar_tensor_tensor(
                out=a, in0=a, scalar=NEG2PI, op0=ALU.mult, in1=b, op1=ALU.add),
                (), inc="V")
            P.gate(keys=("V", "S", "G"), engines=("vector", "scalar", "gpsimd"))
            tsc(sks, PI_IN, ALU.min, sks)
            tsc(sks, -PI_IN, ALU.max, sks)
            act(sks, AF.Sin, sks)
            tsc(cks, RSCALE, ALU.mult, cqs)
            tsc(sks, RSCALE, ALU.mult, sqs)
        P.serial = False

        ph2 = ExitStack()
        lgs = ph2.enter_context(nc.sbuf_tensor("lgs1", [8, TOK], F32))
        rting = lgs
        t2s = evac[:, 0:256]

        P.dma("sync", g0s[:, :], g1[:, :], sem="dO")
        P.gate()
        split(g0s[:, :], g0h[:, :], g0l[:, :])
        P.gate()
        m_b2 = gbuf[:, 0:TOK]
        r_b2 = gbuf[:, TOK:2 * TOK]
        ln_stats(scr_x1, m_b2, r_b2, first_waits=ev_angs)
        ln_apply_and_router(scr_x1, m_b2, r_b2)

        scopy(pb[6][0:8, :], lgs[:, 0:TW])
        scopy(pb[7][0:8, :], lgs[:, TW:TOK])
        P.gate()
        for c in range(8):
            P.op("tensor", lambda e, c=c: e.transpose(
                out=pb[0][:, c * 8:(c + 1) * 8], in_=lgs[:, c * 128:(c + 1) * 128],
                identity=idm[0:8, 0:8]), (), inc="T" if c == 7 else None)
        P.gate()
        scopy(pb[0][:, 0:64], t2s[:, 0:64])
        P.gate()
        P.op("vector", lambda e: e.tensor_reduce(
            out=t2s[:, 64:72], in_=t2s[:, 0:64].rearrange("p (c e) -> p c e", e=8),
            axis=AX.X, op=ALU.max), (), inc="V")
        P.gate()
        tsc(t2s[:, 64:72], -1.0, ALU.mult, t2s[:, 72:80])
        P.gate()
        for c in range(8):
            tsc(t2s[:, c * 8:(c + 1) * 8], t2s[:, 64 + c:65 + c], ALU.is_equal,
                t2s[:, 80 + c * 8:88 + c * 8])
        P.gate()
        for c in range(8):
            P.op("vector", lambda e, c=c: e.scalar_tensor_tensor(
                out=t2s[:, 144 + c * 8:152 + c * 8],
                in0=t2s[:, 80 + c * 8:88 + c * 8], scalar=-1e30, op0=ALU.mult,
                in1=t2s[:, c * 8:(c + 1) * 8], op1=ALU.add), (), inc="V")
        P.gate()
        P.op("vector", lambda e: e.tensor_reduce(
            out=t2s[:, 208:216], in_=t2s[:, 144:208].rearrange("p (c e) -> p c e", e=8),
            axis=AX.X, op=ALU.max), (), inc="V")
        P.gate()
        tt_(t2s[:, 208:216], ALU.subtract, t2s[:, 64:72], t2s[:, 216:224])
        P.gate()
        act(t2s[:, 216:224], AF.Exp, t2s[:, 216:224])
        P.gate()
        tsc(t2s[:, 216:224], 1.0, ALU.add, t2s[:, 216:224])
        P.gate()
        recip(t2s[:, 216:224], t2s[:, 224:232])
        P.gate()
        tsc(t2s[:, 224:232], 2.0, ALU.mult, t2s[:, 224:232])
        P.gate()
        for c in range(8):
            act(t2s[:, c * 8:(c + 1) * 8], AF.Exp, t2s[:, 80 + c * 8:88 + c * 8],
                bias=t2s[:, 72 + c:73 + c])
            tsc(t2s[:, c * 8:(c + 1) * 8], t2s[:, 208 + c:209 + c], ALU.is_ge,
                t2s[:, 144 + c * 8:152 + c * 8])
        P.gate()
        for c in range(8):
            tt_(t2s[:, 80 + c * 8:88 + c * 8], ALU.mult,
                t2s[:, 144 + c * 8:152 + c * 8], t2s[:, 80 + c * 8:88 + c * 8])
        P.gate()
        for c in range(8):
            tsc(t2s[:, 80 + c * 8:88 + c * 8], t2s[:, 224 + c:225 + c], ALU.mult,
                t2s[:, 80 + c * 8:88 + c * 8])
        P.gate()
        for c in range(8):
            ps = pb[2] if c < 4 else pb[3]
            P.op("tensor", lambda e, c=c, ps=ps: e.transpose(
                out=ps[0:8, (c % 4) * 128:(c % 4 + 1) * 128],
                in_=t2s[:, 80 + c * 8:88 + c * 8],
                identity=idm[:, :]), (), inc="T" if c == 7 else None)
        P.gate()
        scopy(pb[2][0:8, :], rting[:, 0:TW])
        scopy(pb[3][0:8, :], rting[:, TW:TOK])
        P.gate()
        split(rting[:, :], rt_hi[:, :], rt_lo[:, :])
        P.gate()
        ph2.close()

        ph3h = ExitStack()
        h1h = ph3h.enter_context(nc.sbuf_tensor("h1h", [64, TOK], BF16))
        h1l = ph3h.enter_context(nc.sbuf_tensor("h1l", [64, TOK], BF16))
        ph2q = ExitStack()
        qsp_h = ph2q.enter_context(nc.sbuf_tensor("qsp_h", [128, TOK], BF16))
        qsp_l = ph2q.enter_context(nc.sbuf_tensor("qsp_l", [128, TOK], BF16))

        def rope_apply(psA, psB, ctab, stab, out_f32, waits):
            last = None
            for tt, ps in ((0, psA), (1, psB)):
                c = ctab[:, tt * TW:(tt + 1) * TW]
                s = stab[:, tt * TW:(tt + 1) * TW]
                oE = out_f32[0:64, tt * TW:(tt + 1) * TW]
                oO = out_f32[64:128, tt * TW:(tt + 1) * TW]
                w = waits if tt == 0 else [last]
                e1 = tt_(ps[0:64, :], ALU.mult, c, t64a[:, :], waits=w)
                e2 = tt_(ps[64:128, :], ALU.mult, s, t64b[:, :], waits=[e1])
                e3 = tt_(t64a[:, :], ALU.subtract, t64b[:, :], oE, waits=[e2])
                e4 = tt_(ps[0:64, :], ALU.mult, s, t64a[:, :], waits=[e3])
                e5 = tt_(ps[64:128, :], ALU.mult, c, t64b[:, :], waits=[e4])
                last = tt_(t64a[:, :], ALU.add, t64b[:, :], oO, waits=[e5])
            return last

        ev_ccdma = {}
        ev_kshi = {}

        def ep_k(ni, evT):
            ev_r = rope_apply(pb[0 + 2 * (ni % 2)], pb[1 + 2 * (ni % 2)],
                              ck, sk_t, qrot,
                              waits=[evT, ev_ccdma.get(ni - 1)])
            ev_ccdma[ni] = P.dma("gpsimd", cc_in[ni * 128:(ni + 1) * 128, :],
                                 qrot[:, :], waits=[ev_r], sem="dCC")
            return ev_r

        stream_proj(w1k, list(range(NKV)),
                    lambda ni: (pb[0 + 2 * (ni % 2)], pb[1 + 2 * (ni % 2)]),
                    ep_k)
        P.gate()

        ev_vt, ev_vsp = {}, {}
        for dvh in range(2):
            for k0 in range(KC):
                j = dvh * KC + k0
                par = j % 2
                wsl = slice(par * QW, par * QW + TW)
                ev_d = P.dma("sync", wstage[:, wsl],
                             w1v[k0 * 128:(k0 + 1) * 128, dvh * TW:(dvh + 1) * TW],
                             waits=[ev_vsp.get(j - 2)], sem="dW%d" % par)
                eh = scopy(wstage[:, wsl], whl_hi[:, wsl],
                           waits=[ev_d, ev_vt.get(j - 2)])
                el = tt_(wstage[:, wsl], ALU.subtract, whl_hi[:, wsl],
                         whl_lo[:, wsl], waits=[eh])
                ev_vsp[j] = el
                ev = None
                for tc in range(8):
                    ps = pb[tc]
                    lhH = act_hi[:, k0 * TOK + tc * 128:k0 * TOK + (tc + 1) * 128]
                    lhL = act_lo[:, k0 * TOK + tc * 128:k0 * TOK + (tc + 1) * 128]
                    first, lastk = (k0 == 0), (k0 == KC - 1)
                    mm(ps, lhH, whl_hi[:, wsl], first, False,
                       waits=[el] if tc == 0 else [])
                    mm(ps, lhH, whl_lo[:, wsl], False, False)
                    ev = mm(ps, lhL, whl_hi[:, wsl], False, lastk,
                            inc="T" if tc == 7 else None)
                ev_vt[j] = ev
            P.gate()
            for tc in range(8):
                buf = (evac if tc < 4 else gbuf)[:, (tc % 4) * TW:(tc % 4 + 1) * TW]
                evc = scopy(pb[tc][:, :], buf)
                P.dma("gpsimd", cc_in[1024 + tc * 128:1024 + (tc + 1) * 128,
                                      dvh * TW:(dvh + 1) * TW], buf, waits=[evc],
                      sem="dCC")
            P.gate()

        ev_cc = P.op("gpsimd", lambda e: e.collective_compute(
            "AllGather", ALU.bypass,
            replica_groups=[[0, 1], [2, 3], [4, 5], [6, 7]],
            ins=[cc_in[:, :]], outs=[cc_out[:, :]]), (), inc="C")

        ev_qdma = {}

        def ep_q(ni, evT):
            ev_r = rope_apply(pb[0 + 2 * (ni % 2)], pb[1 + 2 * (ni % 2)],
                              cq, sq_t, qrot,
                              waits=[evT, ev_qdma.get(ni - 1)])
            eh, el = split(qrot[:, :], qsp_h[:, :], qsp_l[:, :], waits=[ev_r])
            d1 = P.dma("gpsimd", scr_qh[ni * 128:(ni + 1) * 128, :], qsp_h[:, :],
                       waits=[el], sem="dQW")
            d2 = P.dma("gpsimd", scr_ql[ni * 128:(ni + 1) * 128, :], qsp_l[:, :],
                       waits=[el], sem="dQW")
            ev_qdma[ni] = d2
            return ev_r

        stream_proj(w1q, list(range(NQH)),
                    lambda ni: (pb[0 + 2 * (ni % 2)], pb[1 + 2 * (ni % 2)]),
                    ep_q)
        P.gate()
        ph2q.close()

        P.dma("sync", wstage[:, 0:QW], a1[:, 0:QW], sem="dO")
        P.dma("sync", wstage[:, QW:2 * QW], a1[:, QW:2 * QW], sem="dO")
        P.gate()
        split(wstage[:, :], whl_hi[:, :], whl_lo[:, :])
        P.gate()
        for k0 in range(KC):
            lh = whl_hi[:, k0 * 64:(k0 + 1) * 64]
            ll = whl_lo[:, k0 * 64:(k0 + 1) * 64]
            for tt in range(TT):
                ps = pb[4 + tt][0:64, :]
                rh, rl = rhs_act(k0, tt)
                first, lastk = (k0 == 0), (k0 == KC - 1)
                mm(ps, lh, rh, first, False)
                mm(ps, lh, rl, False, False)
                mm(ps, ll, rh, False, lastk,
                   inc="T" if (lastk and tt == 1) else None)
        P.gate()
        scopy(pb[4][0:64, :], qrot[0:64, 0:TW])
        scopy(pb[5][0:64, :], qrot[0:64, TW:TOK])
        for tt in range(TT):
            ps = pb[6 + tt][0:64, :]
            mm(ps, repm[:, :], rt_hi[:, tt * TW:(tt + 1) * TW], True, False)
            mm(ps, repm[:, :], rt_lo[:, tt * TW:(tt + 1) * TW], False, True,
               inc="T" if tt else None)
        P.gate()
        for tt in range(TT):
            tt_(qrot[0:64, tt * TW:(tt + 1) * TW], ALU.mult, pb[6 + tt][0:64, :],
                qrot[0:64, tt * TW:(tt + 1) * TW])
        P.gate()
        split(qrot[0:64, 0:TOK], h1h[:, :], h1l[:, :])
        P.gate()

        ph2b = ExitStack()
        b1st = ph2b.enter_context(nc.sbuf_tensor("b1st", [64, 2 * 128], F32))
        b1h = ph2b.enter_context(nc.sbuf_tensor("b1h", [64, 2 * 128], BF16))
        b1lo = ph2b.enter_context(nc.sbuf_tensor("b1lo", [64, 2 * 128], BF16))

        ev_g1dma = {}
        ev_b1mm = {}

        def lora_B(ni, psA, psB):
            par = ni % 2
            bsl = slice(par * 128, (par + 1) * 128)
            ev_d = P.dma("sync", b1st[:, bsl],
                         b1d[:, ni * 128:(ni + 1) * 128],
                         waits=[ev_b1sp.get(ni - 2)], sem="dB%d" % par)
            ehb, elb = split(b1st[:, bsl], b1h[:, bsl], b1lo[:, bsl],
                             waits=[ev_d, ev_b1mm.get(ni - 2)])
            ev_b1sp[ni] = elb
            ev = None
            for tt, ps in ((0, psA), (1, psB)):
                mm(ps, b1h[:, bsl], h1h[:, tt * TW:(tt + 1) * TW], False, False,
                   waits=[elb] if tt == 0 else [])
                mm(ps, b1h[:, bsl], h1l[:, tt * TW:(tt + 1) * TW], False, False)
                ev = mm(ps, b1lo[:, bsl], h1h[:, tt * TW:(tt + 1) * TW], False,
                        True, inc="T" if tt == 1 else None)
            ev_b1mm[ni] = ev
            return ev

        ev_b1sp = {}

        def ep_g1(ni, evT):
            par = ni % 2
            gg = gbuf[:, par * TOK:(par + 1) * TOK]
            psA, psB = pb[0 + 2 * par], pb[1 + 2 * par]
            ev1 = act(psA[:, :], AF.Silu, gg[:, 0:TW],
                      waits=[evT, ev_g1dma.get(ni - 2)])
            ev2 = act(psB[:, :], AF.Silu, gg[:, TW:TOK])
            ev_g1dma[ni] = P.dma("gpsimd", scr_g1[ni * 128:(ni + 1) * 128, :],
                                 gg, waits=[ev2], sem="dGW%d" % par)
            return ev2

        stream_proj(w1b, list(range(KC)),
                    lambda ni: (pb[0 + 2 * (ni % 2)], pb[1 + 2 * (ni % 2)]),
                    ep_g1, extra_group_mm=lora_B)
        P.gate()
        ph2b.close()
        ph3h.close()
        ph2t.close()

        ph3 = ExitStack()
        ksb = wstage[:, 0:2048]
        vsb = wstage[:, 0:2048]
        khi = whl_hi[:, 0:2048]
        klo = whl_lo[:, 0:2048]
        vhi = ph3.enter_context(nc.sbuf_tensor("vhi", [128, 2048], BF16))
        vlo = ph3.enter_context(nc.sbuf_tensor("vlo", [128, 2048], BF16))
        qh2 = ph3.enter_context(nc.sbuf_tensor("qh2", [128, TOK], BF16))
        ql2 = ph3.enter_context(nc.sbuf_tensor("ql2", [128, TOK], BF16))
        et = ph3.enter_context(nc.sbuf_tensor("et", [128, 2 * TW], F32))
        ehi = ph3.enter_context(nc.sbuf_tensor("ehi", [128, 2 * TW], BF16))
        elo = ph3.enter_context(nc.sbuf_tensor("elo", [128, 2 * TW], BF16))
        negb = ph3.enter_context(nc.sbuf_tensor("negb", [1, TOK], BF16))
        viota = ph3.enter_context(nc.sbuf_tensor("viota", [128, TW], I32))

        iota(viota[:, :], [[1, TW]], 0, 0, inc="G")
        P.gate()

        RHO = [0, 1, 1, 0]
        LAM = [0, 0, 1, 1]

        C_exp = {}
        C_av = {0: None, 1: None}
        C_lastS = None
        C_rsfree = None
        C_ofree = None
        C_bfree = None
        C_obuf = None
        C_pq = {0: None, 1: None}
        C_xq = {0: None, 1: None}
        C_negb = None

        for hv in range(NKV):
            kds = []
            for b in range(4):
                r0 = RHO[b] * 2048 + hv * 128
                c0 = LAM[b] * TW
                kds.append(P.dma("sync", ksb[:, b * TW:(b + 1) * TW],
                                 cc_out[r0:r0 + 128, c0:c0 + TW],
                                 waits=[("C", 1), C_lastS], sem="dK"))
            ehK, elK = split(ksb[:, :], khi[:, :], klo[:, :],
                             waits=kds + [C_lastS])
            e1 = tt_(khi[:, :], ALU.add, klo[:, :], vsb[:, :], waits=[elK])
            e2 = act(vsb[:, :], AF.Square, vsb[:, :], waits=[e1])
            evn = {}
            ev_kn = None
            for b in range(4):
                ps = pb[5 + (b % 2)]
                ev_kn = mm(ps[0:1, :], ones_f[:, :], vsb[:, b * TW:(b + 1) * TW],
                           True, True,
                           waits=([e2] if b < 2 else []) + [evn.get(b - 2),
                                                            C_pq[b % 2]],
                           inc="T")
                evn[b] = scopy(ps[0:1, :], gbuf[0:1, b * TW:(b + 1) * TW],
                               waits=[ev_kn, C_bfree])
            C_pq = {0: evn[2], 1: evn[3]}
            ev_red = P.op("vector", lambda e: e.tensor_reduce(
                out=stat[0:1, 1024:1025], in_=gbuf[0:1, 0:2048], axis=AX.X,
                op=ALU.max), [evn[2], evn[3]], inc="V")
            ev_sq = act(stat[0:1, 1024:1025], AF.Sqrt, stat[0:1, 1025:1026],
                        waits=[ev_red])
            ev_nk = tsc(stat[0:1, 1025:1026], -1.0, ALU.mult,
                        stat[0:1, 1026:1027], waits=[ev_sq])
            vds = []
            for g in range(16):
                b = g // 4
                lc = LAM[b] * 4 + (g % 4)
                r0 = RHO[b] * 2048 + 1024 + lc * 128
                vds.append(P.dma("sync", vsb[:, g * 128:(g + 1) * 128],
                                 cc_out[r0:r0 + 128, hv * 128:(hv + 1) * 128],
                                 waits=[ev_kn, C_av[0], C_av[1]], sem="dV"))
            ehV, elV = split(vsb[:, :], vhi[:, :], vlo[:, :], waits=vds)

            for hq in range(hv * 4, hv * 4 + 4):
                par = hq % 2
                qh = qh2[:, :]
                ql = ql2[:, :]
                d1 = P.dma("sync", qh, scr_qh[hq * 128:(hq + 1) * 128, :],
                           waits=[C_lastS], sem="dQ")
                d2 = P.dma("sync", ql, scr_ql[hq * 128:(hq + 1) * 128, :],
                           sem="dQ")
                xq = xst[:, par * TOK:(par + 1) * TOK]
                e1 = tt_(qh, ALU.add, ql, xq, waits=[d1, d2, C_xq[par]])
                e2 = act(xq, AF.Square, xq, waits=[e1])
                evq = None
                for tt in range(TT):
                    evq = mm(pb[5 + tt][0:1, :], ones_f[:, :],
                             xq[:, tt * TW:(tt + 1) * TW], True, True,
                             waits=[e2, C_pq[tt]], inc="T")
                    evq = act(pb[5 + tt][0:1, :], AF.Sqrt,
                              stat[0:1, tt * TW:(tt + 1) * TW],
                              waits=[evq, C_negb])
                    C_pq[tt] = evq
                C_xq[par] = evq
                evb = tsc(stat[0:1, 0:TOK], stat[0:1, 1026:1027], ALU.mult,
                          negb[0:1, :], waits=[evq, ev_nk, C_lastS])
                C_negb = evb

                for qi in range(2):
                    ev_exp, ev_av = dict(C_exp), dict(C_av)
                    kcnt = KCNT[qi]
                    qsl = slice(qi * TW, (qi + 1) * TW)

                    def emit_S(kc):
                        p2 = kc % 2
                        psS = pb[p2]
                        w0 = [elK, evb, d2, ev_exp.get(kc - 2, ev_exp.get(p2))]
                        mm(psS, khi[:, kc * 128:(kc + 1) * 128], qh[:, qsl],
                           True, False, waits=w0)
                        mm(psS, khi[:, kc * 128:(kc + 1) * 128], ql[:, qsl],
                           False, False)
                        mm(psS, klo[:, kc * 128:(kc + 1) * 128], qh[:, qsl],
                           False, False)
                        return mm(psS, onesr_b[:, :], negb[0:1, qsl], False,
                                  True, inc="T")

                    def emit_EV(kc, evS):
                        p2 = kc % 2
                        psS = pb[p2]
                        esl = slice(p2 * TW, (p2 + 1) * TW)
                        avg = ev_av.get(kc - 2, ev_av.get(p2))
                        evE = act(psS[:, :], AF.Exp, et[:, esl],
                                  waits=[evS, avg])
                        evEM = P.op("vector", lambda e, esl=esl, col=qi * 8 + kc:
                                    e.scalar_tensor_tensor(
                                        out=et[:, esl], in0=viota[:, :],
                                        scalar=thr[:, col:col + 1], op0=ALU.is_ge,
                                        in1=et[:, esl], op1=ALU.mult),
                                    [evE, avg], inc="V")
                        evH = scopy(et[:, esl], ehi[:, esl], waits=[evEM])
                        evL = tt_(et[:, esl], ALU.subtract, ehi[:, esl],
                                  elo[:, esl], waits=[evH])
                        ev_exp[kc] = evE
                        return evL

                    def emit_AV(kc, evL):
                        p2 = kc % 2
                        esl = slice(p2 * TW, (p2 + 1) * TW)
                        first, lastk = (kc == 0), (kc == kcnt - 1)
                        w_rs = [evL] + ([C_rsfree] if first else [])
                        mm(pb[3][0:1, :], ones_b[:, :], ehi[:, esl], first,
                           False, waits=w_rs)
                        mm(pb[3][0:1, :], ones_b[:, :], elo[:, esl], False,
                           lastk)
                        w_o = [elV, C_ofree] if first else []
                        mm(pb[2], vhi[:, kc * 128:(kc + 1) * 128], ehi[:, esl],
                           first, False, waits=w_o)
                        mm(pb[2], vhi[:, kc * 128:(kc + 1) * 128], elo[:, esl],
                           False, False)
                        ev_av[kc] = mm(pb[2], vlo[:, kc * 128:(kc + 1) * 128],
                                       ehi[:, esl], False, lastk, inc="T")
                        return ev_av[kc]

                    pend = []
                    evS_last = None
                    for kc in range(kcnt):
                        evS_last = emit_S(kc)
                        if len(pend) >= 2:
                            emit_AV(*pend.pop(0))
                        evL = emit_EV(kc, evS_last)
                        pend.append((kc, evL))
                    for p_ in pend:
                        emit_AV(*p_)
                    C_lastS = evS_last
                    lk = kcnt - 1
                    C_exp = {0: ev_exp[lk if lk % 2 == 0 else lk - 1],
                             1: ev_exp[lk if lk % 2 == 1 else lk - 1]}
                    C_av = {0: ev_av[lk if lk % 2 == 0 else lk - 1],
                            1: ev_av[lk if lk % 2 == 1 else lk - 1]}

                    evR = recip(pb[3][0:1, :], gbuf[0:1, TW:2 * TW],
                                waits=[ev_av[kcnt - 1], C_bfree])
                    C_rsfree = evR
                    evB = mm(pb[4], onesr_f[:, :], gbuf[0:1, TW:2 * TW],
                             True, True, waits=[evR], inc="T")
                    o1 = scopy(pb[2][:, :], gbuf[:, 0:TW],
                               waits=[evB, C_bfree])
                    C_ofree = o1
                    onrm = tt_(gbuf[:, 0:TW], ALU.mult, pb[4][:, :],
                               evac[:, 0:TW], waits=[o1, C_obuf])
                    C_bfree = onrm
                    dsts = act_hi[:, hq * TOK + qi * TW:hq * TOK + (qi + 1) * TW]
                    dstl = act_lo[:, hq * TOK + qi * TW:hq * TOK + (qi + 1) * TW]
                    eh2, el2 = split(evac[:, 0:TW], dsts, dstl, waits=[onrm])
                    C_obuf = el2
        P.gate()
        ph3.close()

        ev_ocons, ev_gcons2, ev_xcons2 = {}, {}, {}

        def ep_out(ni, evT):
            par = ni % 2
            gg = gbuf[:, par * TOK:(par + 1) * TOK]
            xx = xst[:, par * TOK:(par + 1) * TOK]
            oo = evac[:, par * TOK:(par + 1) * TOK]
            ev_gl = P.dma("sync", gg, scr_g1[ni * 128:(ni + 1) * 128, :],
                          waits=[ev_gcons2.get(ni - 2)], sem="dB%d" % par)
            ev_xl = P.dma("sync", xx, scr_x1[ni * 128:(ni + 1) * 128, :],
                          waits=[ev_xcons2.get(ni - 2)], sem="dX%d" % par)
            e1 = tt_(pb[0 + 2 * par][:, :], ALU.mult, gg[:, 0:TW], oo[:, 0:TW],
                     waits=[evT, ev_gl, ev_ocons.get(ni - 2)])
            e2 = tt_(pb[1 + 2 * par][:, :], ALU.mult, gg[:, TW:TOK], oo[:, TW:TOK],
                     waits=[e1])
            ev_gcons2[ni] = e2
            e3 = tt_(xx, ALU.add, oo, oo, waits=[e2, ev_xl])
            ev_xcons2[ni] = e3
            ev_ocons[ni] = P.dma("gpsimd", out0[ni * 128:(ni + 1) * 128, :], oo,
                                 waits=[e3], sem="dOut" if par == 0 else "dOut1")
            return e2

        stream_proj(w1o, list(range(KC)),
                    lambda ni: (pb[0 + 2 * (ni % 2)], pb[1 + 2 * (ni % 2)]),
                    ep_out)
        P.final_wait("gpsimd", ("dOut", P.cnt["dOut"]))
        P.final_wait("gpsimd", ("dOut1", P.cnt["dOut1"]))

        with nc.Block() as block:
            P.emit(block, sems)

    return nc


_PERMQ = np.concatenate([h * 128 + np.concatenate([np.arange(0, 128, 2),
                                                   np.arange(1, 128, 2)])
                         for h in range(NQH)])
_PERMK = np.concatenate([h * 128 + np.concatenate([np.arange(0, 128, 2),
                                                   np.arange(1, 128, 2)])
                         for h in range(NKV)])


def _t4(w):
    return np.ascontiguousarray(
        w.reshape(KC, 128, KC, 128).transpose(2, 1, 0, 3).reshape(D, D))


def _t4k(w):
    return np.ascontiguousarray(
        w.reshape(KC, 128, NKV, 128).transpose(2, 1, 0, 3).reshape(NKV * 128, D))


_NC_CACHE = {}


def _get_nc(debug=False):
    if debug not in _NC_CACHE:
        _NC_CACHE[debug] = build_nc(debug)
    return _NC_CACHE[debug]


def make_in_maps(L0_attn_data, L0_attn_weight, L0_attn_mask, L0_ffn_weight,
                 L0_ffn_bias, L0_ffn_gate_w, L0_ffn_in_channel_scale,
                 L0_ffn_out_channel_scale, L1_attn_wq, L1_attn_wk, L1_attn_wv,
                 L1_attn_wo, L1_ffn_W_base, L1_ffn_gate_W, L1_ffn_A, L1_ffn_B):
    f = np.ascontiguousarray
    w0a = _t4(np.asarray(L0_attn_weight).T)
    m0a = _t4(np.asarray(L0_attn_mask).T.astype(np.int32))
    w0f = _t4(np.asarray(L0_ffn_weight).T)
    b0fv = f(np.asarray(L0_ffn_bias).reshape(KC, 128).T)
    g0v = f(np.asarray(L0_ffn_gate_w).T.reshape(KC, 128, 8)
            .transpose(1, 0, 2).reshape(128, KC * 8))
    is0v = f(np.asarray(L0_ffn_in_channel_scale))
    os0v = f(np.asarray(L0_ffn_out_channel_scale))
    w1qv = _t4(np.asarray(L1_attn_wq)[:, _PERMQ])
    w1kv = _t4k(np.asarray(L1_attn_wk)[:, _PERMK])
    w1vv = f(np.asarray(L1_attn_wv))
    w1ov = _t4(np.asarray(L1_attn_wo))
    w1bv = _t4(np.asarray(L1_ffn_W_base).T)
    g1v = f(np.asarray(L1_ffn_gate_W).T.reshape(KC, 128, 8)
            .transpose(1, 0, 2).reshape(128, KC * 8))
    a1v = f(np.asarray(L1_ffn_A).reshape(64, D).T.reshape(KC, 128, 64)
            .transpose(1, 0, 2).reshape(128, KC * 64))
    b1v = f(np.asarray(L1_ffn_B).transpose(0, 2, 1).reshape(64, D))
    x = np.asarray(L0_attn_data)

    in_maps = []
    for c in range(8):
        seq, role = c // 2, c % 2
        blocks = (0, 3) if role == 0 else (1, 2)
        xc = np.concatenate([x[seq, 512 * b:512 * b + 512, :] for b in blocks], 0)
        x0v = f(xc.T)
        qbv = np.array([[512.0 * blocks[0], 512.0 * blocks[1]]], np.float32)
        qs = np.zeros((64, 8), np.float32)
        qs[:, blocks[0]] = 1.0
        qs[:, 4 + blocks[1]] = 1.0
        in_maps.append({
            "x0": x0v, "w0a": w0a, "m0a": m0a, "w0f": w0f, "b0f": b0fv,
            "g0": g0v, "is0": is0v, "os0": os0v, "w1q": w1qv, "w1k": w1kv,
            "w1v": w1vv, "w1o": w1ov, "w1b": w1bv, "g1": g1v, "a1": a1v,
            "b1d": b1v, "qb": qbv, "qsel": qs,
        })
    return in_maps


def kernel(**inputs):
    in_maps = make_in_maps(**inputs)
    nc = _get_nc(False)
    res = run_bass_kernel_spmd(nc, in_maps, core_ids=list(range(8)))
    out = np.empty((4, 2048, D), np.float32)
    for c in range(8):
        seq, role = c // 2, c % 2
        blocks = (0, 3) if role == 0 else (1, 2)
        o = np.asarray(res.results[c]["out0"])
        for i, b in enumerate(blocks):
            out[seq, 512 * b:512 * b + 512, :] = o[:, 512 * i:512 * i + 512].T
    return out
